# version 06
# baseline (speedup 1.0000x reference)
import numpy as np
import ml_dtypes
from contextlib import ExitStack

import concourse.bass as bass
import concourse.mybir as mybir
from concourse.bass_utils import run_bass_kernel_spmd

BF16 = ml_dtypes.bfloat16
F32 = mybir.dt.float32
BF = mybir.dt.bfloat16
AF = mybir.ActivationFunctionType
ALU = mybir.AluOpType

T = 2048
D = 1024
H = 16
NTT = 16
NDC = 8
NST = 4
PAIRS4 = [(0, 1), (0, 2), (0, 3), (1, 2), (1, 3), (2, 3)]
SIGNS = [1.0, -1.0, 1.0, 1.0, -1.0, 1.0]
PAIRS_READ = [PAIRS4[5 - p] for p in range(6)]

TILES = []
for _s in range(NST):
    for _kc in range(4 * _s + 4):
        _diag = _kc >= 4 * _s
        TILES.append((_s, _kc, 128 * (_kc - 4 * _s) if _diag else 0, _diag))
N_TILES = len(TILES)


def host_consts():
    c = {}
    E = np.zeros((4, 128, 96), dtype=np.float32)
    for h in range(H):
        for p in range(6):
            col = 6 * h + p
            E[0, 4 * h + PAIRS4[p][0], col] = 1.0
            E[1, 4 * h + PAIRS4[p][1], col] = 1.0
            E[2, 4 * h + PAIRS_READ[p][0], col] = 1.0
            E[3, 4 * h + PAIRS_READ[p][1], col] = 1.0
    for k in range(4):
        E[k, 64:128, :] = E[k, 0:64, :]
    c['E'] = np.concatenate([E[0], E[1], E[2], E[3]], axis=1)
    coll = np.zeros((96, 16), dtype=np.float32)
    for h in range(H):
        coll[6 * h:6 * h + 6, h] = 1.0
    c['collect'] = coll
    spw = np.zeros((48, 96), dtype=np.float32)
    spr = np.zeros((48, 96), dtype=np.float32)
    for h in range(H):
        for p in range(6):
            spw[h, 6 * h + p] = 1.0
            spr[h, 6 * h + p] = SIGNS[p]
    spw[32:48] = spw[0:16]
    spr[32:48] = spr[0:16]
    c['spread_w'] = spw
    c['spread_r'] = spr
    c['tri'] = np.triu(np.ones((128, 128), dtype=np.float32))
    pool = np.zeros((128, 64), dtype=np.float32)
    for r in range(64):
        pool[r, r] = 1.0
        pool[64 + r, r] = 1.0
    c['pool'] = pool
    c['ident64'] = np.eye(64, dtype=np.float32)
    return c


def build_nc(debug=False, np_pairs=8, n_secs=999, skip_final=False, final_lvl=4):
    nc = bass.Bass("TRN2", target_bir_lowering=False)
    dbg_list = []
    dI = {}

    def din(name, shape):
        dI[name] = nc.dram_tensor(name, list(shape), F32, kind="ExternalInput").ap()
        return dI[name]

    x_h = din("x", (T, D))
    qkv_w = din("qkv_w", (D, 3 * D))
    qkv_b = din("qkv_b", (3 * D,))
    for nm in ["w1_write", "w2_write", "w1_read", "w2_read"]:
        din(nm, (D, 64))
    geo_w = din("geo_w", (D, D))
    geo_b = din("geo_b", (D,))
    gate_w = din("gate_w", (D, 16))
    gate_b = din("gate_b", (16,))
    inc_scale = din("inc_scale", (16,))
    out_w = din("out_w", (D, D))
    out_b = din("out_b", (D,))
    ffn_w = din("ffn_weight", (D, D))
    disha = din("disha", (64, D))
    for k, v in host_consts().items():
        din("c_" + k, v.shape)
    y_h = nc.dram_tensor("y", [T, D], F32, kind="ExternalOutput").ap()
    x2s = nc.dram_tensor("x2s", [T, D], F32).ap()
    rs_hbm = nc.dram_tensor("rs_hbm", [2 * T], BF).ap()

    ctr = {}

    def inc(instr, sem, name, k=1):
        if k > 0:
            instr.then_inc(sem, k)
            ctr[name] = ctr.get(name, 0) + k
        return ctr.get(name, 0)

    def val(name):
        return ctr.get(name, 0)

    with ExitStack() as ST:
        def sbuf(name, shape, dt, st=None, side=None):
            kw = {} if side is None else {"side": side}
            return (st or ST).enter_context(nc.sbuf_tensor(name, shape, dt, **kw))

        xln_fm = sbuf("xln_fm", [128, NDC * T], BF)
        w_small = sbuf("w_small", [128, NDC * 320], BF)
        qkb_sb = sbuf("qkb_sb", [128, 24], F32)
        gb_sb = sbuf("gb_sb", [128, 8], F32)
        ob_bc = sbuf("ob_bc", [128, D], F32)
        gateb = sbuf("gateb", [16, 1], F32)
        incs = sbuf("incs", [48, 1], F32)
        cE = sbuf("cE", [128, 384], BF)
        ccoll = sbuf("ccoll", [96, 16], BF)
        cspw = sbuf("cspw", [48, 96], BF)
        cspr = sbuf("cspr", [48, 96], BF)
        ctri = sbuf("ctri", [128, 128], BF)
        cpool = sbuf("cpool", [128, 64], BF)
        cident = sbuf("cident", [64, 64], BF)
        cones = sbuf("cones", [128, 128], BF)
        disha_bf = sbuf("disha_bf", [64, D], BF)
        wstage = sbuf("wstage", [128, NDC * 256], F32)
        stats = sbuf("stats", [128, NTT * 16], F32)

        S_pre = ExitStack()
        rows_f = sbuf("rows_f", [128, 1024], F32, S_pre)
        rows_b = sbuf("rows_b", [128, 2 * T], BF, S_pre)
        omg_bc = sbuf("omg_bc", [64, T], BF, S_pre)
        g_bc = sbuf("g_bc", [64, T], BF, S_pre)
        wl_buf = sbuf("wl_buf", [96, T], BF, S_pre)
        rl_buf = sbuf("rl_buf", [96, T], BF, S_pre)
        S_comb = ExitStack()

        ps_a = ST.enter_context(nc.psum_tensor("ps_a", [128, 2048], F32))
        ps_b = ST.enter_context(nc.psum_tensor("ps_b", [128, 512], F32))
        ps_c = ST.enter_context(nc.psum_tensor("ps_c", [128, 512], F32))
        ps_d = ST.enter_context(nc.psum_tensor("ps_d", [128, 1024], F32))

        sm_d = ST.enter_context(nc.semaphore("sm_d"))
        sm_g = ST.enter_context(nc.semaphore("sm_g"))
        sm_t = ST.enter_context(nc.semaphore("sm_t"))
        sm_s = ST.enter_context(nc.semaphore("sm_s"))
        sm_v = ST.enter_context(nc.semaphore("sm_v"))
        sm_f = ST.enter_context(nc.semaphore("sm_f"))
        sm_g2 = ST.enter_context(nc.semaphore("sm_g2"))
        SEM = {"d": sm_d, "g": sm_g, "t": sm_t, "s": sm_s, "v": sm_v}

        def w3(eng, *names):
            for name in names:
                eng.wait_ge(SEM[name], val(name))

        def dump(name, src_ap, dt):
            if not debug:
                return
            p, w = src_ap.shape[0], src_ap.shape[1]
            od = nc.dram_tensor("dbg_" + name, [p, w], dt, kind="ExternalOutput").ap()
            with nc.Block() as blk:
                @blk.sync
                def _(eng):
                    w3(eng, "v", "s", "t", "g")
                    inc(eng.dma_start(out=od, in_=src_ap), sm_d, "d", 16)
                    eng.wait_ge(sm_d, val("d"))

        with ExitStack() as S0:
            cst = sbuf("cst", [128, 1024], F32, S0)
            with nc.Block() as blk:
                @blk.sync
                def _(eng):
                    ws3 = wstage.ap().rearrange("p (dc c) -> p dc c", c=256)
                    for i, nm in enumerate(["w1_write", "w2_write", "w1_read", "w2_read"]):
                        inc(eng.dma_start(out=ws3[:, :, 64 * i:64 * i + 64],
                                          in_=dI[nm].rearrange("(dc p) c -> p dc c", p=128)), sm_d, "d", 16)
                    inc(eng.dma_start(out=qkb_sb[:, :], in_=qkv_b.rearrange("(c p) -> p c", p=128), allow_slow_non_contiguous=True), sm_d, "d", 16)
                    inc(eng.dma_start(out=gb_sb[:, :], in_=geo_b.rearrange("(c p) -> p c", p=128), allow_slow_non_contiguous=True), sm_d, "d", 16)
                    inc(eng.dma_start(out=gateb[:, :], in_=gate_b.rearrange("(a b) -> a b", b=1)), sm_d, "d", 16)
                    inc(eng.dma_start(out=incs[0:16, :], in_=inc_scale.rearrange("(a b) -> a b", b=1)), sm_d, "d", 16)
                    inc(eng.dma_start(out=incs[32:48, :], in_=inc_scale.rearrange("(a b) -> a b", b=1)), sm_d, "d", 16)
                @blk.vector
                def _(eng):
                    w3(eng, "d")
                    for dc in range(NDC):
                        inc(eng.tensor_copy(w_small[:, 320 * dc:320 * dc + 256],
                                            wstage[:, 256 * dc:256 * dc + 256]), sm_v, "v")
                @blk.gpsimd
                def _(eng):
                    eng.memset(cones[:, :], 1.0)
                    inc(eng.engine_nop(), sm_g, "g")
            with nc.Block() as blk:
                @blk.sync
                def _(eng):
                    ws3 = wstage.ap().rearrange("p (dc c) -> p dc c", c=256)
                    inc(eng.dma_start(out=ws3[:, :, 0:16],
                                      in_=gate_w.rearrange("(dc p) c -> p dc c", p=128)), sm_d, "d", 16)
                    inc(eng.dma_start(out=cst[0:128, 0:384], in_=dI["c_E"]), sm_d, "d", 16)
                    inc(eng.dma_start(out=cst[0:96, 384:400], in_=dI["c_collect"]), sm_d, "d", 16)
                    inc(eng.dma_start(out=cst[0:128, 400:528], in_=dI["c_tri"]), sm_d, "d", 16)
                    inc(eng.dma_start(out=cst[0:128, 528:592], in_=dI["c_pool"]), sm_d, "d", 16)
                    inc(eng.dma_start(out=cst[0:64, 592:656], in_=dI["c_ident64"]), sm_d, "d", 16)
                    inc(eng.dma_start(out=cst[0:48, 656:752], in_=dI["c_spread_w"]), sm_d, "d", 16)
                    inc(eng.dma_start(out=cst[0:48, 752:848], in_=dI["c_spread_r"]), sm_d, "d", 16)
                @blk.vector
                def _(eng):
                    w3(eng, "d")
                    for dc in range(NDC):
                        inc(eng.tensor_copy(w_small[:, 320 * dc + 256:320 * dc + 272],
                                            wstage[:, 256 * dc:256 * dc + 16]), sm_v, "v")
                    inc(eng.tensor_copy(cE[:, :], cst[0:128, 0:384]), sm_v, "v")
                    inc(eng.tensor_copy(ccoll[:, :], cst[0:96, 384:400]), sm_v, "v")
                    inc(eng.tensor_copy(ctri[:, :], cst[0:128, 400:528]), sm_v, "v")
                    inc(eng.tensor_copy(cpool[:, :], cst[0:128, 528:592]), sm_v, "v")
                    inc(eng.tensor_copy(cident[:, :], cst[0:64, 592:656]), sm_v, "v")
                    inc(eng.tensor_copy(cspw[:, :], cst[0:48, 656:752]), sm_v, "v")
                    inc(eng.tensor_scalar(cspr[0:16, :], cst[0:16, 752:848], incs[0:16, :], None, ALU.mult, ALU.bypass), sm_v, "v")
                    inc(eng.tensor_scalar(cspr[32:48, :], cst[32:48, 752:848], incs[32:48, :], None, ALU.mult, ALU.bypass), sm_v, "v")
            with nc.Block() as blk:
                @blk.sync
                def _(eng):
                    inc(eng.dma_start(out=cst[0:64, 0:D], in_=disha), sm_d, "d", 16)
                    inc(eng.dma_start(out=cst[64:65, 0:D], in_=out_b.rearrange("(a d) -> a d", a=1)), sm_d, "d", 16)
                @blk.vector
                def _(eng):
                    w3(eng, "d")
                    inc(eng.tensor_copy(disha_bf[:, :], cst[0:64, 0:D]), sm_v, "v")
                    inc(eng.tensor_copy(rows_b[64:65, 0:D], cst[64:65, 0:D]), sm_v, "v")
                    inc(eng.tensor_tensor(rows_b[64:65, T:T + D], cst[64:65, 0:D], rows_b[64:65, 0:D], ALU.subtract), sm_v, "v")
                @blk.tensor
                def _(eng):
                    w3(eng, "v", "g")
                    for i in range(2):
                        eng.matmul(ps_d[:, 512 * i:512 * i + 512], cones[64:65, 0:128],
                                   rows_b[64:65, 512 * i:512 * i + 512], start=True, stop=False)
                        inc(eng.matmul(ps_d[:, 512 * i:512 * i + 512], cones[64:65, 0:128],
                                       rows_b[64:65, T + 512 * i:T + 512 * i + 512], start=False, stop=True), sm_t, "t")
                @blk.scalar
                def _(eng):
                    w3(eng, "t")
                    inc(eng.activation(ob_bc[:, :], ps_d[:, 0:1024], AF.Copy), sm_s, "s")

        def emit_ln(blk, src, dst, waits):
            @blk.vector
            def _(eng):
                w3(eng, *waits)
                for tt in range(NTT):
                    c0 = 16 * tt
                    eng.bn_stats(stats[:, c0:c0 + 6], src[:, 1024 * tt:1024 * tt + 512])
                    eng.bn_stats(stats[:, c0 + 6:c0 + 12], src[:, 1024 * tt + 512:1024 * tt + 1024])
                for tt in range(NTT):
                    c0 = 16 * tt
                    eng.bn_aggr(stats[:, c0 + 12:c0 + 14], stats[:, c0:c0 + 12])
                for tt in range(NTT):
                    c0 = 16 * tt
                    inc(eng.tensor_scalar(stats[:, c0 + 13:c0 + 14], stats[:, c0 + 13:c0 + 14],
                                          1e-5, None, ALU.add, ALU.bypass), sm_v, "v")
            @blk.scalar
            def _(eng):
                w3(eng, "v")
                for tt in range(NTT):
                    c0 = 16 * tt
                    eng.activation(stats[:, c0 + 13:c0 + 14], stats[:, c0 + 13:c0 + 14], AF.Ln)
                    inc(eng.activation(stats[:, c0 + 14:c0 + 15], stats[:, c0 + 13:c0 + 14],
                                       AF.Exp, scale=-0.5), sm_s, "s")
            @blk.vector
            def _(eng):
                w3(eng, "s")
                for tt in range(NTT):
                    c0 = 16 * tt
                    inc(eng.tensor_scalar(dst[:, 1024 * tt:1024 * tt + 1024],
                                          src[:, 1024 * tt:1024 * tt + 1024],
                                          stats[:, c0 + 12:c0 + 13], stats[:, c0 + 14:c0 + 15],
                                          ALU.subtract, ALU.mult), sm_v, "v")

        def load_cast(blk, jobs):
            @blk.sync
            def _(eng):
                off = 0
                for src, c0, w in jobs:
                    ws3 = wstage.ap().rearrange("p (dc c) -> p dc c", c=256)
                    inc(eng.dma_start(out=ws3[:, :, off:off + w], in_=src), sm_d, "d", 16)
                    off += w
            @blk.vector
            def _(eng):
                w3(eng, "d")
                off = 0
                for src, c0, w in jobs:
                    for dc in range(NDC):
                        inc(eng.tensor_copy(wbf[:, 256 * dc + c0:256 * dc + c0 + w],
                                            wstage[:, 256 * dc + off:256 * dc + off + w]), sm_v, "v")
                    off += w

        with ExitStack() as S1:
            x_res = sbuf("x_res", [128, NTT * 1024], F32, S1)
            xln_tok = sbuf("xln_tok", [128, NTT * 1024], BF, S1)
            with nc.Block() as blk:
                @blk.sync
                def _(eng):
                    inc(eng.dma_start(out=x_res.ap().rearrange("p (tt d) -> p tt d", d=1024),
                                      in_=x_h.rearrange("(tt p) d -> p tt d", p=128)), sm_d, "d", 16)
            with nc.Block() as blk:
                emit_ln(blk, x_res.ap(), xln_tok.ap(), ("d",))
            with nc.Block() as blk:
                @blk.sync
                def _(eng):
                    w3(eng, "v")
                    xf = xln_fm.ap().rearrange("p (dc t) -> p dc t", t=T)
                    for tt in range(NTT):
                        inc(eng.dma_start(out=xf[:, :, 128 * tt:128 * tt + 128],
                                          in_=xln_tok[:, 1024 * tt:1024 * tt + 1024],
                                          transpose=True), sm_d, "d", 16)
                    eng.wait_ge(sm_d, val("d"))
        dump("xln_fm", xln_fm.ap(), BF)

        with ExitStack() as S2:
            pw_w = sbuf("pw_w", [128, T], BF, S2)
            pw_r = sbuf("pw_r", [128, T], BF, S2)
            gate_sig = sbuf("gate_sig", [16, T], BF, S2)
            gw1xi = sbuf("gw1xi", [96, T + 4], BF, S2)
            gw1xj = sbuf("gw1xj", [96, T + 4], BF, S2)
            r1xi = sbuf("r1xi", [96, 512], BF, S2)
            r1xj = sbuf("r1xj", [96, 512], BF, S2)
            m1s = sbuf("m1s", [96, 512], F32, S2)
            m2s = sbuf("m2s", [96, 512], F32, S2)
            Lwsq = sbuf("Lwsq", [96, 512], BF, S2)
            Lrsq = sbuf("Lrsq", [96, 512], BF, S2)
            Lw = sbuf("Lw", [96, 512], BF, S2)
            Lr = sbuf("Lr", [96, 512], BF, S2)
            inv_sb = sbuf("inv_sb", [48, T], BF, S2)

            with nc.Block() as blk:
                t0, s0 = val("t"), val("s")
                @blk.tensor
                def _(eng):
                    for st_ in range(NST):
                        bank = ps_b if st_ % 2 == 0 else ps_c
                        if st_ >= 2:
                            eng.wait_ge(sm_s, s0 + st_ - 1)
                        for dc in range(NDC):
                            inc(eng.matmul(bank[0:16, :], w_small[:, 320 * dc + 256:320 * dc + 272],
                                           xln_fm[:, T * dc + 512 * st_:T * dc + 512 * st_ + 512],
                                           start=(dc == 0), stop=(dc == NDC - 1)),
                                sm_t, "t", 1 if dc == NDC - 1 else 0)
                @blk.scalar
                def _(eng):
                    for st_ in range(NST):
                        eng.wait_ge(sm_t, t0 + st_ + 1)
                        inc(eng.activation(gate_sig[:, 512 * st_:512 * st_ + 512],
                                           (ps_b if st_ % 2 == 0 else ps_c)[0:16, :],
                                           AF.Sigmoid, bias=gateb[0:16, :]), sm_s, "s")
            with nc.Block() as blk:
                @blk.tensor
                def _(eng):
                    w3(eng, "s")
                    for st_ in range(NST):
                        inc(eng.matmul(ps_a[0:1, 512 * st_:512 * st_ + 512], cones[0:16, 0:1],
                                       gate_sig[:, 512 * st_:512 * st_ + 512], start=True, stop=True), sm_t, "t")
                @blk.scalar
                def _(eng):
                    w3(eng, "t")
                    inc(eng.activation(rows_f[0:1, 0:1024], ps_a[0:1, 0:1024], AF.Copy, scale=1.0 / 16.0), sm_s, "s")
                @blk.vector
                def _(eng):
                    w3(eng, "s")
                    inc(eng.tensor_copy(rows_b[0:1, 0:1024], rows_f[0:1, 0:1024]), sm_v, "v")
                    inc(eng.tensor_tensor(rows_b[0:1, T:T + 1024], rows_f[0:1, 0:1024], rows_b[0:1, 0:1024], ALU.subtract), sm_v, "v")
                @blk.scalar
                def _(eng):
                    w3(eng, "v")
                    inc(eng.activation(rows_f[0:1, 0:1024], ps_a[0:1, 1024:2048], AF.Copy, scale=1.0 / 16.0), sm_s, "s")
                @blk.vector
                def _(eng):
                    w3(eng, "s")
                    inc(eng.tensor_copy(rows_b[0:1, 1024:2048], rows_f[0:1, 0:1024]), sm_v, "v")
                    inc(eng.tensor_tensor(rows_b[0:1, T + 1024:2 * T], rows_f[0:1, 0:1024], rows_b[0:1, 1024:2048], ALU.subtract), sm_v, "v")
            for half in range(2):
                c0 = 1024 * half
                with nc.Block() as blk:
                    @blk.tensor
                    def _(eng):
                        w3(eng, "v")
                        for q in range(2):
                            eng.matmul(ps_d[0:64, 512 * q:512 * q + 512], cones[0:1, 0:64],
                                       rows_b[0:1, c0 + 512 * q:c0 + 512 * q + 512], start=True, stop=False)
                            inc(eng.matmul(ps_d[0:64, 512 * q:512 * q + 512], cones[0:1, 0:64],
                                           rows_b[0:1, T + c0 + 512 * q:T + c0 + 512 * q + 512], start=False, stop=True), sm_t, "t")
                    @blk.vector
                    def _(eng):
                        w3(eng, "t")
                        inc(eng.tensor_copy(g_bc[:, c0:c0 + 1024], ps_d[0:64, :]), sm_v, "v")
                        inc(eng.tensor_scalar(omg_bc[:, c0:c0 + 1024], ps_d[0:64, :], 1.0, -1.0, ALU.subtract, ALU.mult), sm_v, "v")

            dump("g_bc", g_bc.ap(), F32)
            dump("omg_bc", omg_bc.ap(), F32)
            with nc.Block() as blk:
                t0, v0 = val("t"), val("v")
                @blk.tensor
                def _(eng):
                    j = 0
                    for which in (0, 128):
                        for st_ in range(NST):
                            bank = ps_b if j % 2 == 0 else ps_c
                            if j >= 2:
                                eng.wait_ge(sm_v, v0 + j - 1)
                            for dc in range(NDC):
                                inc(eng.matmul(bank[:, :], w_small[:, 320 * dc + which:320 * dc + which + 128],
                                               xln_fm[:, T * dc + 512 * st_:T * dc + 512 * st_ + 512],
                                               start=(dc == 0), stop=(dc == NDC - 1)),
                                    sm_t, "t", 1 if dc == NDC - 1 else 0)
                            j += 1
                @blk.vector
                def _(eng):
                    j = 0
                    for which, dst in [(0, pw_w), (128, pw_r)]:
                        for st_ in range(NST):
                            bank = ps_b if j % 2 == 0 else ps_c
                            eng.wait_ge(sm_t, t0 + j + 1)
                            inc(eng.tensor_copy(dst[:, 512 * st_:512 * st_ + 512], bank[:, :]), sm_v, "v")
                            j += 1

            with nc.Block() as blk:
                @blk.gpsimd
                def _(eng):
                    eng.memset(gw1xi[:, 0:1], 0.0)
                    eng.memset(gw1xj[:, 0:1], 0.0)
                    inc(eng.engine_nop(), sm_g, "g")

            for st_ in range(NST):
                q0 = 512 * st_
                with nc.Block() as blk:
                    @blk.tensor
                    def _(eng):
                        w3(eng, "v", "g")
                        inc(eng.matmul(ps_a[0:96, 0:512], cE[0:64, 0:96], pw_w[0:64, q0:q0 + 512], start=True, stop=True), sm_t, "t")
                        inc(eng.matmul(ps_a[0:96, 512:1024], cE[0:64, 96:192], pw_w[0:64, q0:q0 + 512], start=True, stop=True), sm_t, "t")
                        inc(eng.matmul(ps_a[0:96, 1024:1536], cE[64:128, 0:96], pw_w[64:128, q0:q0 + 512], start=True, stop=True), sm_t, "t")
                        inc(eng.matmul(ps_a[0:96, 1536:2048], cE[64:128, 96:192], pw_w[64:128, q0:q0 + 512], start=True, stop=True), sm_t, "t")
                        inc(eng.matmul(ps_d[0:96, 0:512], cE[0:64, 192:288], pw_r[0:64, q0:q0 + 512], start=True, stop=True), sm_t, "t")
                        inc(eng.matmul(ps_d[0:96, 512:1024], cE[0:64, 288:384], pw_r[0:64, q0:q0 + 512], start=True, stop=True), sm_t, "t")
                        inc(eng.matmul(ps_b[0:96, :], cE[64:128, 192:288], pw_r[64:128, q0:q0 + 512], start=True, stop=True), sm_t, "t")
                        inc(eng.matmul(ps_c[0:96, :], cE[64:128, 288:384], pw_r[64:128, q0:q0 + 512], start=True, stop=True), sm_t, "t")
                    @blk.vector
                    def _(eng):
                        w3(eng, "t")
                        inc(eng.tensor_copy(gw1xi[:, q0 + 1:q0 + 513], ps_a[0:96, 0:512]), sm_v, "v")
                        inc(eng.tensor_copy(gw1xj[:, q0 + 1:q0 + 513], ps_a[0:96, 512:1024]), sm_v, "v")
                        inc(eng.tensor_copy(r1xi[:, :], ps_d[0:96, 0:512]), sm_v, "v")
                        inc(eng.tensor_copy(r1xj[:, :], ps_d[0:96, 512:1024]), sm_v, "v")
                        inc(eng.tensor_tensor(m1s[:, :], gw1xi[:, q0:q0 + 512], ps_a[0:96, 1536:2048], ALU.mult), sm_v, "v")
                        inc(eng.tensor_tensor(m2s[:, :], gw1xj[:, q0:q0 + 512], ps_a[0:96, 1024:1536], ALU.mult), sm_v, "v")
                        inc(eng.tensor_tensor(Lw[:, :], m1s[:, :], m2s[:, :], ALU.subtract), sm_v, "v")
                        inc(eng.tensor_tensor(m1s[:, :], r1xi[:, :], ps_c[0:96, :], ALU.mult), sm_v, "v")
                        inc(eng.tensor_tensor(m2s[:, :], r1xj[:, :], ps_b[0:96, :], ALU.mult), sm_v, "v")
                        inc(eng.tensor_tensor(Lr[:, :], m1s[:, :], m2s[:, :], ALU.subtract), sm_v, "v")
                        inc(eng.tensor_tensor(Lwsq[:, :], Lw[:, :], Lw[:, :], ALU.mult), sm_v, "v")
                        inc(eng.tensor_tensor(Lrsq[:, :], Lr[:, :], Lr[:, :], ALU.mult), sm_v, "v")
                    @blk.tensor
                    def _(eng):
                        w3(eng, "v")
                        inc(eng.matmul(ps_b[0:16, :], ccoll[:, :], Lwsq[:, :], start=True, stop=True), sm_t, "t")
                        inc(eng.matmul(ps_b[32:48, :], ccoll[:, :], Lrsq[:, :], start=True, stop=True), sm_t, "t")
                    @blk.vector
                    def _(eng):
                        w3(eng, "t")
                        inc(eng.tensor_scalar(ps_b[0:16, :], ps_b[0:16, :], 1e-19, None, ALU.max, ALU.bypass), sm_v, "v")
                        inc(eng.tensor_scalar(ps_b[32:48, :], ps_b[32:48, :], 1e-19, None, ALU.max, ALU.bypass), sm_v, "v")
                    @blk.scalar
                    def _(eng):
                        w3(eng, "v")
                        eng.activation(ps_b[0:16, :], ps_b[0:16, :], AF.Ln)
                        eng.activation(ps_b[32:48, :], ps_b[32:48, :], AF.Ln)
                        inc(eng.activation(inv_sb[0:16, q0:q0 + 512], ps_b[0:16, :], AF.Exp, scale=-0.5), sm_s, "s")
                        inc(eng.activation(inv_sb[32:48, q0:q0 + 512], ps_b[32:48, :], AF.Exp, scale=-0.5), sm_s, "s")
                    @blk.tensor
                    def _(eng):
                        w3(eng, "s")
                        inc(eng.matmul(ps_a[0:96, 0:512], cspw[0:16, :], inv_sb[0:16, q0:q0 + 512], start=True, stop=True), sm_t, "t")
                        inc(eng.matmul(ps_a[0:96, 512:1024], cspr[32:48, :], inv_sb[32:48, q0:q0 + 512], start=True, stop=True), sm_t, "t")
                    @blk.vector
                    def _(eng):
                        w3(eng, "t")
                        inc(eng.tensor_tensor(wl_buf[:, q0:q0 + 512], Lw[:, :], ps_a[0:96, 0:512], ALU.mult), sm_v, "v")
                        inc(eng.tensor_tensor(rl_buf[:, q0:q0 + 512], Lr[:, :], ps_a[0:96, 512:1024], ALU.mult), sm_v, "v")
            dump("pw_w", pw_w.ap(), BF)
            dump("pw_r", pw_r.ap(), BF)
            dump("inv_dbg", inv_sb.ap(), BF)
        dump("wl_buf", wl_buf.ap(), BF)
        dump("rl_buf", rl_buf.ap(), BF)

        comb = sbuf("comb", [128, 8 * T], BF, S_comb, side="right")
        with ExitStack() as S3:
            qt_p = sbuf("qt_p", [128, T], BF, S3)
            kt_p = sbuf("kt_p", [128, T], BF, S3)
            vfm = sbuf("vfm", [128, T], BF, S3)
            vraw = sbuf("vraw", [128, T], BF, S3)
            v_tok = sbuf("v_tok", [128, NTT * 130], BF, S3)
            gv_tok = sbuf("gv_tok", [128, NTT * 130], BF, S3)
            NSL = 5
            pt = sbuf("pt", [128, NSL * 512], BF, S3)
            wl_scr = sbuf("wl_scr", [38, 2 * T], BF, S3)
            rl_scr = sbuf("rl_scr", [38, 2 * T], BF, S3)
            u_sb = [sbuf("u_sb%d" % i, [65, T], BF, S3) for i in range(4)]
            rsb = sbuf("rsb", [64, 2 * T], BF, S3)
            tmpb = sbuf("tmpb", [64, T], BF, S3)
            cmbB = sbuf("cmbB", [64, T], BF, S3)
            wbf2 = sbuf("wbf2", [128, 2 * NDC * 512], BF, S3)

            with nc.Block() as blk:
                @blk.gpsimd
                def _(eng):
                    vt = v_tok.ap().rearrange("p (tt s c) -> p tt s c", s=2, c=65)
                    gt = gv_tok.ap().rearrange("p (tt s c) -> p tt s c", s=2, c=65)
                    eng.memset(vt[:, :, :, 64:65], 1.0)
                    eng.memset(gt[:, :, :, 64:65], 1.0)
                    inc(eng.nop(), sm_g, "g")

            MK = {}
            NP = np_pairs
            GTI = [0]
            mask_after = {-1: 0}
            EV = {}

            def wslot(p):
                return 4096 * (p % 2)

            def emit_loads(blk, p):
                @blk.sync
                def _(eng):
                    if p > 0:
                        eng.wait_ge(sm_v, MK[("cast_vg", p - 1)])
                    ws3 = wstage.ap().rearrange("p (dc c) -> p dc c", c=256)
                    inc(eng.dma_start(out=ws3[:, :, 0:128],
                                      in_=qkv_w[:, 128 * p:128 * p + 128].rearrange("(dc p) c -> p dc c", p=128)), sm_d, "d", 16)
                    inc(eng.dma_start(out=ws3[:, :, 128:256],
                                      in_=qkv_w[:, D + 128 * p:D + 128 * p + 128].rearrange("(dc p) c -> p dc c", p=128)), sm_d, "d", 16)
                    MK[("d_qk", p)] = val("d")

            def emit_loads_vg(blk, p):
                @blk.sync
                def _(eng):
                    eng.wait_ge(sm_v, MK[("cast_qk", p)])
                    ws3 = wstage.ap().rearrange("p (dc c) -> p dc c", c=256)
                    inc(eng.dma_start(out=ws3[:, :, 0:128],
                                      in_=qkv_w[:, 2 * D + 128 * p:2 * D + 128 * p + 128].rearrange("(dc p) c -> p dc c", p=128)), sm_d, "d", 16)
                    inc(eng.dma_start(out=ws3[:, :, 128:256],
                                      in_=geo_w[:, 128 * p:128 * p + 128].rearrange("(dc p) c -> p dc c", p=128)), sm_d, "d", 16)
                    MK[("d_vg", p)] = val("d")

            def emit_cast_qk(blk, p):
                @blk.vector
                def _(eng):
                    eng.wait_ge(sm_d, MK[("d_qk", p)])
                    if p >= 2:
                        eng.wait_ge(sm_t, MK[("t_proj", p - 2)])
                    sl = wslot(p)
                    for dc in range(NDC):
                        inc(eng.tensor_copy(wbf2[:, sl + 512 * dc:sl + 512 * dc + 256],
                                            wstage[:, 256 * dc:256 * dc + 256]), sm_v, "v")
                    MK[("cast_qk", p)] = val("v")

            def emit_cast_vg(blk, p):
                @blk.vector
                def _(eng):
                    eng.wait_ge(sm_d, MK[("d_vg", p)])
                    sl = wslot(p)
                    for dc in range(NDC):
                        inc(eng.tensor_copy(wbf2[:, sl + 512 * dc + 256:sl + 512 * dc + 512],
                                            wstage[:, 256 * dc:256 * dc + 256]), sm_v, "v")
                    MK[("cast_vg", p)] = val("v")

            def emit_extract(blk, p):
                @blk.sync
                def _(eng):
                    if p >= 2:
                        eng.wait_ge(sm_t, MK[("t_geo_done", p - 2)])
                    eng.wait_ge(sm_d, val("d"))
                    c0 = T * (p % 2)
                    hA, hB = 2 * p, 2 * p + 1
                    inc(eng.dma_start(out=wl_scr[0:6, c0:c0 + T], in_=wl_buf[6 * hA:6 * hA + 6, :]), sm_d, "d", 16)
                    inc(eng.dma_start(out=wl_scr[32:38, c0:c0 + T], in_=wl_buf[6 * hB:6 * hB + 6, :]), sm_d, "d", 16)
                    inc(eng.dma_start(out=rl_scr[0:6, c0:c0 + T], in_=rl_buf[6 * hA:6 * hA + 6, :]), sm_d, "d", 16)
                    inc(eng.dma_start(out=rl_scr[32:38, c0:c0 + T], in_=rl_buf[6 * hB:6 * hB + 6, :]), sm_d, "d", 16)
                    MK[("d_ext", p)] = val("d")

            def emit_proj(blk, p):
                sl = wslot(p)
                t0 = val("t")
                v0 = val("v")
                @blk.tensor
                def _(eng):
                    eng.wait_ge(sm_v, MK[("cast_qk", p)])
                    j = 0
                    for which in (0, 128):
                        for st_ in range(NST):
                            bank = ps_b if j % 2 == 0 else ps_c
                            if j >= 2:
                                eng.wait_ge(sm_v, v0 + j - 1)
                            for dc in range(NDC):
                                inc(eng.matmul(bank[:, :], wbf2[:, sl + 512 * dc + which:sl + 512 * dc + which + 128],
                                               xln_fm[:, T * dc + 512 * st_:T * dc + 512 * st_ + 512],
                                               start=(dc == 0), stop=(dc == NDC - 1)),
                                    sm_t, "t", 1 if dc == NDC - 1 else 0)
                            j += 1
                @blk.vector
                def _(eng):
                    j = 0
                    for dst, bcol in [(qt_p, p), (kt_p, 8 + p)]:
                        for st_ in range(NST):
                            bank = ps_b if j % 2 == 0 else ps_c
                            eng.wait_ge(sm_t, t0 + j + 1)
                            inc(eng.tensor_scalar(dst[:, 512 * st_:512 * st_ + 512], bank[:, :],
                                                  qkb_sb[:, bcol:bcol + 1], None, ALU.add, ALU.bypass), sm_v, "v")
                            j += 1
                for vi in range(2):
                    t1 = val("t")
                    v1 = val("v")
                    which = 256 if vi == 0 else 384
                    @blk.tensor
                    def _(eng):
                        eng.wait_ge(sm_v, MK[("cast_vg", p)])
                        for st_ in range(NST):
                            bank = ps_b if st_ % 2 == 0 else ps_c
                            if st_ >= 2:
                                eng.wait_ge(sm_v, v1 + st_ - 1)
                            for dc in range(NDC):
                                inc(eng.matmul(bank[:, :], wbf2[:, sl + 512 * dc + which:sl + 512 * dc + which + 128],
                                               xln_fm[:, T * dc + 512 * st_:T * dc + 512 * st_ + 512],
                                               start=(dc == 0), stop=(dc == NDC - 1)),
                                    sm_t, "t", 1 if dc == NDC - 1 else 0)
                    MK[("t_proj", p)] = val("t")
                    @blk.vector
                    def _(eng):
                        if vi == 1:
                            eng.wait_ge(sm_d, MK[("d_trV", p)])
                        for st_ in range(NST):
                            bank = ps_b if st_ % 2 == 0 else ps_c
                            eng.wait_ge(sm_t, t1 + st_ + 1)
                            bias = qkb_sb[:, 16 + p:17 + p] if vi == 0 else gb_sb[:, p:p + 1]
                            inc(eng.tensor_scalar(vfm[:, 512 * st_:512 * st_ + 512], bank[:, :],
                                                  bias, None, ALU.add, ALU.bypass), sm_v, "v")
                    MK[("v_vfm", p, vi)] = val("v")
                    @blk.sync
                    def _(eng):
                        eng.wait_ge(sm_v, MK[("v_vfm", p, vi)])
                        if vi == 0 and p >= 1:
                            eng.wait_ge(sm_v, MK[("v_scat", p - 1, 1)])
                        if vi == 1:
                            eng.wait_ge(sm_v, MK[("v_scat", p, 0)])
                        eng.wait_ge(sm_d, val("d"))
                        inc(eng.dma_start(out=vraw.ap().rearrange("p (m c) -> p m c", c=128),
                                          in_=vfm[:, :], transpose=True), sm_d, "d", 16)
                        MK[("d_trV" if vi == 0 else "d_trG", p)] = val("d")
                    @blk.vector
                    def _(eng):
                        eng.wait_ge(sm_d, MK[("d_trV" if vi == 0 else "d_trG", p)])
                        if p >= 1 and vi == 0:
                            eng.wait_ge(sm_t, MK[("t_pv_done", p - 1, 3)])
                        dst = (v_tok if vi == 0 else gv_tok).ap().rearrange("p (tt s c) -> p tt s c", s=2, c=65)
                        inc(eng.tensor_copy(dst[:, :, :, 0:64],
                                            vraw.ap().rearrange("p (tt s c) -> p tt s c", s=2, c=64)), sm_v, "v")
                        MK[("v_scat", p, vi)] = val("v")

            def emit_rs(blk, sec, attn_i, ubuf, pv_mark):
                @blk.scalar
                def _(eng):
                    eng.wait_ge(sm_v, MK[("v_evac4", sec)])
                    for hh in range(2):
                        eng.activation(rows_f[64:65, 0:1024], ubuf[64:65, 1024 * hh:1024 * hh + 1024], AF.Ln)
                        inc(eng.activation(rows_b[64:65, T * attn_i + 1024 * hh:T * attn_i + 1024 * hh + 1024],
                                           rows_f[64:65, 0:1024], AF.Exp, scale=-1.0), sm_s, "s")
                MK[("rs", sec)] = val("s")

            def emit_section(blk, p, hi, attn_i, sec, prev_sec, pend_rs_info, pend_tail_info):
                is_std = attn_i == 0
                scale = 0.125 if is_std else 1.0
                vt3 = (v_tok if is_std else gv_tok).ap().rearrange("p (tt s c) -> p tt s c", s=2, c=65)
                ubuf = u_sb[sec % 4]
                e0, m0, t0 = val("s"), val("v"), val("t")
                g0 = GTI[0]

                rs_bump = 2 if pend_rs_info is not None else 0
                exp_no = {}
                for i in range(N_TILES):
                    exp_no[i] = e0 + (i + 1) + (rs_bump if i >= 4 else 0)
                if pend_rs_info is not None:
                    MK[("rs", pend_rs_info[0])] = e0 + 6
                nm = 0
                for i, (s_, kc, off, diag) in enumerate(TILES):
                    if diag:
                        nm += 1
                    mask_after[g0 + i] = m0 + nm
                SBANKS = [ps_b[0:128, 0:512], ps_c[0:128, 0:512],
                          ps_d[0:128, 0:512], ps_d[0:128, 512:1024]]
                DEPTH = 4
                if pend_tail_info is not None:
                    emit_fdma(blk, *pend_tail_info)
                @blk.tensor
                def _(eng):
                    if is_std:
                        eng.wait_ge(sm_v, MK[("v_scat", p, 0)])
                        eng.wait_ge(sm_v, MK[("v_vfm", p, 1)])
                        if ("v_perm_done", p - 1) in MK:
                            eng.wait_ge(sm_v, MK[("v_perm_done", p - 1)])
                    else:
                        eng.wait_ge(sm_v, MK[("v_scat", p, 1)])
                        eng.wait_ge(sm_d, MK[("d_ext", p)])
                    def S_mm(i):
                        s_, kc, off, diag = TILES[i]
                        bank = SBANKS[i % 4]
                        if is_std:
                            lhsT = kt_p[64 * hi:64 * hi + 64, 128 * kc:128 * kc + 128]
                            rhs = qt_p[64 * hi:64 * hi + 64, 512 * s_ + off:512 * s_ + 512]
                        else:
                            b0 = 32 * hi
                            c0 = T * (p % 2)
                            lhsT = wl_scr[b0:b0 + 6, c0 + 128 * kc:c0 + 128 * kc + 128]
                            rhs = rl_scr[b0:b0 + 6, c0 + 512 * s_ + off:c0 + 512 * s_ + 512]
                        inc(eng.matmul(bank[0:128, off:512], lhsT, rhs, start=True, stop=True), sm_t, "t")
                    for i0 in range(DEPTH):
                        S_mm(i0)
                    for g in range(N_TILES // 4):
                        i0 = 4 * g
                        eng.wait_ge(sm_s, exp_no[i0 + 3])
                        dmax = max((i for i in range(i0, i0 + 4) if TILES[i][3]), default=None)
                        if dmax is not None:
                            eng.wait_ge(sm_v, mask_after[g0 + dmax])
                        if prev_sec is not None:
                            for i in range(i0, i0 + 4):
                                if TILES[i][1] == 0:
                                    eng.wait_ge(sm_v, EV[(prev_sec, TILES[i][0])])
                        for i in range(i0, i0 + 4):
                            s_, kc, off, diag = TILES[i]
                            eng.matmul(ps_a[0:65, 512 * s_ + off:512 * s_ + 512],
                                       vt3[:, kc, hi, 0:65],
                                       pt[:, 512 * ((g0 + i) % NSL) + off:512 * ((g0 + i) % NSL) + 512],
                                       start=(kc == 0), stop=(kc == 4 * s_ + 3))
                        for i in range(i0 + 4, min(i0 + 8, N_TILES)):
                            S_mm(i)
                    inc(eng.nop(), sm_t, "t")
                    MK[("t_pv_done", p, hi * 2 + attn_i)] = val("t")
                    if hi == 1 and attn_i == 1:
                        MK[("t_geo_done", p)] = val("t")
                @blk.scalar
                def _(eng):
                    for g in range(N_TILES // 4):
                        i0 = 4 * g
                        eng.wait_ge(sm_t, t0 + i0 + 4)
                        eng.wait_ge(sm_v, mask_after.get(g0 + i0 + 3 - NSL, 0))
                        for i in range(i0, i0 + 4):
                            s_, kc, off, diag = TILES[i]
                            bank = SBANKS[i % 4]
                            inc(eng.activation(pt[:, 512 * ((g0 + i) % NSL) + off:512 * ((g0 + i) % NSL) + 512],
                                               bank[0:128, off:512], AF.Exp, scale=scale), sm_s, "s")
                        if g == 0 and pend_rs_info is not None:
                            sec_pv, attn_pv, ubuf_pv = pend_rs_info
                            eng.wait_ge(sm_v, MK[("v_evac4", sec_pv)])
                            for hh in range(2):
                                eng.activation(rows_f[64:65, 0:1024],
                                               ubuf_pv[64:65, 1024 * hh:1024 * hh + 1024], AF.Ln)
                                inc(eng.activation(rows_b[64:65, T * attn_pv + 1024 * hh:T * attn_pv + 1024 * hh + 1024],
                                                   rows_f[64:65, 0:1024], AF.Exp, scale=-1.0), sm_s, "s")
                @blk.vector
                def _(eng):
                    for i in range(N_TILES):
                        s_, kc, off, diag = TILES[i]
                        if pend_tail_info is not None and i == 11:
                            tail_vec_part1(eng, *pend_tail_info)
                        if pend_tail_info is not None and i == 13:
                            tail_vec_part2(eng, *pend_tail_info)
                        if diag:
                            eng.wait_ge(sm_s, exp_no[i])
                            inc(eng.tensor_tensor(pt[:, 512 * ((g0 + i) % NSL) + off:512 * ((g0 + i) % NSL) + off + 128],
                                                  pt[:, 512 * ((g0 + i) % NSL) + off:512 * ((g0 + i) % NSL) + off + 128],
                                                  ctri[:, :], ALU.mult), sm_v, "v")
                @blk.vector
                def _(eng):
                    eng.wait_ge(sm_t, MK[("t_pv_done", p, hi * 2 + attn_i)])
                    for s_ in range(NST):
                        inc(eng.tensor_copy(ubuf[0:65, 512 * s_:512 * s_ + 512],
                                            ps_a[0:65, 512 * s_:512 * s_ + 512]), sm_v, "v")
                        EV[(sec, s_)] = val("v")
                MK[("v_evac4", sec)] = val("v")
                GTI[0] = g0 + N_TILES

            def emit_fdma(blk, p_t, hi_t):
                @blk.sync
                def _(eng):
                    eng.wait_ge(sm_s, MK[("rs", p_t * 4 + hi_t * 2 + 1)])
                    eng.wait_ge(sm_v, val("v"))
                    eng.wait_ge(sm_d, val("d"))
                    inc(eng.dma_start(out=rs_hbm.rearrange("(a c) -> a c", a=1),
                                      in_=rows_b.ap()[64:65, 0:2 * T]), sm_d, "d", 16)
                    eng.wait_ge(sm_d, val("d"))
                    inc(eng.dma_start(out=rsb[:, :],
                                      in_=rs_hbm.rearrange("(a c) -> a c", a=1).broadcast_to([64, 2 * T])), sm_d, "d", 16)
                    MK[("d_f", p_t, hi_t)] = val("d")

            def tail_vec_part1(eng, p_t, hi_t):
                sec_std = p_t * 4 + hi_t * 2
                ub_s = u_sb[sec_std % 4]
                dst = comb.ap()[0:64, T * p_t:T * p_t + T] if hi_t == 0 else cmbB.ap()[:, :]
                eng.wait_ge(sm_d, MK[("d_f", p_t, hi_t)])
                inc(eng.tensor_tensor(tmpb[:, :], ub_s[0:64, :], rsb[:, 0:T], ALU.mult), sm_v, "v")
                inc(eng.tensor_tensor(dst, tmpb[:, :], omg_bc[:, :], ALU.mult), sm_v, "v")

            def tail_vec_part2(eng, p_t, hi_t):
                sec_geo = p_t * 4 + hi_t * 2 + 1
                ub_g = u_sb[sec_geo % 4]
                dst = comb.ap()[0:64, T * p_t:T * p_t + T] if hi_t == 0 else cmbB.ap()[:, :]
                inc(eng.tensor_tensor(tmpb[:, :], ub_g[0:64, :], rsb[:, T:2 * T], ALU.mult), sm_v, "v")
                inc(eng.tensor_tensor(tmpb[:, :], tmpb[:, :], g_bc[:, :], ALU.mult), sm_v, "v")
                inc(eng.tensor_tensor(dst, dst, tmpb[:, :], ALU.add), sm_v, "v")
                MK[("v_comb", p_t, hi_t)] = val("v")

            def emit_tail_vector(blk, p_t, hi_t):
                @blk.vector
                def _(eng):
                    tail_vec_part1(eng, p_t, hi_t)
                    tail_vec_part2(eng, p_t, hi_t)

            def emit_permuteB(blk, p_t):
                t0 = val("t")
                v0p = val("v")
                @blk.tensor
                def _(eng):
                    eng.wait_ge(sm_v, MK[("v_comb", p_t, 1)])
                    eng.wait_ge(sm_v, MK[("v_vfm", p_t + 1, 1)] if ("v_vfm", p_t + 1, 1) in MK else 0)
                    for q in range(4):
                        if q == 2:
                            eng.wait_ge(sm_v, v0p + 1)
                        bank = ps_b if q % 2 == 0 else ps_c
                        inc(eng.matmul(bank[64:128, 0:512], cident[:, :],
                                       cmbB[:, 512 * q:512 * q + 512], start=True, stop=True), sm_t, "t")
                @blk.vector
                def _(eng):
                    for r in range(2):
                        eng.wait_ge(sm_t, t0 + 2 * (r + 1))
                        for q2 in range(2):
                            inc(eng.tensor_copy(comb[64:128, T * p_t + 1024 * r + 512 * q2:T * p_t + 1024 * r + 512 * q2 + 512],
                                                (ps_b if q2 == 0 else ps_c)[64:128, 0:512]), sm_v, "v")
                MK[("v_perm_done", p_t)] = val("v")

            with nc.Block() as blk:
                emit_loads(blk, 0)
                emit_cast_qk(blk, 0)
                emit_loads_vg(blk, 0)
                emit_cast_vg(blk, 0)
                emit_extract(blk, 0)
                sec = 0
                prev_sec = None
                pend_rs = None
                pend_tail = None
                for p in range(NP):
                    emit_proj(blk, p)
                    if p + 1 < NP:
                        emit_loads(blk, p + 1)
                        emit_extract(blk, p + 1)
                    if p >= 1 and pend_rs is not None:
                        emit_rs(blk, pend_rs[0], pend_rs[1], pend_rs[2],
                                ("t_pv_done", p - 1, 3))
                        pend_rs = None
                        emit_fdma(blk, p - 1, 1)
                        emit_tail_vector(blk, p - 1, 1)
                        emit_permuteB(blk, p - 1)
                    if p + 1 < NP:
                        emit_cast_qk(blk, p + 1)
                        emit_loads_vg(blk, p + 1)
                        emit_cast_vg(blk, p + 1)
                    for si, (hi, attn_i) in enumerate([(0, 0), (0, 1), (1, 0), (1, 1)]):
                        if sec >= n_secs:
                            sec += 1
                            continue
                        emit_section(blk, p, hi, attn_i, sec, prev_sec,
                                     pend_rs, pend_tail)
                        pend_tail = None
                        pend_rs = (sec, attn_i, u_sb[sec % 4])
                        prev_sec = sec
                        sec += 1
                        if si == 1:
                            pend_tail = (p, 0)
                if pend_rs is not None and sec - 1 < n_secs and (NP - 1) * 4 + 3 < n_secs and not skip_final:
                    if final_lvl >= 1:
                        emit_rs(blk, pend_rs[0], pend_rs[1], pend_rs[2], ("t_pv_done", NP - 1, 3))
                        pend_rs = None
                    if final_lvl >= 2:
                        emit_fdma(blk, NP - 1, 1)
                    if final_lvl >= 3:
                        emit_tail_vector(blk, NP - 1, 1)
                    if final_lvl >= 4:
                        emit_permuteB(blk, NP - 1)
        S_pre.close()

        with ExitStack() as S_ao:
            big_f = sbuf("big_f", [128, NTT * 1024], F32, S_ao)
            wbf = sbuf("wbf", [128, NDC * 256], BF, S_ao)
            with ExitStack() as S4:
                wout = sbuf("wout", [128, NDC * D], BF, S4)
                for dc in range(NDC):
                    with nc.Block() as blk:
                        @blk.sync
                        def _(eng):
                            inc(eng.dma_start(out=wstage[:, 0:1024],
                                              in_=out_w[128 * dc:128 * dc + 128, :]), sm_d, "d", 16)
                        @blk.vector
                        def _(eng):
                            w3(eng, "d")
                            inc(eng.tensor_copy(wout[:, D * dc:D * dc + D], wstage[:, 0:1024]), sm_v, "v")
                with nc.Block() as blk:
                    t0, v0 = val("t"), val("v")
                    @blk.tensor
                    def _(eng):
                        w3(eng, "v")
                        j = 0
                        for tt in range(NTT):
                            for os_ in range(2):
                                bank = ps_b if j % 2 == 0 else ps_c
                                if j >= 2:
                                    eng.wait_ge(sm_v, v0 + j - 1)
                                for c in range(8):
                                    inc(eng.matmul(bank[:, :], comb[:, T * c + 128 * tt:T * c + 128 * tt + 128],
                                                   wout[:, D * c + 512 * os_:D * c + 512 * os_ + 512],
                                                   start=(c == 0), stop=(c == 7)),
                                        sm_t, "t", 1 if c == 7 else 0)
                                j += 1
                    @blk.vector
                    def _(eng):
                        j = 0
                        for tt in range(NTT):
                            for os_ in range(2):
                                bank = ps_b if j % 2 == 0 else ps_c
                                eng.wait_ge(sm_t, t0 + j + 1)
                                inc(eng.tensor_tensor(big_f[:, 1024 * tt + 512 * os_:1024 * tt + 512 * os_ + 512],
                                                      bank[:, :], ob_bc[:, 512 * os_:512 * os_ + 512], ALU.add), sm_v, "v")
                                j += 1
            dump("comb", comb.ap(), BF)
            dump("attn_out", big_f.ap(), F32)
            S_comb.close()

            with ExitStack() as S5:
                xstage = sbuf("xstage", [128, 2 * 1024], F32, S5)
                with nc.Block() as blk:
                    emit_ln(blk, big_f.ap(), big_f.ap(), ("v",))
                with nc.Block() as blk:
                    d0, v0 = val("d"), val("v")
                    @blk.sync
                    def _(eng):
                        for tt in range(NTT):
                            if tt >= 2:
                                eng.wait_ge(sm_v, v0 + tt - 1)
                            inc(eng.dma_start(out=xstage[:, 1024 * (tt % 2):1024 * (tt % 2) + 1024],
                                              in_=x_h[128 * tt:128 * tt + 128, :]), sm_d, "d", 16)
                    @blk.vector
                    def _(eng):
                        for tt in range(NTT):
                            eng.wait_ge(sm_d, d0 + 16 * (tt + 1))
                            inc(eng.tensor_tensor(big_f[:, 1024 * tt:1024 * tt + 1024],
                                                  big_f[:, 1024 * tt:1024 * tt + 1024],
                                                  xstage[:, 1024 * (tt % 2):1024 * (tt % 2) + 1024], ALU.add), sm_v, "v")
                dump("x2", big_f.ap(), F32)
                with nc.Block() as blk:
                    @blk.gpsimd
                    def _(eng):
                        w3(eng, "v")
                        inc(eng.dma_start(out=x2s.rearrange("(tt p) d -> p tt d", p=128),
                                          in_=big_f.ap().rearrange("p (tt d) -> p tt d", d=1024)), sm_g2, "g2", 16)
                        eng.wait_ge(sm_g2, val("g2"))
                xln2_tok = sbuf("xln2_tok", [128, NTT * 1024], BF, S5)
                with nc.Block() as blk:
                    emit_ln(blk, big_f.ap(), xln2_tok.ap(), ("v",))
                with nc.Block() as blk:
                    @blk.sync
                    def _(eng):
                        w3(eng, "v")
                        xf = xln_fm.ap().rearrange("p (dc t) -> p dc t", t=T)
                        for tt in range(NTT):
                            inc(eng.dma_start(out=xf[:, :, 128 * tt:128 * tt + 128],
                                              in_=xln2_tok[:, 1024 * tt:1024 * tt + 1024],
                                              transpose=True), sm_d, "d", 16)
                        eng.wait_ge(sm_d, val("d"))

            with ExitStack() as S6:
                wt_fm = sbuf("wt_fm", [128, NDC * D], BF, S6)
                pooled = sbuf("pooled", [64, T], BF, S6)
                for oc in range(NDC):
                    with nc.Block() as blk:
                        @blk.sync
                        def _(eng):
                            inc(eng.dma_start(out=wstage[:, 0:1024],
                                              in_=ffn_w[128 * oc:128 * oc + 128, :]), sm_d, "d", 16)
                        @blk.vector
                        def _(eng):
                            w3(eng, "d")
                            inc(eng.tensor_copy(wbf[:, 0:1024], wstage[:, 0:1024]), sm_v, "v")
                        @blk.sync
                        def _(eng):
                            w3(eng, "v")
                            wt3 = wt_fm.ap().rearrange("p (dc oc c) -> p dc oc c", oc=8, c=128)
                            inc(eng.dma_start(out=wt3[:, :, oc, :], in_=wbf[:, 0:1024], transpose=True), sm_d, "d", 16)
                            eng.wait_ge(sm_d, val("d"))
                with nc.Block() as blk:
                    t0, v0 = val("t"), val("v")
                    @blk.tensor
                    def _(eng):
                        for st_ in range(NST):
                            if st_ >= 1:
                                eng.wait_ge(sm_v, v0 + st_)
                            for dc in range(NDC):
                                inc(eng.matmul(ps_d[0:64, 0:512], cpool[:, :],
                                               xln_fm[:, T * dc + 512 * st_:T * dc + 512 * st_ + 512],
                                               start=(dc == 0), stop=(dc == NDC - 1)),
                                    sm_t, "t", 1 if dc == NDC - 1 else 0)
                    @blk.vector
                    def _(eng):
                        for st_ in range(NST):
                            eng.wait_ge(sm_t, t0 + st_ + 1)
                            inc(eng.tensor_copy(pooled[:, 512 * st_:512 * st_ + 512], ps_d[0:64, 0:512]), sm_v, "v")
                with nc.Block() as blk:
                    t0, v0 = val("t"), val("v")
                    @blk.tensor
                    def _(eng):
                        w3(eng, "v")
                        j = 0
                        for tt in range(NTT):
                            for os_ in range(2):
                                bank = ps_b if j % 2 == 0 else ps_c
                                if j >= 2:
                                    eng.wait_ge(sm_v, v0 + j - 1)
                                for dc in range(NDC):
                                    eng.matmul(bank[:, :], xln_fm[:, T * dc + 128 * tt:T * dc + 128 * tt + 128],
                                               wt_fm[:, 1024 * dc + 512 * os_:1024 * dc + 512 * os_ + 512],
                                               start=(dc == 0), stop=False)
                                inc(eng.matmul(bank[:, :], pooled[:, 128 * tt:128 * tt + 128],
                                               disha_bf[:, 512 * os_:512 * os_ + 512],
                                               start=False, stop=True), sm_t, "t")
                                j += 1
                    @blk.vector
                    def _(eng):
                        j = 0
                        for tt in range(NTT):
                            for os_ in range(2):
                                bank = ps_b if j % 2 == 0 else ps_c
                                eng.wait_ge(sm_t, t0 + j + 1)
                                inc(eng.tensor_copy(big_f[:, 1024 * tt + 512 * os_:1024 * tt + 512 * os_ + 512],
                                                    bank[:, :]), sm_v, "v")
                                j += 1

            dump("ffn_raw", big_f.ap(), F32)

            with ExitStack() as S7:
                xstage = sbuf("xstage7", [128, 2 * 1024], F32, S7)
                with nc.Block() as blk:
                    emit_ln(blk, big_f.ap(), big_f.ap(), ("v",))
                with nc.Block() as blk:
                    d0, v0 = val("d"), val("v")
                    @blk.sync
                    def _(eng):
                        for tt in range(NTT):
                            if tt >= 2:
                                eng.wait_ge(sm_v, v0 + tt - 1)
                            inc(eng.dma_start(out=xstage[:, 1024 * (tt % 2):1024 * (tt % 2) + 1024],
                                              in_=x2s[128 * tt:128 * tt + 128, :]), sm_d, "d", 16)
                    @blk.vector
                    def _(eng):
                        for tt in range(NTT):
                            eng.wait_ge(sm_d, d0 + 16 * (tt + 1))
                            inc(eng.tensor_tensor(big_f[:, 1024 * tt:1024 * tt + 1024],
                                                  big_f[:, 1024 * tt:1024 * tt + 1024],
                                                  xstage[:, 1024 * (tt % 2):1024 * (tt % 2) + 1024], ALU.add), sm_v, "v")
                with nc.Block() as blk:
                    @blk.sync
                    def _(eng):
                        w3(eng, "v")
                        inc(eng.dma_start(out=y_h.rearrange("(tt p) d -> p tt d", p=128),
                                          in_=big_f.ap().rearrange("p (tt d) -> p tt d", d=1024)), sm_d, "d", 16)
                        eng.wait_ge(sm_d, val("d"))
    return nc


_NC_CACHE = [None]


def kernel(**inputs):
    consts = host_consts()
    base = {
        "qkv_w": inputs["L0_attn_qkv_w"], "qkv_b": inputs["L0_attn_qkv_b"],
        "w1_write": inputs["L0_attn_w1_write"], "w2_write": inputs["L0_attn_w2_write"],
        "w1_read": inputs["L0_attn_w1_read"], "w2_read": inputs["L0_attn_w2_read"],
        "geo_w": inputs["L0_attn_geo_w"], "geo_b": inputs["L0_attn_geo_b"],
        "gate_w": inputs["L0_attn_gate_w"], "gate_b": inputs["L0_attn_gate_b"],
        "inc_scale": inputs["L0_attn_inc_scale"],
        "out_w": inputs["L0_attn_out_w"], "out_b": inputs["L0_attn_out_b"],
        "ffn_weight": inputs["L0_ffn_weight"], "disha": inputs["L0_ffn_disha"],
    }
    base = {k: np.ascontiguousarray(np.asarray(v, dtype=np.float32)) for k, v in base.items()}
    for k, v in consts.items():
        base["c_" + k] = np.ascontiguousarray(v)
    x_full = np.asarray(inputs["L0_attn_x"], dtype=np.float32)
    in_maps = []
    for b in range(8):
        m = dict(base)
        m["x"] = np.ascontiguousarray(x_full[b])
        in_maps.append(m)
    if _NC_CACHE[0] is None:
        _NC_CACHE[0] = build_nc()
    res = run_bass_kernel_spmd(_NC_CACHE[0], in_maps, core_ids=list(range(8)))
    return np.stack([np.asarray(r["y"], dtype=np.float32) for r in res.results], axis=0)


# version 07
# speedup vs baseline: 1.1799x; 1.1799x over previous
import numpy as np
import ml_dtypes
from contextlib import ExitStack

import concourse.bass as bass
import concourse.mybir as mybir
from concourse.bass_utils import run_bass_kernel_spmd

BF16 = ml_dtypes.bfloat16
F32 = mybir.dt.float32
BF = mybir.dt.bfloat16
AF = mybir.ActivationFunctionType
ALU = mybir.AluOpType

T = 2048
D = 1024
H = 16
NTT = 16
NDC = 8
NST = 4
PAIRS4 = [(0, 1), (0, 2), (0, 3), (1, 2), (1, 3), (2, 3)]
SIGNS = [1.0, -1.0, 1.0, 1.0, -1.0, 1.0]
PAIRS_READ = [PAIRS4[5 - p] for p in range(6)]

TILES = []
for _s in range(NST):
    for _kc in range(4 * _s + 4):
        _diag = _kc >= 4 * _s
        TILES.append((_s, _kc, 128 * (_kc - 4 * _s) if _diag else 0, _diag))
N_TILES = len(TILES)


def host_consts():
    c = {}
    E = np.zeros((4, 128, 96), dtype=np.float32)
    for h in range(H):
        for p in range(6):
            col = 6 * h + p
            E[0, 4 * h + PAIRS4[p][0], col] = 1.0
            E[1, 4 * h + PAIRS4[p][1], col] = 1.0
            E[2, 4 * h + PAIRS_READ[p][0], col] = 1.0
            E[3, 4 * h + PAIRS_READ[p][1], col] = 1.0
    for k in range(4):
        E[k, 64:128, :] = E[k, 0:64, :]
    c['E'] = np.concatenate([E[0], E[1], E[2], E[3]], axis=1)
    coll = np.zeros((96, 16), dtype=np.float32)
    for h in range(H):
        coll[6 * h:6 * h + 6, h] = 1.0
    c['collect'] = coll
    spw = np.zeros((48, 96), dtype=np.float32)
    spr = np.zeros((48, 96), dtype=np.float32)
    for h in range(H):
        for p in range(6):
            spw[h, 6 * h + p] = 1.0
            spr[h, 6 * h + p] = SIGNS[p]
    spw[32:48] = spw[0:16]
    spr[32:48] = spr[0:16]
    c['spread_w'] = spw
    c['spread_r'] = spr
    c['tri'] = np.triu(np.ones((128, 128), dtype=np.float32))
    pool = np.zeros((128, 64), dtype=np.float32)
    for r in range(64):
        pool[r, r] = 1.0
        pool[64 + r, r] = 1.0
    c['pool'] = pool
    c['ident64'] = np.eye(64, dtype=np.float32)
    return c


def build_nc(debug=False, np_pairs=8, n_secs=999, skip_final=False, final_lvl=4):
    nc = bass.Bass("TRN2", target_bir_lowering=False)
    dbg_list = []
    dI = {}

    def din(name, shape):
        dI[name] = nc.dram_tensor(name, list(shape), F32, kind="ExternalInput").ap()
        return dI[name]

    x_h = din("x", (T, D))
    qkv_w = din("qkv_w", (D, 3 * D))
    qkv_b = din("qkv_b", (3 * D,))
    for nm in ["w1_write", "w2_write", "w1_read", "w2_read"]:
        din(nm, (D, 64))
    geo_w = din("geo_w", (D, D))
    geo_b = din("geo_b", (D,))
    gate_w = din("gate_w", (D, 16))
    gate_b = din("gate_b", (16,))
    inc_scale = din("inc_scale", (16,))
    out_w = din("out_w", (D, D))
    out_b = din("out_b", (D,))
    ffn_w = din("ffn_weight", (D, D))
    disha = din("disha", (64, D))
    for k, v in host_consts().items():
        din("c_" + k, v.shape)
    y_h = nc.dram_tensor("y", [T, D], F32, kind="ExternalOutput").ap()
    x2s = nc.dram_tensor("x2s", [T, D], F32).ap()
    rs_hbm = nc.dram_tensor("rs_hbm", [2 * T], BF).ap()

    ctr = {}

    def inc(instr, sem, name, k=1):
        if k > 0:
            instr.then_inc(sem, k)
            ctr[name] = ctr.get(name, 0) + k
        return ctr.get(name, 0)

    def val(name):
        return ctr.get(name, 0)

    with ExitStack() as ST:
        def sbuf(name, shape, dt, st=None, side=None):
            kw = {} if side is None else {"side": side}
            return (st or ST).enter_context(nc.sbuf_tensor(name, shape, dt, **kw))

        xln_fm = sbuf("xln_fm", [128, NDC * T], BF)
        w_small = sbuf("w_small", [128, NDC * 320], BF)
        qkb_sb = sbuf("qkb_sb", [128, 24], F32)
        gb_sb = sbuf("gb_sb", [128, 8], F32)
        gateb = sbuf("gateb", [16, 1], F32)
        incs = sbuf("incs", [48, 1], F32)
        cE = sbuf("cE", [128, 384], BF)
        ccoll = sbuf("ccoll", [96, 16], BF)
        cspw = sbuf("cspw", [48, 96], BF)
        cspr = sbuf("cspr", [48, 96], BF)
        ctri = sbuf("ctri", [128, 128], BF)
        cpool = sbuf("cpool", [128, 64], BF)
        cident = sbuf("cident", [64, 64], BF)
        cones = sbuf("cones", [128, 128], BF)
        disha_bf = sbuf("disha_bf", [64, D], BF)
        wstage = sbuf("wstage", [128, NDC * 256], F32)
        stats = sbuf("stats", [128, NTT * 16], F32)

        S_pre = ExitStack()
        rows_f = sbuf("rows_f", [128, 1024], F32, S_pre)
        rows_b = sbuf("rows_b", [128, 2 * T], BF, S_pre)
        omg_bc = sbuf("omg_bc", [64, T], BF, S_pre)
        g_bc = sbuf("g_bc", [64, T], BF, S_pre)
        wl_buf = sbuf("wl_buf", [96, T], BF, S_pre)
        rl_buf = sbuf("rl_buf", [96, T], BF, S_pre)
        S_comb = ExitStack()

        ps_a = ST.enter_context(nc.psum_tensor("ps_a", [128, 2048], F32))
        ps_b = ST.enter_context(nc.psum_tensor("ps_b", [128, 512], F32))
        ps_c = ST.enter_context(nc.psum_tensor("ps_c", [128, 512], F32))
        ps_d = ST.enter_context(nc.psum_tensor("ps_d", [128, 1024], F32))

        sm_d = ST.enter_context(nc.semaphore("sm_d"))
        sm_g = ST.enter_context(nc.semaphore("sm_g"))
        sm_t = ST.enter_context(nc.semaphore("sm_t"))
        sm_s = ST.enter_context(nc.semaphore("sm_s"))
        sm_v = ST.enter_context(nc.semaphore("sm_v"))
        sm_f = ST.enter_context(nc.semaphore("sm_f"))
        sm_g2 = ST.enter_context(nc.semaphore("sm_g2"))
        SEM = {"d": sm_d, "g": sm_g, "t": sm_t, "s": sm_s, "v": sm_v}

        def w3(eng, *names):
            for name in names:
                eng.wait_ge(SEM[name], val(name))

        def dump(name, src_ap, dt):
            if not debug:
                return
            p, w = src_ap.shape[0], src_ap.shape[1]
            od = nc.dram_tensor("dbg_" + name, [p, w], dt, kind="ExternalOutput").ap()
            with nc.Block() as blk:
                @blk.sync
                def _(eng):
                    w3(eng, "v", "s", "t", "g")
                    inc(eng.dma_start(out=od, in_=src_ap), sm_d, "d", 16)
                    eng.wait_ge(sm_d, val("d"))

        with ExitStack() as S0:
            cst = sbuf("cst", [128, 1024], F32, S0)
            with nc.Block() as blk:
                @blk.sync
                def _(eng):
                    ws3 = wstage.ap().rearrange("p (dc c) -> p dc c", c=256)
                    for i, nm in enumerate(["w1_write", "w2_write", "w1_read", "w2_read"]):
                        inc(eng.dma_start(out=ws3[:, :, 64 * i:64 * i + 64],
                                          in_=dI[nm].rearrange("(dc p) c -> p dc c", p=128)), sm_d, "d", 16)
                    inc(eng.dma_start(out=qkb_sb[:, :], in_=qkv_b.rearrange("(c p) -> p c", p=128), allow_slow_non_contiguous=True), sm_d, "d", 16)
                    inc(eng.dma_start(out=gb_sb[:, :], in_=geo_b.rearrange("(c p) -> p c", p=128), allow_slow_non_contiguous=True), sm_d, "d", 16)
                    inc(eng.dma_start(out=gateb[:, :], in_=gate_b.rearrange("(a b) -> a b", b=1)), sm_d, "d", 16)
                    inc(eng.dma_start(out=incs[0:16, :], in_=inc_scale.rearrange("(a b) -> a b", b=1)), sm_d, "d", 16)
                    inc(eng.dma_start(out=incs[32:48, :], in_=inc_scale.rearrange("(a b) -> a b", b=1)), sm_d, "d", 16)
                @blk.vector
                def _(eng):
                    w3(eng, "d")
                    for dc in range(NDC):
                        inc(eng.tensor_copy(w_small[:, 320 * dc:320 * dc + 256],
                                            wstage[:, 256 * dc:256 * dc + 256]), sm_v, "v")
                @blk.gpsimd
                def _(eng):
                    eng.memset(cones[:, :], 1.0)
                    inc(eng.engine_nop(), sm_g, "g")
            with nc.Block() as blk:
                @blk.sync
                def _(eng):
                    ws3 = wstage.ap().rearrange("p (dc c) -> p dc c", c=256)
                    inc(eng.dma_start(out=ws3[:, :, 0:16],
                                      in_=gate_w.rearrange("(dc p) c -> p dc c", p=128)), sm_d, "d", 16)
                    inc(eng.dma_start(out=cst[0:128, 0:384], in_=dI["c_E"]), sm_d, "d", 16)
                    inc(eng.dma_start(out=cst[0:96, 384:400], in_=dI["c_collect"]), sm_d, "d", 16)
                    inc(eng.dma_start(out=cst[0:128, 400:528], in_=dI["c_tri"]), sm_d, "d", 16)
                    inc(eng.dma_start(out=cst[0:128, 528:592], in_=dI["c_pool"]), sm_d, "d", 16)
                    inc(eng.dma_start(out=cst[0:64, 592:656], in_=dI["c_ident64"]), sm_d, "d", 16)
                    inc(eng.dma_start(out=cst[0:48, 656:752], in_=dI["c_spread_w"]), sm_d, "d", 16)
                    inc(eng.dma_start(out=cst[0:48, 752:848], in_=dI["c_spread_r"]), sm_d, "d", 16)
                @blk.vector
                def _(eng):
                    w3(eng, "d")
                    for dc in range(NDC):
                        inc(eng.tensor_copy(w_small[:, 320 * dc + 256:320 * dc + 272],
                                            wstage[:, 256 * dc:256 * dc + 16]), sm_v, "v")
                    inc(eng.tensor_copy(cE[:, :], cst[0:128, 0:384]), sm_v, "v")
                    inc(eng.tensor_copy(ccoll[:, :], cst[0:96, 384:400]), sm_v, "v")
                    inc(eng.tensor_copy(ctri[:, :], cst[0:128, 400:528]), sm_v, "v")
                    inc(eng.tensor_copy(cpool[:, :], cst[0:128, 528:592]), sm_v, "v")
                    inc(eng.tensor_copy(cident[:, :], cst[0:64, 592:656]), sm_v, "v")
                    inc(eng.tensor_copy(cspw[:, :], cst[0:48, 656:752]), sm_v, "v")
                    inc(eng.tensor_scalar(cspr[0:16, :], cst[0:16, 752:848], incs[0:16, :], None, ALU.mult, ALU.bypass), sm_v, "v")
                    inc(eng.tensor_scalar(cspr[32:48, :], cst[32:48, 752:848], incs[32:48, :], None, ALU.mult, ALU.bypass), sm_v, "v")
            with nc.Block() as blk:
                @blk.sync
                def _(eng):
                    inc(eng.dma_start(out=cst[0:64, 0:D], in_=disha), sm_d, "d", 16)
                @blk.vector
                def _(eng):
                    w3(eng, "d")
                    inc(eng.tensor_copy(disha_bf[:, :], cst[0:64, 0:D]), sm_v, "v")

        def emit_ln(blk, src, dst, waits):
            @blk.vector
            def _(eng):
                w3(eng, *waits)
                for tt in range(NTT):
                    c0 = 16 * tt
                    eng.bn_stats(stats[:, c0:c0 + 6], src[:, 1024 * tt:1024 * tt + 512])
                    eng.bn_stats(stats[:, c0 + 6:c0 + 12], src[:, 1024 * tt + 512:1024 * tt + 1024])
                for tt in range(NTT):
                    c0 = 16 * tt
                    eng.bn_aggr(stats[:, c0 + 12:c0 + 14], stats[:, c0:c0 + 12])
                for tt in range(NTT):
                    c0 = 16 * tt
                    inc(eng.tensor_scalar(stats[:, c0 + 13:c0 + 14], stats[:, c0 + 13:c0 + 14],
                                          1e-5, None, ALU.add, ALU.bypass), sm_v, "v")
            @blk.scalar
            def _(eng):
                w3(eng, "v")
                for tt in range(NTT):
                    c0 = 16 * tt
                    eng.activation(stats[:, c0 + 13:c0 + 14], stats[:, c0 + 13:c0 + 14], AF.Ln)
                    inc(eng.activation(stats[:, c0 + 14:c0 + 15], stats[:, c0 + 13:c0 + 14],
                                       AF.Exp, scale=-0.5), sm_s, "s")
            @blk.vector
            def _(eng):
                w3(eng, "s")
                for tt in range(NTT):
                    c0 = 16 * tt
                    inc(eng.tensor_scalar(dst[:, 1024 * tt:1024 * tt + 1024],
                                          src[:, 1024 * tt:1024 * tt + 1024],
                                          stats[:, c0 + 12:c0 + 13], stats[:, c0 + 14:c0 + 15],
                                          ALU.subtract, ALU.mult), sm_v, "v")

        def load_cast(blk, jobs):
            @blk.sync
            def _(eng):
                off = 0
                for src, c0, w in jobs:
                    ws3 = wstage.ap().rearrange("p (dc c) -> p dc c", c=256)
                    inc(eng.dma_start(out=ws3[:, :, off:off + w], in_=src), sm_d, "d", 16)
                    off += w
            @blk.vector
            def _(eng):
                w3(eng, "d")
                off = 0
                for src, c0, w in jobs:
                    for dc in range(NDC):
                        inc(eng.tensor_copy(wbf[:, 256 * dc + c0:256 * dc + c0 + w],
                                            wstage[:, 256 * dc + off:256 * dc + off + w]), sm_v, "v")
                    off += w

        with ExitStack() as S1:
            x_res = sbuf("x_res", [128, NTT * 1024], F32, S1)
            xln_tok = sbuf("xln_tok", [128, NTT * 1024], BF, S1)
            with nc.Block() as blk:
                @blk.sync
                def _(eng):
                    inc(eng.dma_start(out=x_res.ap().rearrange("p (tt d) -> p tt d", d=1024),
                                      in_=x_h.rearrange("(tt p) d -> p tt d", p=128)), sm_d, "d", 16)
            with nc.Block() as blk:
                emit_ln(blk, x_res.ap(), xln_tok.ap(), ("d",))
            with nc.Block() as blk:
                @blk.sync
                def _(eng):
                    w3(eng, "v")
                    xf = xln_fm.ap().rearrange("p (dc t) -> p dc t", t=T)
                    for tt in range(NTT):
                        inc(eng.dma_start(out=xf[:, :, 128 * tt:128 * tt + 128],
                                          in_=xln_tok[:, 1024 * tt:1024 * tt + 1024],
                                          transpose=True), sm_d, "d", 16)
                    eng.wait_ge(sm_d, val("d"))
        dump("xln_fm", xln_fm.ap(), BF)

        with ExitStack() as S2:
            pw_w = sbuf("pw_w", [128, T], BF, S2)
            pw_r = sbuf("pw_r", [128, T], BF, S2)
            gate_sig = sbuf("gate_sig", [16, T], BF, S2)
            gw1xi = sbuf("gw1xi", [96, T + 4], BF, S2)
            gw1xj = sbuf("gw1xj", [96, T + 4], BF, S2)
            r1xi = sbuf("r1xi", [96, 512], BF, S2)
            r1xj = sbuf("r1xj", [96, 512], BF, S2)
            m1s = sbuf("m1s", [96, 512], F32, S2)
            m2s = sbuf("m2s", [96, 512], F32, S2)
            Lwsq = sbuf("Lwsq", [96, 512], BF, S2)
            Lrsq = sbuf("Lrsq", [96, 512], BF, S2)
            Lw = sbuf("Lw", [96, 512], BF, S2)
            Lr = sbuf("Lr", [96, 512], BF, S2)
            inv_sb = sbuf("inv_sb", [48, T], BF, S2)

            with nc.Block() as blk:
                t0, s0 = val("t"), val("s")
                @blk.tensor
                def _(eng):
                    for st_ in range(NST):
                        bank = ps_b if st_ % 2 == 0 else ps_c
                        if st_ >= 2:
                            eng.wait_ge(sm_s, s0 + st_ - 1)
                        for dc in range(NDC):
                            inc(eng.matmul(bank[0:16, :], w_small[:, 320 * dc + 256:320 * dc + 272],
                                           xln_fm[:, T * dc + 512 * st_:T * dc + 512 * st_ + 512],
                                           start=(dc == 0), stop=(dc == NDC - 1)),
                                sm_t, "t", 1 if dc == NDC - 1 else 0)
                @blk.scalar
                def _(eng):
                    for st_ in range(NST):
                        eng.wait_ge(sm_t, t0 + st_ + 1)
                        inc(eng.activation(gate_sig[:, 512 * st_:512 * st_ + 512],
                                           (ps_b if st_ % 2 == 0 else ps_c)[0:16, :],
                                           AF.Sigmoid, bias=gateb[0:16, :]), sm_s, "s")
            with nc.Block() as blk:
                @blk.tensor
                def _(eng):
                    w3(eng, "s")
                    for st_ in range(NST):
                        inc(eng.matmul(ps_a[0:1, 512 * st_:512 * st_ + 512], cones[0:16, 0:1],
                                       gate_sig[:, 512 * st_:512 * st_ + 512], start=True, stop=True), sm_t, "t")
                @blk.scalar
                def _(eng):
                    w3(eng, "t")
                    inc(eng.activation(rows_f[0:1, 0:1024], ps_a[0:1, 0:1024], AF.Copy, scale=1.0 / 16.0), sm_s, "s")
                @blk.vector
                def _(eng):
                    w3(eng, "s")
                    inc(eng.tensor_copy(rows_b[0:1, 0:1024], rows_f[0:1, 0:1024]), sm_v, "v")
                    inc(eng.tensor_tensor(rows_b[0:1, T:T + 1024], rows_f[0:1, 0:1024], rows_b[0:1, 0:1024], ALU.subtract), sm_v, "v")
                @blk.scalar
                def _(eng):
                    w3(eng, "v")
                    inc(eng.activation(rows_f[0:1, 0:1024], ps_a[0:1, 1024:2048], AF.Copy, scale=1.0 / 16.0), sm_s, "s")
                @blk.vector
                def _(eng):
                    w3(eng, "s")
                    inc(eng.tensor_copy(rows_b[0:1, 1024:2048], rows_f[0:1, 0:1024]), sm_v, "v")
                    inc(eng.tensor_tensor(rows_b[0:1, T + 1024:2 * T], rows_f[0:1, 0:1024], rows_b[0:1, 1024:2048], ALU.subtract), sm_v, "v")
            for half in range(2):
                c0 = 1024 * half
                with nc.Block() as blk:
                    @blk.tensor
                    def _(eng):
                        w3(eng, "v")
                        for q in range(2):
                            eng.matmul(ps_d[0:64, 512 * q:512 * q + 512], cones[0:1, 0:64],
                                       rows_b[0:1, c0 + 512 * q:c0 + 512 * q + 512], start=True, stop=False)
                            inc(eng.matmul(ps_d[0:64, 512 * q:512 * q + 512], cones[0:1, 0:64],
                                           rows_b[0:1, T + c0 + 512 * q:T + c0 + 512 * q + 512], start=False, stop=True), sm_t, "t")
                    @blk.vector
                    def _(eng):
                        w3(eng, "t")
                        inc(eng.tensor_copy(g_bc[:, c0:c0 + 1024], ps_d[0:64, :]), sm_v, "v")
                        inc(eng.tensor_scalar(omg_bc[:, c0:c0 + 1024], ps_d[0:64, :], 1.0, -1.0, ALU.subtract, ALU.mult), sm_v, "v")

            dump("g_bc", g_bc.ap(), F32)
            dump("omg_bc", omg_bc.ap(), F32)
            with nc.Block() as blk:
                t0, v0 = val("t"), val("v")
                @blk.tensor
                def _(eng):
                    j = 0
                    for which in (0, 128):
                        for st_ in range(NST):
                            bank = ps_b if j % 2 == 0 else ps_c
                            if j >= 2:
                                eng.wait_ge(sm_v, v0 + j - 1)
                            for dc in range(NDC):
                                inc(eng.matmul(bank[:, :], w_small[:, 320 * dc + which:320 * dc + which + 128],
                                               xln_fm[:, T * dc + 512 * st_:T * dc + 512 * st_ + 512],
                                               start=(dc == 0), stop=(dc == NDC - 1)),
                                    sm_t, "t", 1 if dc == NDC - 1 else 0)
                            j += 1
                @blk.vector
                def _(eng):
                    j = 0
                    for which, dst in [(0, pw_w), (128, pw_r)]:
                        for st_ in range(NST):
                            bank = ps_b if j % 2 == 0 else ps_c
                            eng.wait_ge(sm_t, t0 + j + 1)
                            inc(eng.tensor_copy(dst[:, 512 * st_:512 * st_ + 512], bank[:, :]), sm_v, "v")
                            j += 1

            with nc.Block() as blk:
                @blk.gpsimd
                def _(eng):
                    eng.memset(gw1xi[:, 0:1], 0.0)
                    eng.memset(gw1xj[:, 0:1], 0.0)
                    inc(eng.engine_nop(), sm_g, "g")

            for st_ in range(NST):
                q0 = 512 * st_
                with nc.Block() as blk:
                    @blk.tensor
                    def _(eng):
                        w3(eng, "v", "g")
                        inc(eng.matmul(ps_a[0:96, 0:512], cE[0:64, 0:96], pw_w[0:64, q0:q0 + 512], start=True, stop=True), sm_t, "t")
                        inc(eng.matmul(ps_a[0:96, 512:1024], cE[0:64, 96:192], pw_w[0:64, q0:q0 + 512], start=True, stop=True), sm_t, "t")
                        inc(eng.matmul(ps_a[0:96, 1024:1536], cE[64:128, 0:96], pw_w[64:128, q0:q0 + 512], start=True, stop=True), sm_t, "t")
                        inc(eng.matmul(ps_a[0:96, 1536:2048], cE[64:128, 96:192], pw_w[64:128, q0:q0 + 512], start=True, stop=True), sm_t, "t")
                        inc(eng.matmul(ps_d[0:96, 0:512], cE[0:64, 192:288], pw_r[0:64, q0:q0 + 512], start=True, stop=True), sm_t, "t")
                        inc(eng.matmul(ps_d[0:96, 512:1024], cE[0:64, 288:384], pw_r[0:64, q0:q0 + 512], start=True, stop=True), sm_t, "t")
                        inc(eng.matmul(ps_b[0:96, :], cE[64:128, 192:288], pw_r[64:128, q0:q0 + 512], start=True, stop=True), sm_t, "t")
                        inc(eng.matmul(ps_c[0:96, :], cE[64:128, 288:384], pw_r[64:128, q0:q0 + 512], start=True, stop=True), sm_t, "t")
                    @blk.vector
                    def _(eng):
                        w3(eng, "t")
                        inc(eng.tensor_copy(gw1xi[:, q0 + 1:q0 + 513], ps_a[0:96, 0:512]), sm_v, "v")
                        inc(eng.tensor_copy(gw1xj[:, q0 + 1:q0 + 513], ps_a[0:96, 512:1024]), sm_v, "v")
                        inc(eng.tensor_copy(r1xi[:, :], ps_d[0:96, 0:512]), sm_v, "v")
                        inc(eng.tensor_copy(r1xj[:, :], ps_d[0:96, 512:1024]), sm_v, "v")
                        inc(eng.tensor_tensor(m1s[:, :], gw1xi[:, q0:q0 + 512], ps_a[0:96, 1536:2048], ALU.mult), sm_v, "v")
                        inc(eng.tensor_tensor(m2s[:, :], gw1xj[:, q0:q0 + 512], ps_a[0:96, 1024:1536], ALU.mult), sm_v, "v")
                        inc(eng.tensor_tensor(Lw[:, :], m1s[:, :], m2s[:, :], ALU.subtract), sm_v, "v")
                        inc(eng.tensor_tensor(m1s[:, :], r1xi[:, :], ps_c[0:96, :], ALU.mult), sm_v, "v")
                        inc(eng.tensor_tensor(m2s[:, :], r1xj[:, :], ps_b[0:96, :], ALU.mult), sm_v, "v")
                        inc(eng.tensor_tensor(Lr[:, :], m1s[:, :], m2s[:, :], ALU.subtract), sm_v, "v")
                        inc(eng.tensor_tensor(Lwsq[:, :], Lw[:, :], Lw[:, :], ALU.mult), sm_v, "v")
                        inc(eng.tensor_tensor(Lrsq[:, :], Lr[:, :], Lr[:, :], ALU.mult), sm_v, "v")
                    @blk.tensor
                    def _(eng):
                        w3(eng, "v")
                        inc(eng.matmul(ps_b[0:16, :], ccoll[:, :], Lwsq[:, :], start=True, stop=True), sm_t, "t")
                        inc(eng.matmul(ps_b[32:48, :], ccoll[:, :], Lrsq[:, :], start=True, stop=True), sm_t, "t")
                    @blk.vector
                    def _(eng):
                        w3(eng, "t")
                        inc(eng.tensor_scalar(ps_b[0:16, :], ps_b[0:16, :], 1e-19, None, ALU.max, ALU.bypass), sm_v, "v")
                        inc(eng.tensor_scalar(ps_b[32:48, :], ps_b[32:48, :], 1e-19, None, ALU.max, ALU.bypass), sm_v, "v")
                    @blk.scalar
                    def _(eng):
                        w3(eng, "v")
                        eng.activation(ps_b[0:16, :], ps_b[0:16, :], AF.Ln)
                        eng.activation(ps_b[32:48, :], ps_b[32:48, :], AF.Ln)
                        inc(eng.activation(inv_sb[0:16, q0:q0 + 512], ps_b[0:16, :], AF.Exp, scale=-0.5), sm_s, "s")
                        inc(eng.activation(inv_sb[32:48, q0:q0 + 512], ps_b[32:48, :], AF.Exp, scale=-0.5), sm_s, "s")
                    @blk.tensor
                    def _(eng):
                        w3(eng, "s")
                        inc(eng.matmul(ps_a[0:96, 0:512], cspw[0:16, :], inv_sb[0:16, q0:q0 + 512], start=True, stop=True), sm_t, "t")
                        inc(eng.matmul(ps_a[0:96, 512:1024], cspr[32:48, :], inv_sb[32:48, q0:q0 + 512], start=True, stop=True), sm_t, "t")
                    @blk.vector
                    def _(eng):
                        w3(eng, "t")
                        inc(eng.tensor_tensor(wl_buf[:, q0:q0 + 512], Lw[:, :], ps_a[0:96, 0:512], ALU.mult), sm_v, "v")
                        inc(eng.tensor_tensor(rl_buf[:, q0:q0 + 512], Lr[:, :], ps_a[0:96, 512:1024], ALU.mult), sm_v, "v")
            dump("pw_w", pw_w.ap(), BF)
            dump("pw_r", pw_r.ap(), BF)
            dump("inv_dbg", inv_sb.ap(), BF)
        dump("wl_buf", wl_buf.ap(), BF)
        dump("rl_buf", rl_buf.ap(), BF)

        comb = sbuf("comb", [128, 8 * T], BF, S_comb, side="right")
        with ExitStack() as S3:
            qt_p = sbuf("qt_p", [128, T], BF, S3)
            kt_p = sbuf("kt_p", [128, T], BF, S3)
            vfm = sbuf("vfm", [128, T], BF, S3)
            vraw = sbuf("vraw", [128, T], BF, S3)
            v_tok = sbuf("v_tok", [128, NTT * 130], BF, S3)
            gv_tok = sbuf("gv_tok", [128, NTT * 130], BF, S3)
            NSL = 8
            pt = sbuf("pt", [128, NSL * 512], BF, S3)
            wl_scr = sbuf("wl_scr", [128, 2 * T], BF, S3)
            rl_scr = sbuf("rl_scr", [128, 2 * T], BF, S3)
            u_sb = [sbuf("u_sb%d" % i, [65, T], BF, S3) for i in range(4)]
            rsb = sbuf("rsb", [64, 2 * T], BF, S3)
            tmpb = sbuf("tmpb", [64, T], BF, S3)
            cmbB = sbuf("cmbB", [64, T], BF, S3)
            wbf2 = sbuf("wbf2", [128, 2 * NDC * 512], BF, S3)

            with nc.Block() as blk:
                @blk.gpsimd
                def _(eng):
                    vt = v_tok.ap().rearrange("p (tt s c) -> p tt s c", s=2, c=65)
                    gt = gv_tok.ap().rearrange("p (tt s c) -> p tt s c", s=2, c=65)
                    eng.memset(vt[:, :, :, 64:65], 1.0)
                    eng.memset(gt[:, :, :, 64:65], 1.0)
                    eng.memset(wl_scr[:, :], 0.0)
                    eng.memset(rl_scr[:, :], 0.0)
                    inc(eng.nop(), sm_g, "g")

            MK = {}
            NP = np_pairs
            GTI = [0]
            mask_after = {-1: 0}
            EV = {}

            def wslot(p):
                return 4096 * (p % 2)

            def emit_loads(blk, p):
                @blk.sync
                def _(eng):
                    if p > 0:
                        eng.wait_ge(sm_v, MK[("cast_vg", p - 1)])
                    ws3 = wstage.ap().rearrange("p (dc c) -> p dc c", c=256)
                    inc(eng.dma_start(out=ws3[:, :, 0:128],
                                      in_=qkv_w[:, 128 * p:128 * p + 128].rearrange("(dc p) c -> p dc c", p=128)), sm_d, "d", 16)
                    inc(eng.dma_start(out=ws3[:, :, 128:256],
                                      in_=qkv_w[:, D + 128 * p:D + 128 * p + 128].rearrange("(dc p) c -> p dc c", p=128)), sm_d, "d", 16)
                    MK[("d_qk", p)] = val("d")

            def emit_loads_vg(blk, p):
                @blk.sync
                def _(eng):
                    eng.wait_ge(sm_v, MK[("cast_qk", p)])
                    ws3 = wstage.ap().rearrange("p (dc c) -> p dc c", c=256)
                    inc(eng.dma_start(out=ws3[:, :, 0:128],
                                      in_=qkv_w[:, 2 * D + 128 * p:2 * D + 128 * p + 128].rearrange("(dc p) c -> p dc c", p=128)), sm_d, "d", 16)
                    inc(eng.dma_start(out=ws3[:, :, 128:256],
                                      in_=geo_w[:, 128 * p:128 * p + 128].rearrange("(dc p) c -> p dc c", p=128)), sm_d, "d", 16)
                    MK[("d_vg", p)] = val("d")

            def emit_cast_qk(blk, p):
                @blk.vector
                def _(eng):
                    eng.wait_ge(sm_d, MK[("d_qk", p)])
                    if p >= 2:
                        eng.wait_ge(sm_t, MK[("t_proj", p - 2)])
                    sl = wslot(p)
                    for dc in range(NDC):
                        inc(eng.tensor_copy(wbf2[:, sl + 512 * dc:sl + 512 * dc + 256],
                                            wstage[:, 256 * dc:256 * dc + 256]), sm_v, "v")
                    MK[("cast_qk", p)] = val("v")

            def emit_cast_vg(blk, p):
                @blk.vector
                def _(eng):
                    eng.wait_ge(sm_d, MK[("d_vg", p)])
                    sl = wslot(p)
                    for dc in range(NDC):
                        inc(eng.tensor_copy(wbf2[:, sl + 512 * dc + 256:sl + 512 * dc + 512],
                                            wstage[:, 256 * dc:256 * dc + 256]), sm_v, "v")
                    MK[("cast_vg", p)] = val("v")

            def emit_extract(blk, p):
                @blk.sync
                def _(eng):
                    if p >= 2:
                        eng.wait_ge(sm_t, MK[("t_geo_done", p - 2)])
                    eng.wait_ge(sm_d, val("d"))
                    c0 = T * (p % 2)
                    hA, hB = 2 * p, 2 * p + 1
                    inc(eng.dma_start(out=wl_scr[0:6, c0:c0 + T], in_=wl_buf[6 * hA:6 * hA + 6, :]), sm_d, "d", 16)
                    inc(eng.dma_start(out=wl_scr[64:70, c0:c0 + T], in_=wl_buf[6 * hB:6 * hB + 6, :]), sm_d, "d", 16)
                    inc(eng.dma_start(out=rl_scr[0:6, c0:c0 + T], in_=rl_buf[6 * hA:6 * hA + 6, :]), sm_d, "d", 16)
                    inc(eng.dma_start(out=rl_scr[64:70, c0:c0 + T], in_=rl_buf[6 * hB:6 * hB + 6, :]), sm_d, "d", 16)
                    MK[("d_ext", p)] = val("d")

            def emit_proj(blk, p):
                sl = wslot(p)
                t0 = val("t")
                v0 = val("v")
                @blk.tensor
                def _(eng):
                    eng.wait_ge(sm_v, MK[("cast_qk", p)])
                    j = 0
                    for which in (0, 128):
                        for st_ in range(NST):
                            bank = ps_b if j % 2 == 0 else ps_c
                            if j >= 2:
                                eng.wait_ge(sm_v, v0 + j - 1)
                            for dc in range(NDC):
                                inc(eng.matmul(bank[:, :], wbf2[:, sl + 512 * dc + which:sl + 512 * dc + which + 128],
                                               xln_fm[:, T * dc + 512 * st_:T * dc + 512 * st_ + 512],
                                               start=(dc == 0), stop=(dc == NDC - 1)),
                                    sm_t, "t", 1 if dc == NDC - 1 else 0)
                            j += 1
                @blk.vector
                def _(eng):
                    j = 0
                    for dst, bcol in [(qt_p, p), (kt_p, 8 + p)]:
                        for st_ in range(NST):
                            bank = ps_b if j % 2 == 0 else ps_c
                            eng.wait_ge(sm_t, t0 + j + 1)
                            inc(eng.tensor_scalar(dst[:, 512 * st_:512 * st_ + 512], bank[:, :],
                                                  qkb_sb[:, bcol:bcol + 1], None, ALU.add, ALU.bypass), sm_v, "v")
                            j += 1
                for vi in range(2):
                    t1 = val("t")
                    v1 = val("v")
                    which = 256 if vi == 0 else 384
                    @blk.tensor
                    def _(eng):
                        eng.wait_ge(sm_v, MK[("cast_vg", p)])
                        for st_ in range(NST):
                            bank = ps_b if st_ % 2 == 0 else ps_c
                            if st_ >= 2:
                                eng.wait_ge(sm_v, v1 + st_ - 1)
                            for dc in range(NDC):
                                inc(eng.matmul(bank[:, :], wbf2[:, sl + 512 * dc + which:sl + 512 * dc + which + 128],
                                               xln_fm[:, T * dc + 512 * st_:T * dc + 512 * st_ + 512],
                                               start=(dc == 0), stop=(dc == NDC - 1)),
                                    sm_t, "t", 1 if dc == NDC - 1 else 0)
                    MK[("t_proj", p)] = val("t")
                    @blk.vector
                    def _(eng):
                        if vi == 1:
                            eng.wait_ge(sm_d, MK[("d_trV", p)])
                        for st_ in range(NST):
                            bank = ps_b if st_ % 2 == 0 else ps_c
                            eng.wait_ge(sm_t, t1 + st_ + 1)
                            bias = qkb_sb[:, 16 + p:17 + p] if vi == 0 else gb_sb[:, p:p + 1]
                            inc(eng.tensor_scalar(vfm[:, 512 * st_:512 * st_ + 512], bank[:, :],
                                                  bias, None, ALU.add, ALU.bypass), sm_v, "v")
                    MK[("v_vfm", p, vi)] = val("v")
                    @blk.sync
                    def _(eng):
                        eng.wait_ge(sm_v, MK[("v_vfm", p, vi)])
                        if vi == 0 and p >= 1:
                            eng.wait_ge(sm_v, MK[("v_scat", p - 1, 1)])
                        if vi == 1:
                            eng.wait_ge(sm_v, MK[("v_scat", p, 0)])
                        eng.wait_ge(sm_d, val("d"))
                        inc(eng.dma_start(out=vraw.ap().rearrange("p (m c) -> p m c", c=128),
                                          in_=vfm[:, :], transpose=True), sm_d, "d", 16)
                        MK[("d_trV" if vi == 0 else "d_trG", p)] = val("d")
                    @blk.vector
                    def _(eng):
                        eng.wait_ge(sm_d, MK[("d_trV" if vi == 0 else "d_trG", p)])
                        if p >= 1 and vi == 0:
                            eng.wait_ge(sm_t, MK[("t_pv_done", p - 1, 3)])
                        dst = (v_tok if vi == 0 else gv_tok).ap().rearrange("p (tt s c) -> p tt s c", s=2, c=65)
                        inc(eng.tensor_copy(dst[:, :, :, 0:64],
                                            vraw.ap().rearrange("p (tt s c) -> p tt s c", s=2, c=64)), sm_v, "v")
                        MK[("v_scat", p, vi)] = val("v")

            def emit_rs(blk, sec, attn_i, ubuf, pv_mark):
                @blk.scalar
                def _(eng):
                    eng.wait_ge(sm_v, MK[("v_evac4", sec)])
                    for hh in range(2):
                        eng.activation(rows_f[64:65, 0:1024], ubuf[64:65, 1024 * hh:1024 * hh + 1024], AF.Ln)
                        inc(eng.activation(rows_b[64:65, T * attn_i + 1024 * hh:T * attn_i + 1024 * hh + 1024],
                                           rows_f[64:65, 0:1024], AF.Exp, scale=-1.0), sm_s, "s")
                MK[("rs", sec)] = val("s")

            def emit_section(blk, p, hi, attn_i, sec, prev_sec, pend_rs_info, pend_tail_info):
                is_std = attn_i == 0
                scale = 0.125 if is_std else 1.0
                vt3 = (v_tok if is_std else gv_tok).ap().rearrange("p (tt s c) -> p tt s c", s=2, c=65)
                ubuf = u_sb[sec % 4]
                e0, m0, t0 = val("s"), val("v"), val("t")
                g0 = GTI[0]

                rs_bump = 2 if pend_rs_info is not None else 0
                exp_no = {}
                for i in range(N_TILES):
                    exp_no[i] = e0 + (i + 1) + (rs_bump if i >= 4 else 0)
                if pend_rs_info is not None:
                    MK[("rs", pend_rs_info[0])] = e0 + 6
                nm = 0
                for i, (s_, kc, off, diag) in enumerate(TILES):
                    if diag:
                        nm += 1
                    mask_after[g0 + i] = m0 + nm
                SBANKS = [ps_b[0:128, 0:512], ps_c[0:128, 0:512],
                          ps_d[0:128, 0:512], ps_d[0:128, 512:1024]]
                DEPTH = 4
                if pend_tail_info is not None:
                    emit_fdma(blk, *pend_tail_info)
                @blk.tensor
                def _(eng):
                    if is_std:
                        eng.wait_ge(sm_v, MK[("v_scat", p, 0)])
                        eng.wait_ge(sm_v, MK[("v_vfm", p, 1)])
                        if ("v_perm_done", p - 1) in MK:
                            eng.wait_ge(sm_v, MK[("v_perm_done", p - 1)])
                    else:
                        eng.wait_ge(sm_v, MK[("v_scat", p, 1)])
                        eng.wait_ge(sm_d, MK[("d_ext", p)])
                    def S_mm(i):
                        s_, kc, off, diag = TILES[i]
                        bank = SBANKS[i % 4]
                        if is_std:
                            lhsT = kt_p[64 * hi:64 * hi + 64, 128 * kc:128 * kc + 128]
                            rhs = qt_p[64 * hi:64 * hi + 64, 512 * s_ + off:512 * s_ + 512]
                        else:
                            b0 = 64 * hi
                            c0 = T * (p % 2)
                            lhsT = wl_scr[b0:b0 + 64, c0 + 128 * kc:c0 + 128 * kc + 128]
                            rhs = rl_scr[b0:b0 + 64, c0 + 512 * s_ + off:c0 + 512 * s_ + 512]
                        inc(eng.matmul(bank[0:128, off:512], lhsT, rhs, start=True, stop=True), sm_t, "t")
                    for i0 in range(DEPTH):
                        S_mm(i0)
                    for g in range(N_TILES // 4):
                        i0 = 4 * g
                        eng.wait_ge(sm_s, exp_no[i0 + 3])
                        for i in range(i0 + 4, min(i0 + 8, N_TILES)):
                            S_mm(i)
                        dmax = max((i for i in range(i0, i0 + 4) if TILES[i][3]), default=None)
                        if dmax is not None:
                            eng.wait_ge(sm_v, mask_after[g0 + dmax])
                        if prev_sec is not None:
                            for i in range(i0, i0 + 4):
                                if TILES[i][1] == 0:
                                    eng.wait_ge(sm_v, EV[(prev_sec, TILES[i][0])])
                        for i in range(i0, i0 + 4):
                            s_, kc, off, diag = TILES[i]
                            eng.matmul(ps_a[0:65, 512 * s_ + off:512 * s_ + 512],
                                       vt3[:, kc, hi, 0:65],
                                       pt[:, 512 * ((g0 + i) % NSL) + off:512 * ((g0 + i) % NSL) + 512],
                                       start=(kc == 0), stop=(kc == 4 * s_ + 3))
                    inc(eng.nop(), sm_t, "t")
                    MK[("t_pv_done", p, hi * 2 + attn_i)] = val("t")
                    if hi == 1 and attn_i == 1:
                        MK[("t_geo_done", p)] = val("t")
                @blk.scalar
                def _(eng):
                    for g in range(N_TILES // 4):
                        i0 = 4 * g
                        eng.wait_ge(sm_t, t0 + i0 + 4)
                        eng.wait_ge(sm_v, mask_after.get(g0 + i0 + 3 - NSL, 0))
                        for i in range(i0, i0 + 4):
                            s_, kc, off, diag = TILES[i]
                            bank = SBANKS[i % 4]
                            inc(eng.activation(pt[:, 512 * ((g0 + i) % NSL) + off:512 * ((g0 + i) % NSL) + 512],
                                               bank[0:128, off:512], AF.Exp, scale=scale), sm_s, "s")
                        if g == 0 and pend_rs_info is not None:
                            sec_pv, attn_pv, ubuf_pv = pend_rs_info
                            eng.wait_ge(sm_v, MK[("v_evac4", sec_pv)])
                            for hh in range(2):
                                eng.activation(rows_f[64:65, 0:1024],
                                               ubuf_pv[64:65, 1024 * hh:1024 * hh + 1024], AF.Ln)
                                inc(eng.activation(rows_b[64:65, T * attn_pv + 1024 * hh:T * attn_pv + 1024 * hh + 1024],
                                                   rows_f[64:65, 0:1024], AF.Exp, scale=-1.0), sm_s, "s")
                @blk.vector
                def _(eng):
                    for i in range(N_TILES):
                        s_, kc, off, diag = TILES[i]
                        if pend_tail_info is not None and i == 11:
                            tail_vec_part1(eng, *pend_tail_info)
                        if pend_tail_info is not None and i == 13:
                            tail_vec_part2(eng, *pend_tail_info)
                        if diag:
                            eng.wait_ge(sm_s, exp_no[i])
                            inc(eng.tensor_tensor(pt[:, 512 * ((g0 + i) % NSL) + off:512 * ((g0 + i) % NSL) + off + 128],
                                                  pt[:, 512 * ((g0 + i) % NSL) + off:512 * ((g0 + i) % NSL) + off + 128],
                                                  ctri[:, :], ALU.mult), sm_v, "v")
                @blk.vector
                def _(eng):
                    eng.wait_ge(sm_t, MK[("t_pv_done", p, hi * 2 + attn_i)])
                    for s_ in range(NST):
                        inc(eng.tensor_copy(ubuf[0:65, 512 * s_:512 * s_ + 512],
                                            ps_a[0:65, 512 * s_:512 * s_ + 512]), sm_v, "v")
                        EV[(sec, s_)] = val("v")
                MK[("v_evac4", sec)] = val("v")
                GTI[0] = g0 + N_TILES

            def emit_fdma(blk, p_t, hi_t):
                @blk.sync
                def _(eng):
                    eng.wait_ge(sm_s, MK[("rs", p_t * 4 + hi_t * 2 + 1)])
                    eng.wait_ge(sm_v, val("v"))
                    eng.wait_ge(sm_d, val("d"))
                    inc(eng.dma_start(out=rs_hbm.rearrange("(a c) -> a c", a=1),
                                      in_=rows_b.ap()[64:65, 0:2 * T]), sm_d, "d", 16)
                    eng.wait_ge(sm_d, val("d"))
                    inc(eng.dma_start(out=rsb[:, :],
                                      in_=rs_hbm.rearrange("(a c) -> a c", a=1).broadcast_to([64, 2 * T])), sm_d, "d", 16)
                    MK[("d_f", p_t, hi_t)] = val("d")

            def tail_vec_part1(eng, p_t, hi_t):
                sec_std = p_t * 4 + hi_t * 2
                ub_s = u_sb[sec_std % 4]
                dst = comb.ap()[0:64, T * p_t:T * p_t + T] if hi_t == 0 else cmbB.ap()[:, :]
                eng.wait_ge(sm_d, MK[("d_f", p_t, hi_t)])
                inc(eng.tensor_tensor(tmpb[:, :], ub_s[0:64, :], rsb[:, 0:T], ALU.mult), sm_v, "v")
                inc(eng.tensor_tensor(dst, tmpb[:, :], omg_bc[:, :], ALU.mult), sm_v, "v")

            def tail_vec_part2(eng, p_t, hi_t):
                sec_geo = p_t * 4 + hi_t * 2 + 1
                ub_g = u_sb[sec_geo % 4]
                dst = comb.ap()[0:64, T * p_t:T * p_t + T] if hi_t == 0 else cmbB.ap()[:, :]
                inc(eng.tensor_tensor(tmpb[:, :], ub_g[0:64, :], rsb[:, T:2 * T], ALU.mult), sm_v, "v")
                inc(eng.tensor_tensor(tmpb[:, :], tmpb[:, :], g_bc[:, :], ALU.mult), sm_v, "v")
                inc(eng.tensor_tensor(dst, dst, tmpb[:, :], ALU.add), sm_v, "v")
                MK[("v_comb", p_t, hi_t)] = val("v")

            def emit_tail_vector(blk, p_t, hi_t):
                @blk.vector
                def _(eng):
                    tail_vec_part1(eng, p_t, hi_t)
                    tail_vec_part2(eng, p_t, hi_t)

            def emit_permuteB(blk, p_t):
                t0 = val("t")
                v0p = val("v")
                @blk.tensor
                def _(eng):
                    eng.wait_ge(sm_v, MK[("v_comb", p_t, 1)])
                    eng.wait_ge(sm_v, MK[("v_vfm", p_t + 1, 1)] if ("v_vfm", p_t + 1, 1) in MK else 0)
                    for q in range(4):
                        if q == 2:
                            eng.wait_ge(sm_v, v0p + 1)
                        bank = ps_b if q % 2 == 0 else ps_c
                        inc(eng.matmul(bank[64:128, 0:512], cident[:, :],
                                       cmbB[:, 512 * q:512 * q + 512], start=True, stop=True), sm_t, "t")
                @blk.vector
                def _(eng):
                    for r in range(2):
                        eng.wait_ge(sm_t, t0 + 2 * (r + 1))
                        for q2 in range(2):
                            inc(eng.tensor_copy(comb[64:128, T * p_t + 1024 * r + 512 * q2:T * p_t + 1024 * r + 512 * q2 + 512],
                                                (ps_b if q2 == 0 else ps_c)[64:128, 0:512]), sm_v, "v")
                MK[("v_perm_done", p_t)] = val("v")

            with nc.Block() as blk:
                emit_loads(blk, 0)
                emit_cast_qk(blk, 0)
                emit_loads_vg(blk, 0)
                emit_cast_vg(blk, 0)
                emit_extract(blk, 0)
                sec = 0
                prev_sec = None
                pend_rs = None
                pend_tail = None
                for p in range(NP):
                    emit_proj(blk, p)
                    if p + 1 < NP:
                        emit_loads(blk, p + 1)
                        emit_extract(blk, p + 1)
                    if p >= 1 and pend_rs is not None:
                        emit_rs(blk, pend_rs[0], pend_rs[1], pend_rs[2],
                                ("t_pv_done", p - 1, 3))
                        pend_rs = None
                        emit_fdma(blk, p - 1, 1)
                        emit_tail_vector(blk, p - 1, 1)
                        emit_permuteB(blk, p - 1)
                    if p + 1 < NP:
                        emit_cast_qk(blk, p + 1)
                        emit_loads_vg(blk, p + 1)
                        emit_cast_vg(blk, p + 1)
                    for si, (hi, attn_i) in enumerate([(0, 0), (0, 1), (1, 0), (1, 1)]):
                        if sec >= n_secs:
                            sec += 1
                            continue
                        emit_section(blk, p, hi, attn_i, sec, prev_sec,
                                     pend_rs, pend_tail)
                        pend_tail = None
                        pend_rs = (sec, attn_i, u_sb[sec % 4])
                        prev_sec = sec
                        sec += 1
                        if si == 1:
                            pend_tail = (p, 0)
                if pend_rs is not None and sec - 1 < n_secs and (NP - 1) * 4 + 3 < n_secs and not skip_final:
                    if final_lvl >= 1:
                        emit_rs(blk, pend_rs[0], pend_rs[1], pend_rs[2], ("t_pv_done", NP - 1, 3))
                        pend_rs = None
                    if final_lvl >= 2:
                        emit_fdma(blk, NP - 1, 1)
                    if final_lvl >= 3:
                        emit_tail_vector(blk, NP - 1, 1)
                    if final_lvl >= 4:
                        emit_permuteB(blk, NP - 1)
        S_pre.close()

        with ExitStack() as S_ao:
            big_f = sbuf("big_f", [128, NTT * 1024], F32, S_ao)
            wbf = sbuf("wbf", [128, NDC * 256], BF, S_ao)
            with ExitStack() as S4:
                wout = sbuf("wout", [128, NDC * D], BF, S4)
                ob_bc = sbuf("ob_bc", [128, D], F32, S4)
                with nc.Block() as blk:
                    @blk.sync
                    def _(eng):
                        inc(eng.dma_start(out=wstage[0:1, 0:1024], in_=out_b.rearrange("(a d) -> a d", a=1)), sm_d, "d", 16)
                    @blk.vector
                    def _(eng):
                        w3(eng, "d")
                        inc(eng.tensor_copy(wbf[0:1, 0:D], wstage[0:1, 0:1024]), sm_v, "v")
                        inc(eng.tensor_tensor(wbf[0:1, D:2 * D], wstage[0:1, 0:1024], wbf[0:1, 0:D], ALU.subtract), sm_v, "v")
                    @blk.tensor
                    def _(eng):
                        w3(eng, "v")
                        for i in range(2):
                            eng.matmul(ps_d[:, 512 * i:512 * i + 512], cones[0:1, 0:128],
                                       wbf[0:1, 512 * i:512 * i + 512], start=True, stop=False)
                            inc(eng.matmul(ps_d[:, 512 * i:512 * i + 512], cones[0:1, 0:128],
                                           wbf[0:1, D + 512 * i:D + 512 * i + 512], start=False, stop=True), sm_t, "t")
                    @blk.scalar
                    def _(eng):
                        w3(eng, "t")
                        inc(eng.activation(ob_bc[:, :], ps_d[:, 0:1024], AF.Copy), sm_s, "s")
                for dc in range(NDC):
                    with nc.Block() as blk:
                        @blk.sync
                        def _(eng):
                            inc(eng.dma_start(out=wstage[:, 0:1024],
                                              in_=out_w[128 * dc:128 * dc + 128, :]), sm_d, "d", 16)
                        @blk.vector
                        def _(eng):
                            w3(eng, "d")
                            inc(eng.tensor_copy(wout[:, D * dc:D * dc + D], wstage[:, 0:1024]), sm_v, "v")
                with nc.Block() as blk:
                    t0, v0 = val("t"), val("v")
                    @blk.tensor
                    def _(eng):
                        w3(eng, "v")
                        j = 0
                        for tt in range(NTT):
                            for os_ in range(2):
                                bank = ps_b if j % 2 == 0 else ps_c
                                if j >= 2:
                                    eng.wait_ge(sm_v, v0 + j - 1)
                                for c in range(8):
                                    inc(eng.matmul(bank[:, :], comb[:, T * c + 128 * tt:T * c + 128 * tt + 128],
                                                   wout[:, D * c + 512 * os_:D * c + 512 * os_ + 512],
                                                   start=(c == 0), stop=(c == 7)),
                                        sm_t, "t", 1 if c == 7 else 0)
                                j += 1
                    @blk.vector
                    def _(eng):
                        j = 0
                        for tt in range(NTT):
                            for os_ in range(2):
                                bank = ps_b if j % 2 == 0 else ps_c
                                eng.wait_ge(sm_t, t0 + j + 1)
                                inc(eng.tensor_tensor(big_f[:, 1024 * tt + 512 * os_:1024 * tt + 512 * os_ + 512],
                                                      bank[:, :], ob_bc[:, 512 * os_:512 * os_ + 512], ALU.add), sm_v, "v")
                                j += 1
            dump("comb", comb.ap(), BF)
            dump("attn_out", big_f.ap(), F32)
            S_comb.close()

            with ExitStack() as S5:
                xstage = sbuf("xstage", [128, 2 * 1024], F32, S5)
                with nc.Block() as blk:
                    emit_ln(blk, big_f.ap(), big_f.ap(), ("v",))
                with nc.Block() as blk:
                    d0, v0 = val("d"), val("v")
                    @blk.sync
                    def _(eng):
                        for tt in range(NTT):
                            if tt >= 2:
                                eng.wait_ge(sm_v, v0 + tt - 1)
                            inc(eng.dma_start(out=xstage[:, 1024 * (tt % 2):1024 * (tt % 2) + 1024],
                                              in_=x_h[128 * tt:128 * tt + 128, :]), sm_d, "d", 16)
                    @blk.vector
                    def _(eng):
                        for tt in range(NTT):
                            eng.wait_ge(sm_d, d0 + 16 * (tt + 1))
                            inc(eng.tensor_tensor(big_f[:, 1024 * tt:1024 * tt + 1024],
                                                  big_f[:, 1024 * tt:1024 * tt + 1024],
                                                  xstage[:, 1024 * (tt % 2):1024 * (tt % 2) + 1024], ALU.add), sm_v, "v")
                dump("x2", big_f.ap(), F32)
                with nc.Block() as blk:
                    @blk.gpsimd
                    def _(eng):
                        w3(eng, "v")
                        inc(eng.dma_start(out=x2s.rearrange("(tt p) d -> p tt d", p=128),
                                          in_=big_f.ap().rearrange("p (tt d) -> p tt d", d=1024)), sm_g2, "g2", 16)
                        eng.wait_ge(sm_g2, val("g2"))
                xln2_tok = sbuf("xln2_tok", [128, NTT * 1024], BF, S5)
                with nc.Block() as blk:
                    emit_ln(blk, big_f.ap(), xln2_tok.ap(), ("v",))
                with nc.Block() as blk:
                    @blk.sync
                    def _(eng):
                        w3(eng, "v")
                        xf = xln_fm.ap().rearrange("p (dc t) -> p dc t", t=T)
                        for tt in range(NTT):
                            inc(eng.dma_start(out=xf[:, :, 128 * tt:128 * tt + 128],
                                              in_=xln2_tok[:, 1024 * tt:1024 * tt + 1024],
                                              transpose=True), sm_d, "d", 16)
                        eng.wait_ge(sm_d, val("d"))

            with ExitStack() as S6:
                wt_fm = sbuf("wt_fm", [128, NDC * D], BF, S6)
                pooled = sbuf("pooled", [64, T], BF, S6)
                for oc in range(NDC):
                    with nc.Block() as blk:
                        @blk.sync
                        def _(eng):
                            inc(eng.dma_start(out=wstage[:, 0:1024],
                                              in_=ffn_w[128 * oc:128 * oc + 128, :]), sm_d, "d", 16)
                        @blk.vector
                        def _(eng):
                            w3(eng, "d")
                            inc(eng.tensor_copy(wbf[:, 0:1024], wstage[:, 0:1024]), sm_v, "v")
                        @blk.sync
                        def _(eng):
                            w3(eng, "v")
                            wt3 = wt_fm.ap().rearrange("p (dc oc c) -> p dc oc c", oc=8, c=128)
                            inc(eng.dma_start(out=wt3[:, :, oc, :], in_=wbf[:, 0:1024], transpose=True), sm_d, "d", 16)
                            eng.wait_ge(sm_d, val("d"))
                with nc.Block() as blk:
                    t0, v0 = val("t"), val("v")
                    @blk.tensor
                    def _(eng):
                        for st_ in range(NST):
                            if st_ >= 1:
                                eng.wait_ge(sm_v, v0 + st_)
                            for dc in range(NDC):
                                inc(eng.matmul(ps_d[0:64, 0:512], cpool[:, :],
                                               xln_fm[:, T * dc + 512 * st_:T * dc + 512 * st_ + 512],
                                               start=(dc == 0), stop=(dc == NDC - 1)),
                                    sm_t, "t", 1 if dc == NDC - 1 else 0)
                    @blk.vector
                    def _(eng):
                        for st_ in range(NST):
                            eng.wait_ge(sm_t, t0 + st_ + 1)
                            inc(eng.tensor_copy(pooled[:, 512 * st_:512 * st_ + 512], ps_d[0:64, 0:512]), sm_v, "v")
                with nc.Block() as blk:
                    t0, v0 = val("t"), val("v")
                    @blk.tensor
                    def _(eng):
                        w3(eng, "v")
                        j = 0
                        for tt in range(NTT):
                            for os_ in range(2):
                                bank = ps_b if j % 2 == 0 else ps_c
                                if j >= 2:
                                    eng.wait_ge(sm_v, v0 + j - 1)
                                for dc in range(NDC):
                                    eng.matmul(bank[:, :], xln_fm[:, T * dc + 128 * tt:T * dc + 128 * tt + 128],
                                               wt_fm[:, 1024 * dc + 512 * os_:1024 * dc + 512 * os_ + 512],
                                               start=(dc == 0), stop=False)
                                inc(eng.matmul(bank[:, :], pooled[:, 128 * tt:128 * tt + 128],
                                               disha_bf[:, 512 * os_:512 * os_ + 512],
                                               start=False, stop=True), sm_t, "t")
                                j += 1
                    @blk.vector
                    def _(eng):
                        j = 0
                        for tt in range(NTT):
                            for os_ in range(2):
                                bank = ps_b if j % 2 == 0 else ps_c
                                eng.wait_ge(sm_t, t0 + j + 1)
                                inc(eng.tensor_copy(big_f[:, 1024 * tt + 512 * os_:1024 * tt + 512 * os_ + 512],
                                                    bank[:, :]), sm_v, "v")
                                j += 1

            dump("ffn_raw", big_f.ap(), F32)

            with ExitStack() as S7:
                xstage = sbuf("xstage7", [128, 2 * 1024], F32, S7)
                with nc.Block() as blk:
                    emit_ln(blk, big_f.ap(), big_f.ap(), ("v",))
                with nc.Block() as blk:
                    d0, v0 = val("d"), val("v")
                    @blk.sync
                    def _(eng):
                        for tt in range(NTT):
                            if tt >= 2:
                                eng.wait_ge(sm_v, v0 + tt - 1)
                            inc(eng.dma_start(out=xstage[:, 1024 * (tt % 2):1024 * (tt % 2) + 1024],
                                              in_=x2s[128 * tt:128 * tt + 128, :]), sm_d, "d", 16)
                    @blk.vector
                    def _(eng):
                        for tt in range(NTT):
                            eng.wait_ge(sm_d, d0 + 16 * (tt + 1))
                            inc(eng.tensor_tensor(big_f[:, 1024 * tt:1024 * tt + 1024],
                                                  big_f[:, 1024 * tt:1024 * tt + 1024],
                                                  xstage[:, 1024 * (tt % 2):1024 * (tt % 2) + 1024], ALU.add), sm_v, "v")
                with nc.Block() as blk:
                    @blk.sync
                    def _(eng):
                        w3(eng, "v")
                        inc(eng.dma_start(out=y_h.rearrange("(tt p) d -> p tt d", p=128),
                                          in_=big_f.ap().rearrange("p (tt d) -> p tt d", d=1024)), sm_d, "d", 16)
                        eng.wait_ge(sm_d, val("d"))
    return nc


_NC_CACHE = [None]


def kernel(**inputs):
    consts = host_consts()
    base = {
        "qkv_w": inputs["L0_attn_qkv_w"], "qkv_b": inputs["L0_attn_qkv_b"],
        "w1_write": inputs["L0_attn_w1_write"], "w2_write": inputs["L0_attn_w2_write"],
        "w1_read": inputs["L0_attn_w1_read"], "w2_read": inputs["L0_attn_w2_read"],
        "geo_w": inputs["L0_attn_geo_w"], "geo_b": inputs["L0_attn_geo_b"],
        "gate_w": inputs["L0_attn_gate_w"], "gate_b": inputs["L0_attn_gate_b"],
        "inc_scale": inputs["L0_attn_inc_scale"],
        "out_w": inputs["L0_attn_out_w"], "out_b": inputs["L0_attn_out_b"],
        "ffn_weight": inputs["L0_ffn_weight"], "disha": inputs["L0_ffn_disha"],
    }
    base = {k: np.ascontiguousarray(np.asarray(v, dtype=np.float32)) for k, v in base.items()}
    for k, v in consts.items():
        base["c_" + k] = np.ascontiguousarray(v)
    x_full = np.asarray(inputs["L0_attn_x"], dtype=np.float32)
    in_maps = []
    for b in range(8):
        m = dict(base)
        m["x"] = np.ascontiguousarray(x_full[b])
        in_maps.append(m)
    if _NC_CACHE[0] is None:
        _NC_CACHE[0] = build_nc()
    res = run_bass_kernel_spmd(_NC_CACHE[0], in_maps, core_ids=list(range(8)))
    return np.stack([np.asarray(r["y"], dtype=np.float32) for r in res.results], axis=0)


# version 08
# speedup vs baseline: 1.1859x; 1.0051x over previous
import numpy as np
import ml_dtypes
from contextlib import ExitStack

import concourse.bass as bass
import concourse.mybir as mybir
from concourse.bass_utils import run_bass_kernel_spmd

BF16 = ml_dtypes.bfloat16
F32 = mybir.dt.float32
BF = mybir.dt.bfloat16
AF = mybir.ActivationFunctionType
ALU = mybir.AluOpType

T = 2048
D = 1024
H = 16
NTT = 16
NDC = 8
NST = 4
PAIRS4 = [(0, 1), (0, 2), (0, 3), (1, 2), (1, 3), (2, 3)]
SIGNS = [1.0, -1.0, 1.0, 1.0, -1.0, 1.0]
PAIRS_READ = [PAIRS4[5 - p] for p in range(6)]

TILES = []
for _s in range(NST):
    for _kc in range(4 * _s + 4):
        _diag = _kc >= 4 * _s
        TILES.append((_s, _kc, 128 * (_kc - 4 * _s) if _diag else 0, _diag))
N_TILES = len(TILES)


def host_consts():
    c = {}
    E = np.zeros((4, 128, 96), dtype=np.float32)
    for h in range(H):
        for p in range(6):
            col = 6 * h + p
            E[0, 4 * h + PAIRS4[p][0], col] = 1.0
            E[1, 4 * h + PAIRS4[p][1], col] = 1.0
            E[2, 4 * h + PAIRS_READ[p][0], col] = 1.0
            E[3, 4 * h + PAIRS_READ[p][1], col] = 1.0
    for k in range(4):
        E[k, 64:128, :] = E[k, 0:64, :]
    c['E'] = np.concatenate([E[0], E[1], E[2], E[3]], axis=1)
    coll = np.zeros((96, 16), dtype=np.float32)
    for h in range(H):
        coll[6 * h:6 * h + 6, h] = 1.0
    c['collect'] = coll
    spw = np.zeros((48, 96), dtype=np.float32)
    spr = np.zeros((48, 96), dtype=np.float32)
    for h in range(H):
        for p in range(6):
            spw[h, 6 * h + p] = 1.0
            spr[h, 6 * h + p] = SIGNS[p]
    spw[32:48] = spw[0:16]
    spr[32:48] = spr[0:16]
    c['spread_w'] = spw
    c['spread_r'] = spr
    c['tri'] = np.triu(np.ones((128, 128), dtype=np.float32))
    pool = np.zeros((128, 64), dtype=np.float32)
    for r in range(64):
        pool[r, r] = 1.0
        pool[64 + r, r] = 1.0
    c['pool'] = pool
    c['ident64'] = np.eye(64, dtype=np.float32)
    return c


def build_nc(debug=False, np_pairs=8, n_secs=999, skip_final=False, final_lvl=4):
    nc = bass.Bass("TRN2", target_bir_lowering=False)
    dbg_list = []
    dI = {}

    def din(name, shape):
        dI[name] = nc.dram_tensor(name, list(shape), F32, kind="ExternalInput").ap()
        return dI[name]

    x_h = din("x", (T, D))
    qkv_w = din("qkv_w", (D, 3 * D))
    qkv_b = din("qkv_b", (3 * D,))
    for nm in ["w1_write", "w2_write", "w1_read", "w2_read"]:
        din(nm, (D, 64))
    geo_w = din("geo_w", (D, D))
    geo_b = din("geo_b", (D,))
    gate_w = din("gate_w", (D, 16))
    gate_b = din("gate_b", (16,))
    inc_scale = din("inc_scale", (16,))
    out_w = din("out_w", (D, D))
    out_b = din("out_b", (D,))
    ffn_w = din("ffn_weight", (D, D))
    disha = din("disha", (64, D))
    for k, v in host_consts().items():
        din("c_" + k, v.shape)
    y_h = nc.dram_tensor("y", [T, D], F32, kind="ExternalOutput").ap()
    x2s = nc.dram_tensor("x2s", [T, D], F32).ap()
    rs_hbm = nc.dram_tensor("rs_hbm", [2 * T], BF).ap()

    ctr = {}

    def inc(instr, sem, name, k=1):
        if k > 0:
            instr.then_inc(sem, k)
            ctr[name] = ctr.get(name, 0) + k
        return ctr.get(name, 0)

    def val(name):
        return ctr.get(name, 0)

    with ExitStack() as ST:
        def sbuf(name, shape, dt, st=None, side=None):
            kw = {} if side is None else {"side": side}
            return (st or ST).enter_context(nc.sbuf_tensor(name, shape, dt, **kw))

        xln_fm = sbuf("xln_fm", [128, NDC * T], BF)
        w_small = sbuf("w_small", [128, NDC * 320], BF)
        qkb_sb = sbuf("qkb_sb", [128, 24], F32)
        gb_sb = sbuf("gb_sb", [128, 8], F32)
        gateb = sbuf("gateb", [16, 1], F32)
        incs = sbuf("incs", [48, 1], F32)
        cE = sbuf("cE", [128, 384], BF)
        ccoll = sbuf("ccoll", [96, 16], BF)
        cspw = sbuf("cspw", [48, 96], BF)
        cspr = sbuf("cspr", [48, 96], BF)
        ctri = sbuf("ctri", [128, 128], BF)
        cpool = sbuf("cpool", [128, 64], BF)
        cident = sbuf("cident", [64, 64], BF)
        cones = sbuf("cones", [128, 128], BF)
        disha_bf = sbuf("disha_bf", [64, D], BF)
        wstage = sbuf("wstage", [128, NDC * 256], F32)
        stats = sbuf("stats", [128, NTT * 16], F32)

        S_pre = ExitStack()
        rows_f = sbuf("rows_f", [128, 1024], F32, S_pre)
        rows_b = sbuf("rows_b", [128, 2 * T], BF, S_pre)
        omg_bc = sbuf("omg_bc", [64, T], BF, S_pre)
        g_bc = sbuf("g_bc", [64, T], BF, S_pre)
        wl_buf = sbuf("wl_buf", [96, T], BF, S_pre)
        rl_buf = sbuf("rl_buf", [96, T], BF, S_pre)
        S_comb = ExitStack()

        ps_a = ST.enter_context(nc.psum_tensor("ps_a", [128, 2048], F32))
        ps_b = ST.enter_context(nc.psum_tensor("ps_b", [128, 512], F32))
        ps_c = ST.enter_context(nc.psum_tensor("ps_c", [128, 512], F32))
        ps_d = ST.enter_context(nc.psum_tensor("ps_d", [128, 1024], F32))

        sm_d = ST.enter_context(nc.semaphore("sm_d"))
        sm_g = ST.enter_context(nc.semaphore("sm_g"))
        sm_t = ST.enter_context(nc.semaphore("sm_t"))
        sm_s = ST.enter_context(nc.semaphore("sm_s"))
        sm_v = ST.enter_context(nc.semaphore("sm_v"))
        sm_f = ST.enter_context(nc.semaphore("sm_f"))
        sm_g2 = ST.enter_context(nc.semaphore("sm_g2"))
        SEM = {"d": sm_d, "g": sm_g, "t": sm_t, "s": sm_s, "v": sm_v}

        def w3(eng, *names):
            for name in names:
                eng.wait_ge(SEM[name], val(name))

        def dump(name, src_ap, dt):
            if not debug:
                return
            p, w = src_ap.shape[0], src_ap.shape[1]
            od = nc.dram_tensor("dbg_" + name, [p, w], dt, kind="ExternalOutput").ap()
            with nc.Block() as blk:
                @blk.sync
                def _(eng):
                    w3(eng, "v", "s", "t", "g")
                    inc(eng.dma_start(out=od, in_=src_ap), sm_d, "d", 16)
                    eng.wait_ge(sm_d, val("d"))

        with ExitStack() as S0:
            cst = sbuf("cst", [128, 1024], F32, S0)
            with nc.Block() as blk:
                @blk.sync
                def _(eng):
                    ws3 = wstage.ap().rearrange("p (dc c) -> p dc c", c=256)
                    for i, nm in enumerate(["w1_write", "w2_write", "w1_read", "w2_read"]):
                        inc(eng.dma_start(out=ws3[:, :, 64 * i:64 * i + 64],
                                          in_=dI[nm].rearrange("(dc p) c -> p dc c", p=128)), sm_d, "d", 16)
                    inc(eng.dma_start(out=qkb_sb[:, :], in_=qkv_b.rearrange("(c p) -> p c", p=128), allow_slow_non_contiguous=True), sm_d, "d", 16)
                    inc(eng.dma_start(out=gb_sb[:, :], in_=geo_b.rearrange("(c p) -> p c", p=128), allow_slow_non_contiguous=True), sm_d, "d", 16)
                    inc(eng.dma_start(out=gateb[:, :], in_=gate_b.rearrange("(a b) -> a b", b=1)), sm_d, "d", 16)
                    inc(eng.dma_start(out=incs[0:16, :], in_=inc_scale.rearrange("(a b) -> a b", b=1)), sm_d, "d", 16)
                    inc(eng.dma_start(out=incs[32:48, :], in_=inc_scale.rearrange("(a b) -> a b", b=1)), sm_d, "d", 16)
                @blk.vector
                def _(eng):
                    w3(eng, "d")
                    for dc in range(NDC):
                        inc(eng.tensor_copy(w_small[:, 320 * dc:320 * dc + 256],
                                            wstage[:, 256 * dc:256 * dc + 256]), sm_v, "v")
                @blk.gpsimd
                def _(eng):
                    eng.memset(cones[:, :], 1.0)
                    inc(eng.engine_nop(), sm_g, "g")
            with nc.Block() as blk:
                @blk.sync
                def _(eng):
                    ws3 = wstage.ap().rearrange("p (dc c) -> p dc c", c=256)
                    inc(eng.dma_start(out=ws3[:, :, 0:16],
                                      in_=gate_w.rearrange("(dc p) c -> p dc c", p=128)), sm_d, "d", 16)
                    inc(eng.dma_start(out=cst[0:128, 0:384], in_=dI["c_E"]), sm_d, "d", 16)
                    inc(eng.dma_start(out=cst[0:96, 384:400], in_=dI["c_collect"]), sm_d, "d", 16)
                    inc(eng.dma_start(out=cst[0:128, 400:528], in_=dI["c_tri"]), sm_d, "d", 16)
                    inc(eng.dma_start(out=cst[0:128, 528:592], in_=dI["c_pool"]), sm_d, "d", 16)
                    inc(eng.dma_start(out=cst[0:64, 592:656], in_=dI["c_ident64"]), sm_d, "d", 16)
                    inc(eng.dma_start(out=cst[0:48, 656:752], in_=dI["c_spread_w"]), sm_d, "d", 16)
                    inc(eng.dma_start(out=cst[0:48, 752:848], in_=dI["c_spread_r"]), sm_d, "d", 16)
                @blk.vector
                def _(eng):
                    w3(eng, "d")
                    for dc in range(NDC):
                        inc(eng.tensor_copy(w_small[:, 320 * dc + 256:320 * dc + 272],
                                            wstage[:, 256 * dc:256 * dc + 16]), sm_v, "v")
                    inc(eng.tensor_copy(cE[:, :], cst[0:128, 0:384]), sm_v, "v")
                    inc(eng.tensor_copy(ccoll[:, :], cst[0:96, 384:400]), sm_v, "v")
                    inc(eng.tensor_copy(ctri[:, :], cst[0:128, 400:528]), sm_v, "v")
                    inc(eng.tensor_copy(cpool[:, :], cst[0:128, 528:592]), sm_v, "v")
                    inc(eng.tensor_copy(cident[:, :], cst[0:64, 592:656]), sm_v, "v")
                    inc(eng.tensor_copy(cspw[:, :], cst[0:48, 656:752]), sm_v, "v")
                    inc(eng.tensor_scalar(cspr[0:16, :], cst[0:16, 752:848], incs[0:16, :], None, ALU.mult, ALU.bypass), sm_v, "v")
                    inc(eng.tensor_scalar(cspr[32:48, :], cst[32:48, 752:848], incs[32:48, :], None, ALU.mult, ALU.bypass), sm_v, "v")
            with nc.Block() as blk:
                @blk.sync
                def _(eng):
                    inc(eng.dma_start(out=cst[0:64, 0:D], in_=disha), sm_d, "d", 16)
                @blk.vector
                def _(eng):
                    w3(eng, "d")
                    inc(eng.tensor_copy(disha_bf[:, :], cst[0:64, 0:D]), sm_v, "v")

        def emit_ln(blk, src, dst, waits):
            @blk.vector
            def _(eng):
                w3(eng, *waits)
                for tt in range(NTT):
                    c0 = 16 * tt
                    eng.bn_stats(stats[:, c0:c0 + 6], src[:, 1024 * tt:1024 * tt + 512])
                    eng.bn_stats(stats[:, c0 + 6:c0 + 12], src[:, 1024 * tt + 512:1024 * tt + 1024])
                for tt in range(NTT):
                    c0 = 16 * tt
                    eng.bn_aggr(stats[:, c0 + 12:c0 + 14], stats[:, c0:c0 + 12])
                for tt in range(NTT):
                    c0 = 16 * tt
                    inc(eng.tensor_scalar(stats[:, c0 + 13:c0 + 14], stats[:, c0 + 13:c0 + 14],
                                          1e-5, None, ALU.add, ALU.bypass), sm_v, "v")
            @blk.scalar
            def _(eng):
                w3(eng, "v")
                for tt in range(NTT):
                    c0 = 16 * tt
                    eng.activation(stats[:, c0 + 13:c0 + 14], stats[:, c0 + 13:c0 + 14], AF.Ln)
                    inc(eng.activation(stats[:, c0 + 14:c0 + 15], stats[:, c0 + 13:c0 + 14],
                                       AF.Exp, scale=-0.5), sm_s, "s")
            @blk.vector
            def _(eng):
                w3(eng, "s")
                for tt in range(NTT):
                    c0 = 16 * tt
                    inc(eng.tensor_scalar(dst[:, 1024 * tt:1024 * tt + 1024],
                                          src[:, 1024 * tt:1024 * tt + 1024],
                                          stats[:, c0 + 12:c0 + 13], stats[:, c0 + 14:c0 + 15],
                                          ALU.subtract, ALU.mult), sm_v, "v")

        def load_cast(blk, jobs):
            @blk.sync
            def _(eng):
                off = 0
                for src, c0, w in jobs:
                    ws3 = wstage.ap().rearrange("p (dc c) -> p dc c", c=256)
                    inc(eng.dma_start(out=ws3[:, :, off:off + w], in_=src), sm_d, "d", 16)
                    off += w
            @blk.vector
            def _(eng):
                w3(eng, "d")
                off = 0
                for src, c0, w in jobs:
                    for dc in range(NDC):
                        inc(eng.tensor_copy(wbf[:, 256 * dc + c0:256 * dc + c0 + w],
                                            wstage[:, 256 * dc + off:256 * dc + off + w]), sm_v, "v")
                    off += w

        with ExitStack() as S1:
            x_res = sbuf("x_res", [128, NTT * 1024], F32, S1)
            xln_tok = sbuf("xln_tok", [128, NTT * 1024], BF, S1)
            with nc.Block() as blk:
                @blk.sync
                def _(eng):
                    inc(eng.dma_start(out=x_res.ap().rearrange("p (tt d) -> p tt d", d=1024),
                                      in_=x_h.rearrange("(tt p) d -> p tt d", p=128)), sm_d, "d", 16)
            with nc.Block() as blk:
                emit_ln(blk, x_res.ap(), xln_tok.ap(), ("d",))
            with nc.Block() as blk:
                @blk.sync
                def _(eng):
                    w3(eng, "v")
                    xf = xln_fm.ap().rearrange("p (dc t) -> p dc t", t=T)
                    for tt in range(NTT):
                        inc(eng.dma_start(out=xf[:, :, 128 * tt:128 * tt + 128],
                                          in_=xln_tok[:, 1024 * tt:1024 * tt + 1024],
                                          transpose=True), sm_d, "d", 16)
                    eng.wait_ge(sm_d, val("d"))
        dump("xln_fm", xln_fm.ap(), BF)

        with ExitStack() as S2:
            pw_w = sbuf("pw_w", [128, T], BF, S2)
            pw_r = sbuf("pw_r", [128, T], BF, S2)
            gate_sig = sbuf("gate_sig", [16, T], BF, S2)
            gw1xi = sbuf("gw1xi", [96, T + 4], BF, S2)
            gw1xj = sbuf("gw1xj", [96, T + 4], BF, S2)
            r1xi = sbuf("r1xi", [96, 512], BF, S2)
            r1xj = sbuf("r1xj", [96, 512], BF, S2)
            m1s = sbuf("m1s", [96, 512], F32, S2)
            m2s = sbuf("m2s", [96, 512], F32, S2)
            Lwsq = sbuf("Lwsq", [96, 512], BF, S2)
            Lrsq = sbuf("Lrsq", [96, 512], BF, S2)
            Lw = sbuf("Lw", [96, 512], BF, S2)
            Lr = sbuf("Lr", [96, 512], BF, S2)
            inv_sb = sbuf("inv_sb", [48, T], BF, S2)

            with nc.Block() as blk:
                t0, s0 = val("t"), val("s")
                @blk.tensor
                def _(eng):
                    for st_ in range(NST):
                        bank = ps_b if st_ % 2 == 0 else ps_c
                        if st_ >= 2:
                            eng.wait_ge(sm_s, s0 + st_ - 1)
                        for dc in range(NDC):
                            inc(eng.matmul(bank[0:16, :], w_small[:, 320 * dc + 256:320 * dc + 272],
                                           xln_fm[:, T * dc + 512 * st_:T * dc + 512 * st_ + 512],
                                           start=(dc == 0), stop=(dc == NDC - 1)),
                                sm_t, "t", 1 if dc == NDC - 1 else 0)
                @blk.scalar
                def _(eng):
                    for st_ in range(NST):
                        eng.wait_ge(sm_t, t0 + st_ + 1)
                        inc(eng.activation(gate_sig[:, 512 * st_:512 * st_ + 512],
                                           (ps_b if st_ % 2 == 0 else ps_c)[0:16, :],
                                           AF.Sigmoid, bias=gateb[0:16, :]), sm_s, "s")
            with nc.Block() as blk:
                @blk.tensor
                def _(eng):
                    w3(eng, "s")
                    for st_ in range(NST):
                        inc(eng.matmul(ps_a[0:1, 512 * st_:512 * st_ + 512], cones[0:16, 0:1],
                                       gate_sig[:, 512 * st_:512 * st_ + 512], start=True, stop=True), sm_t, "t")
                @blk.scalar
                def _(eng):
                    w3(eng, "t")
                    inc(eng.activation(rows_f[0:1, 0:1024], ps_a[0:1, 0:1024], AF.Copy, scale=1.0 / 16.0), sm_s, "s")
                @blk.vector
                def _(eng):
                    w3(eng, "s")
                    inc(eng.tensor_copy(rows_b[0:1, 0:1024], rows_f[0:1, 0:1024]), sm_v, "v")
                    inc(eng.tensor_tensor(rows_b[0:1, T:T + 1024], rows_f[0:1, 0:1024], rows_b[0:1, 0:1024], ALU.subtract), sm_v, "v")
                @blk.scalar
                def _(eng):
                    w3(eng, "v")
                    inc(eng.activation(rows_f[0:1, 0:1024], ps_a[0:1, 1024:2048], AF.Copy, scale=1.0 / 16.0), sm_s, "s")
                @blk.vector
                def _(eng):
                    w3(eng, "s")
                    inc(eng.tensor_copy(rows_b[0:1, 1024:2048], rows_f[0:1, 0:1024]), sm_v, "v")
                    inc(eng.tensor_tensor(rows_b[0:1, T + 1024:2 * T], rows_f[0:1, 0:1024], rows_b[0:1, 1024:2048], ALU.subtract), sm_v, "v")
            for half in range(2):
                c0 = 1024 * half
                with nc.Block() as blk:
                    @blk.tensor
                    def _(eng):
                        w3(eng, "v")
                        for q in range(2):
                            eng.matmul(ps_d[0:64, 512 * q:512 * q + 512], cones[0:1, 0:64],
                                       rows_b[0:1, c0 + 512 * q:c0 + 512 * q + 512], start=True, stop=False)
                            inc(eng.matmul(ps_d[0:64, 512 * q:512 * q + 512], cones[0:1, 0:64],
                                           rows_b[0:1, T + c0 + 512 * q:T + c0 + 512 * q + 512], start=False, stop=True), sm_t, "t")
                    @blk.vector
                    def _(eng):
                        w3(eng, "t")
                        inc(eng.tensor_copy(g_bc[:, c0:c0 + 1024], ps_d[0:64, :]), sm_v, "v")
                        inc(eng.tensor_scalar(omg_bc[:, c0:c0 + 1024], ps_d[0:64, :], 1.0, -1.0, ALU.subtract, ALU.mult), sm_v, "v")

            dump("g_bc", g_bc.ap(), F32)
            dump("omg_bc", omg_bc.ap(), F32)
            with nc.Block() as blk:
                t0, v0 = val("t"), val("v")
                @blk.tensor
                def _(eng):
                    j = 0
                    for which in (0, 128):
                        for st_ in range(NST):
                            bank = ps_b if j % 2 == 0 else ps_c
                            if j >= 2:
                                eng.wait_ge(sm_v, v0 + j - 1)
                            for dc in range(NDC):
                                inc(eng.matmul(bank[:, :], w_small[:, 320 * dc + which:320 * dc + which + 128],
                                               xln_fm[:, T * dc + 512 * st_:T * dc + 512 * st_ + 512],
                                               start=(dc == 0), stop=(dc == NDC - 1)),
                                    sm_t, "t", 1 if dc == NDC - 1 else 0)
                            j += 1
                @blk.vector
                def _(eng):
                    j = 0
                    for which, dst in [(0, pw_w), (128, pw_r)]:
                        for st_ in range(NST):
                            bank = ps_b if j % 2 == 0 else ps_c
                            eng.wait_ge(sm_t, t0 + j + 1)
                            inc(eng.tensor_copy(dst[:, 512 * st_:512 * st_ + 512], bank[:, :]), sm_v, "v")
                            j += 1

            with nc.Block() as blk:
                @blk.gpsimd
                def _(eng):
                    eng.memset(gw1xi[:, 0:1], 0.0)
                    eng.memset(gw1xj[:, 0:1], 0.0)
                    inc(eng.engine_nop(), sm_g, "g")

            for st_ in range(NST):
                q0 = 512 * st_
                with nc.Block() as blk:
                    @blk.tensor
                    def _(eng):
                        w3(eng, "v", "g")
                        inc(eng.matmul(ps_a[0:96, 0:512], cE[0:64, 0:96], pw_w[0:64, q0:q0 + 512], start=True, stop=True), sm_t, "t")
                        inc(eng.matmul(ps_a[0:96, 512:1024], cE[0:64, 96:192], pw_w[0:64, q0:q0 + 512], start=True, stop=True), sm_t, "t")
                        inc(eng.matmul(ps_a[0:96, 1024:1536], cE[64:128, 0:96], pw_w[64:128, q0:q0 + 512], start=True, stop=True), sm_t, "t")
                        inc(eng.matmul(ps_a[0:96, 1536:2048], cE[64:128, 96:192], pw_w[64:128, q0:q0 + 512], start=True, stop=True), sm_t, "t")
                        inc(eng.matmul(ps_d[0:96, 0:512], cE[0:64, 192:288], pw_r[0:64, q0:q0 + 512], start=True, stop=True), sm_t, "t")
                        inc(eng.matmul(ps_d[0:96, 512:1024], cE[0:64, 288:384], pw_r[0:64, q0:q0 + 512], start=True, stop=True), sm_t, "t")
                        inc(eng.matmul(ps_b[0:96, :], cE[64:128, 192:288], pw_r[64:128, q0:q0 + 512], start=True, stop=True), sm_t, "t")
                        inc(eng.matmul(ps_c[0:96, :], cE[64:128, 288:384], pw_r[64:128, q0:q0 + 512], start=True, stop=True), sm_t, "t")
                    @blk.vector
                    def _(eng):
                        w3(eng, "t")
                        inc(eng.tensor_copy(gw1xi[:, q0 + 1:q0 + 513], ps_a[0:96, 0:512]), sm_v, "v")
                        inc(eng.tensor_copy(gw1xj[:, q0 + 1:q0 + 513], ps_a[0:96, 512:1024]), sm_v, "v")
                        inc(eng.tensor_copy(r1xi[:, :], ps_d[0:96, 0:512]), sm_v, "v")
                        inc(eng.tensor_copy(r1xj[:, :], ps_d[0:96, 512:1024]), sm_v, "v")
                        inc(eng.tensor_tensor(m1s[:, :], gw1xi[:, q0:q0 + 512], ps_a[0:96, 1536:2048], ALU.mult), sm_v, "v")
                        inc(eng.tensor_tensor(m2s[:, :], gw1xj[:, q0:q0 + 512], ps_a[0:96, 1024:1536], ALU.mult), sm_v, "v")
                        inc(eng.tensor_tensor(Lw[:, :], m1s[:, :], m2s[:, :], ALU.subtract), sm_v, "v")
                        inc(eng.tensor_tensor(m1s[:, :], r1xi[:, :], ps_c[0:96, :], ALU.mult), sm_v, "v")
                        inc(eng.tensor_tensor(m2s[:, :], r1xj[:, :], ps_b[0:96, :], ALU.mult), sm_v, "v")
                        inc(eng.tensor_tensor(Lr[:, :], m1s[:, :], m2s[:, :], ALU.subtract), sm_v, "v")
                        inc(eng.tensor_tensor(Lwsq[:, :], Lw[:, :], Lw[:, :], ALU.mult), sm_v, "v")
                        inc(eng.tensor_tensor(Lrsq[:, :], Lr[:, :], Lr[:, :], ALU.mult), sm_v, "v")
                    @blk.tensor
                    def _(eng):
                        w3(eng, "v")
                        inc(eng.matmul(ps_b[0:16, :], ccoll[:, :], Lwsq[:, :], start=True, stop=True), sm_t, "t")
                        inc(eng.matmul(ps_b[32:48, :], ccoll[:, :], Lrsq[:, :], start=True, stop=True), sm_t, "t")
                    @blk.vector
                    def _(eng):
                        w3(eng, "t")
                        inc(eng.tensor_scalar(ps_b[0:16, :], ps_b[0:16, :], 1e-19, None, ALU.max, ALU.bypass), sm_v, "v")
                        inc(eng.tensor_scalar(ps_b[32:48, :], ps_b[32:48, :], 1e-19, None, ALU.max, ALU.bypass), sm_v, "v")
                    @blk.scalar
                    def _(eng):
                        w3(eng, "v")
                        eng.activation(ps_b[0:16, :], ps_b[0:16, :], AF.Ln)
                        eng.activation(ps_b[32:48, :], ps_b[32:48, :], AF.Ln)
                        inc(eng.activation(inv_sb[0:16, q0:q0 + 512], ps_b[0:16, :], AF.Exp, scale=-0.5), sm_s, "s")
                        inc(eng.activation(inv_sb[32:48, q0:q0 + 512], ps_b[32:48, :], AF.Exp, scale=-0.5), sm_s, "s")
                    @blk.tensor
                    def _(eng):
                        w3(eng, "s")
                        inc(eng.matmul(ps_a[0:96, 0:512], cspw[0:16, :], inv_sb[0:16, q0:q0 + 512], start=True, stop=True), sm_t, "t")
                        inc(eng.matmul(ps_a[0:96, 512:1024], cspr[32:48, :], inv_sb[32:48, q0:q0 + 512], start=True, stop=True), sm_t, "t")
                    @blk.vector
                    def _(eng):
                        w3(eng, "t")
                        inc(eng.tensor_tensor(wl_buf[:, q0:q0 + 512], Lw[:, :], ps_a[0:96, 0:512], ALU.mult), sm_v, "v")
                        inc(eng.tensor_tensor(rl_buf[:, q0:q0 + 512], Lr[:, :], ps_a[0:96, 512:1024], ALU.mult), sm_v, "v")
            dump("pw_w", pw_w.ap(), BF)
            dump("pw_r", pw_r.ap(), BF)
            dump("inv_dbg", inv_sb.ap(), BF)
        dump("wl_buf", wl_buf.ap(), BF)
        dump("rl_buf", rl_buf.ap(), BF)

        comb = sbuf("comb", [128, 8 * T], BF, S_comb, side="right")
        with ExitStack() as S3:
            qt_p = sbuf("qt_p", [128, T], BF, S3)
            kt_p = sbuf("kt_p", [128, T], BF, S3)
            vfm = sbuf("vfm", [128, T], BF, S3)
            vraw = sbuf("vraw", [128, T], BF, S3)
            v_tok = sbuf("v_tok", [128, NTT * 130], BF, S3)
            gv_tok = sbuf("gv_tok", [128, NTT * 130], BF, S3)
            NSL = 8
            pt = sbuf("pt", [128, NSL * 512], BF, S3)
            wl_scr = sbuf("wl_scr", [128, 2 * T], BF, S3)
            rl_scr = sbuf("rl_scr", [128, 2 * T], BF, S3)
            u_sb = [sbuf("u_sb%d" % i, [65, T], BF, S3) for i in range(4)]
            rsb = sbuf("rsb", [64, 2 * T], BF, S3)
            tmpb = sbuf("tmpb", [64, T], BF, S3)
            cmbB = sbuf("cmbB", [64, T], BF, S3)
            wbf2 = sbuf("wbf2", [128, 2 * NDC * 512], BF, S3)

            with nc.Block() as blk:
                @blk.gpsimd
                def _(eng):
                    vt = v_tok.ap().rearrange("p (tt s c) -> p tt s c", s=2, c=65)
                    gt = gv_tok.ap().rearrange("p (tt s c) -> p tt s c", s=2, c=65)
                    eng.memset(vt[:, :, :, 64:65], 1.0)
                    eng.memset(gt[:, :, :, 64:65], 1.0)
                    eng.memset(wl_scr[:, :], 0.0)
                    eng.memset(rl_scr[:, :], 0.0)
                    inc(eng.nop(), sm_g, "g")

            MK = {}
            NP = np_pairs
            GTI = [0]
            mask_after = {-1: 0}
            EV = {}

            def wslot(p):
                return 4096 * (p % 2)

            def emit_loads(blk, p):
                @blk.sync
                def _(eng):
                    if p > 0:
                        eng.wait_ge(sm_v, MK[("cast_vg", p - 1)])
                    ws3 = wstage.ap().rearrange("p (dc c) -> p dc c", c=256)
                    inc(eng.dma_start(out=ws3[:, :, 0:128],
                                      in_=qkv_w[:, 128 * p:128 * p + 128].rearrange("(dc p) c -> p dc c", p=128)), sm_d, "d", 16)
                    inc(eng.dma_start(out=ws3[:, :, 128:256],
                                      in_=qkv_w[:, D + 128 * p:D + 128 * p + 128].rearrange("(dc p) c -> p dc c", p=128)), sm_d, "d", 16)
                    MK[("d_qk", p)] = val("d")

            def emit_loads_vg(blk, p):
                @blk.sync
                def _(eng):
                    eng.wait_ge(sm_v, MK[("cast_qk", p)])
                    ws3 = wstage.ap().rearrange("p (dc c) -> p dc c", c=256)
                    inc(eng.dma_start(out=ws3[:, :, 0:128],
                                      in_=qkv_w[:, 2 * D + 128 * p:2 * D + 128 * p + 128].rearrange("(dc p) c -> p dc c", p=128)), sm_d, "d", 16)
                    inc(eng.dma_start(out=ws3[:, :, 128:256],
                                      in_=geo_w[:, 128 * p:128 * p + 128].rearrange("(dc p) c -> p dc c", p=128)), sm_d, "d", 16)
                    MK[("d_vg", p)] = val("d")

            def emit_cast_qk(blk, p):
                @blk.vector
                def _(eng):
                    eng.wait_ge(sm_d, MK[("d_qk", p)])
                    if p >= 2:
                        eng.wait_ge(sm_t, MK[("t_proj", p - 2)])
                    sl = wslot(p)
                    for dc in range(NDC):
                        inc(eng.tensor_copy(wbf2[:, sl + 512 * dc:sl + 512 * dc + 256],
                                            wstage[:, 256 * dc:256 * dc + 256]), sm_v, "v")
                    MK[("cast_qk", p)] = val("v")

            def emit_cast_vg(blk, p):
                @blk.vector
                def _(eng):
                    eng.wait_ge(sm_d, MK[("d_vg", p)])
                    sl = wslot(p)
                    for dc in range(NDC):
                        inc(eng.tensor_copy(wbf2[:, sl + 512 * dc + 256:sl + 512 * dc + 512],
                                            wstage[:, 256 * dc:256 * dc + 256]), sm_v, "v")
                    MK[("cast_vg", p)] = val("v")

            def emit_extract(blk, p):
                @blk.sync
                def _(eng):
                    if p >= 2:
                        eng.wait_ge(sm_t, MK[("t_geo_done", p - 2)])
                    eng.wait_ge(sm_d, val("d"))
                    c0 = T * (p % 2)
                    hA, hB = 2 * p, 2 * p + 1
                    inc(eng.dma_start(out=wl_scr[0:6, c0:c0 + T], in_=wl_buf[6 * hA:6 * hA + 6, :]), sm_d, "d", 16)
                    inc(eng.dma_start(out=wl_scr[64:70, c0:c0 + T], in_=wl_buf[6 * hB:6 * hB + 6, :]), sm_d, "d", 16)
                    inc(eng.dma_start(out=rl_scr[0:6, c0:c0 + T], in_=rl_buf[6 * hA:6 * hA + 6, :]), sm_d, "d", 16)
                    inc(eng.dma_start(out=rl_scr[64:70, c0:c0 + T], in_=rl_buf[6 * hB:6 * hB + 6, :]), sm_d, "d", 16)
                    MK[("d_ext", p)] = val("d")

            def emit_proj(blk, p):
                sl = wslot(p)
                t0 = val("t")
                v0 = val("v")
                @blk.tensor
                def _(eng):
                    eng.wait_ge(sm_v, MK[("cast_qk", p)])
                    j = 0
                    for which in (0, 128):
                        for st_ in range(NST):
                            bank = ps_b if j % 2 == 0 else ps_c
                            if j >= 2:
                                eng.wait_ge(sm_v, v0 + j - 1)
                            for dc in range(NDC):
                                inc(eng.matmul(bank[:, :], wbf2[:, sl + 512 * dc + which:sl + 512 * dc + which + 128],
                                               xln_fm[:, T * dc + 512 * st_:T * dc + 512 * st_ + 512],
                                               start=(dc == 0), stop=(dc == NDC - 1)),
                                    sm_t, "t", 1 if dc == NDC - 1 else 0)
                            j += 1
                @blk.vector
                def _(eng):
                    j = 0
                    for dst, bcol in [(qt_p, p), (kt_p, 8 + p)]:
                        for st_ in range(NST):
                            bank = ps_b if j % 2 == 0 else ps_c
                            eng.wait_ge(sm_t, t0 + j + 1)
                            inc(eng.tensor_scalar(dst[:, 512 * st_:512 * st_ + 512], bank[:, :],
                                                  qkb_sb[:, bcol:bcol + 1], None, ALU.add, ALU.bypass), sm_v, "v")
                            j += 1
                for vi in range(2):
                    t1 = val("t")
                    v1 = val("v")
                    which = 256 if vi == 0 else 384
                    @blk.tensor
                    def _(eng):
                        eng.wait_ge(sm_v, MK[("cast_vg", p)])
                        for st_ in range(NST):
                            bank = ps_b if st_ % 2 == 0 else ps_c
                            if st_ >= 2:
                                eng.wait_ge(sm_v, v1 + st_ - 1)
                            for dc in range(NDC):
                                inc(eng.matmul(bank[:, :], wbf2[:, sl + 512 * dc + which:sl + 512 * dc + which + 128],
                                               xln_fm[:, T * dc + 512 * st_:T * dc + 512 * st_ + 512],
                                               start=(dc == 0), stop=(dc == NDC - 1)),
                                    sm_t, "t", 1 if dc == NDC - 1 else 0)
                    MK[("t_proj", p)] = val("t")
                    @blk.vector
                    def _(eng):
                        if vi == 1:
                            eng.wait_ge(sm_d, MK[("d_trV", p)])
                        for st_ in range(NST):
                            bank = ps_b if st_ % 2 == 0 else ps_c
                            eng.wait_ge(sm_t, t1 + st_ + 1)
                            bias = qkb_sb[:, 16 + p:17 + p] if vi == 0 else gb_sb[:, p:p + 1]
                            inc(eng.tensor_scalar(vfm[:, 512 * st_:512 * st_ + 512], bank[:, :],
                                                  bias, None, ALU.add, ALU.bypass), sm_v, "v")
                    MK[("v_vfm", p, vi)] = val("v")
                    @blk.sync
                    def _(eng):
                        eng.wait_ge(sm_v, MK[("v_vfm", p, vi)])
                        if vi == 0 and p >= 1:
                            eng.wait_ge(sm_v, MK[("v_scat", p - 1, 1)])
                        if vi == 1:
                            eng.wait_ge(sm_v, MK[("v_scat", p, 0)])
                        eng.wait_ge(sm_d, val("d"))
                        inc(eng.dma_start(out=vraw.ap().rearrange("p (m c) -> p m c", c=128),
                                          in_=vfm[:, :], transpose=True), sm_d, "d", 16)
                        MK[("d_trV" if vi == 0 else "d_trG", p)] = val("d")
                    @blk.vector
                    def _(eng):
                        eng.wait_ge(sm_d, MK[("d_trV" if vi == 0 else "d_trG", p)])
                        if p >= 1 and vi == 0:
                            eng.wait_ge(sm_t, MK[("t_pv_done", p - 1, 3)])
                        dst = (v_tok if vi == 0 else gv_tok).ap().rearrange("p (tt s c) -> p tt s c", s=2, c=65)
                        inc(eng.tensor_copy(dst[:, :, :, 0:64],
                                            vraw.ap().rearrange("p (tt s c) -> p tt s c", s=2, c=64)), sm_v, "v")
                        MK[("v_scat", p, vi)] = val("v")

            def emit_rs(blk, sec, attn_i, ubuf, pv_mark):
                @blk.scalar
                def _(eng):
                    eng.wait_ge(sm_v, MK[("v_evac4", sec)])
                    for hh in range(2):
                        eng.activation(rows_f[64:65, 0:1024], ubuf[64:65, 1024 * hh:1024 * hh + 1024], AF.Ln)
                        inc(eng.activation(rows_b[64:65, T * attn_i + 1024 * hh:T * attn_i + 1024 * hh + 1024],
                                           rows_f[64:65, 0:1024], AF.Exp, scale=-1.0), sm_s, "s")
                MK[("rs", sec)] = val("s")

            def emit_section(blk, p, hi, attn_i, sec, prev_sec, pend_rs_info, pend_tail_info):
                is_std = attn_i == 0
                scale = 0.125 if is_std else 1.0
                vt3 = (v_tok if is_std else gv_tok).ap().rearrange("p (tt s c) -> p tt s c", s=2, c=65)
                ubuf = u_sb[sec % 4]
                e0, m0, t0 = val("s"), val("v"), val("t")
                g0 = GTI[0]

                rs_bump = 2 if pend_rs_info is not None else 0
                exp_no = {}
                for i in range(N_TILES):
                    exp_no[i] = e0 + (i + 1) + (rs_bump if i >= 4 else 0)
                if pend_rs_info is not None:
                    MK[("rs", pend_rs_info[0])] = e0 + 6
                m0g = val("g")
                nm = 0
                for i, (s_, kc, off, diag) in enumerate(TILES):
                    if diag:
                        nm += 1
                    mask_after[g0 + i] = m0g + nm
                SBANKS = [ps_b[0:128, 0:512], ps_c[0:128, 0:512],
                          ps_d[0:128, 0:512], ps_d[0:128, 512:1024]]
                DEPTH = 4
                if pend_tail_info is not None:
                    emit_fdma(blk, *pend_tail_info)
                @blk.tensor
                def _(eng):
                    if is_std:
                        eng.wait_ge(sm_v, MK[("v_scat", p, 0)])
                        eng.wait_ge(sm_v, MK[("v_vfm", p, 1)])
                        if ("v_perm_done", p - 1) in MK:
                            eng.wait_ge(sm_v, MK[("v_perm_done", p - 1)])
                    else:
                        eng.wait_ge(sm_v, MK[("v_scat", p, 1)])
                        eng.wait_ge(sm_d, MK[("d_ext", p)])
                    def S_mm(i):
                        s_, kc, off, diag = TILES[i]
                        bank = SBANKS[i % 4]
                        if is_std:
                            lhsT = kt_p[64 * hi:64 * hi + 64, 128 * kc:128 * kc + 128]
                            rhs = qt_p[64 * hi:64 * hi + 64, 512 * s_ + off:512 * s_ + 512]
                        else:
                            b0 = 64 * hi
                            c0 = T * (p % 2)
                            lhsT = wl_scr[b0:b0 + 64, c0 + 128 * kc:c0 + 128 * kc + 128]
                            rhs = rl_scr[b0:b0 + 64, c0 + 512 * s_ + off:c0 + 512 * s_ + 512]
                        inc(eng.matmul(bank[0:128, off:512], lhsT, rhs, start=True, stop=True), sm_t, "t")
                    for i0 in range(DEPTH):
                        S_mm(i0)
                    for g in range(N_TILES // 4):
                        i0 = 4 * g
                        eng.wait_ge(sm_s, exp_no[i0 + 3])
                        for i in range(i0 + 4, min(i0 + 8, N_TILES)):
                            S_mm(i)
                        dmax = max((i for i in range(i0, i0 + 4) if TILES[i][3]), default=None)
                        if dmax is not None:
                            eng.wait_ge(sm_g, mask_after[g0 + dmax])
                        if prev_sec is not None:
                            for i in range(i0, i0 + 4):
                                if TILES[i][1] == 0:
                                    eng.wait_ge(sm_v, EV[(prev_sec, TILES[i][0])])
                        for i in range(i0, i0 + 4):
                            s_, kc, off, diag = TILES[i]
                            eng.matmul(ps_a[0:65, 512 * s_ + off:512 * s_ + 512],
                                       vt3[:, kc, hi, 0:65],
                                       pt[:, 512 * ((g0 + i) % NSL) + off:512 * ((g0 + i) % NSL) + 512],
                                       start=(kc == 0), stop=(kc == 4 * s_ + 3))
                    inc(eng.nop(), sm_t, "t")
                    MK[("t_pv_done", p, hi * 2 + attn_i)] = val("t")
                    if hi == 1 and attn_i == 1:
                        MK[("t_geo_done", p)] = val("t")
                @blk.scalar
                def _(eng):
                    for g in range(N_TILES // 4):
                        i0 = 4 * g
                        eng.wait_ge(sm_t, t0 + i0 + 4)
                        eng.wait_ge(sm_g, mask_after.get(g0 + i0 + 3 - NSL, 0))
                        for i in range(i0, i0 + 4):
                            s_, kc, off, diag = TILES[i]
                            bank = SBANKS[i % 4]
                            inc(eng.activation(pt[:, 512 * ((g0 + i) % NSL) + off:512 * ((g0 + i) % NSL) + 512],
                                               bank[0:128, off:512], AF.Exp, scale=scale), sm_s, "s")
                        if g == 0 and pend_rs_info is not None:
                            sec_pv, attn_pv, ubuf_pv = pend_rs_info
                            eng.wait_ge(sm_v, MK[("v_evac4", sec_pv)])
                            for hh in range(2):
                                eng.activation(rows_f[64:65, 0:1024],
                                               ubuf_pv[64:65, 1024 * hh:1024 * hh + 1024], AF.Ln)
                                inc(eng.activation(rows_b[64:65, T * attn_pv + 1024 * hh:T * attn_pv + 1024 * hh + 1024],
                                                   rows_f[64:65, 0:1024], AF.Exp, scale=-1.0), sm_s, "s")
                @blk.gpsimd
                def _(eng):
                    for i in range(N_TILES):
                        s_, kc, off, diag = TILES[i]
                        if diag:
                            eng.wait_ge(sm_s, exp_no[i])
                            inc(eng.tensor_tensor(pt[:, 512 * ((g0 + i) % NSL) + off:512 * ((g0 + i) % NSL) + off + 128],
                                                  pt[:, 512 * ((g0 + i) % NSL) + off:512 * ((g0 + i) % NSL) + off + 128],
                                                  ctri[:, :], ALU.mult), sm_g, "g")
                @blk.vector
                def _(eng):
                    if pend_tail_info is not None:
                        tail_vec_part1(eng, *pend_tail_info)
                        tail_vec_part2(eng, *pend_tail_info)
                @blk.vector
                def _(eng):
                    eng.wait_ge(sm_t, MK[("t_pv_done", p, hi * 2 + attn_i)])
                    for s_ in range(NST):
                        inc(eng.tensor_copy(ubuf[0:65, 512 * s_:512 * s_ + 512],
                                            ps_a[0:65, 512 * s_:512 * s_ + 512]), sm_v, "v")
                        EV[(sec, s_)] = val("v")
                MK[("v_evac4", sec)] = val("v")
                GTI[0] = g0 + N_TILES

            def emit_fdma(blk, p_t, hi_t):
                @blk.sync
                def _(eng):
                    eng.wait_ge(sm_s, MK[("rs", p_t * 4 + hi_t * 2 + 1)])
                    eng.wait_ge(sm_v, val("v"))
                    eng.wait_ge(sm_d, val("d"))
                    inc(eng.dma_start(out=rs_hbm.rearrange("(a c) -> a c", a=1),
                                      in_=rows_b.ap()[64:65, 0:2 * T]), sm_d, "d", 16)
                    eng.wait_ge(sm_d, val("d"))
                    inc(eng.dma_start(out=rsb[:, :],
                                      in_=rs_hbm.rearrange("(a c) -> a c", a=1).broadcast_to([64, 2 * T])), sm_d, "d", 16)
                    MK[("d_f", p_t, hi_t)] = val("d")

            def tail_vec_part1(eng, p_t, hi_t):
                sec_std = p_t * 4 + hi_t * 2
                ub_s = u_sb[sec_std % 4]
                dst = comb.ap()[0:64, T * p_t:T * p_t + T] if hi_t == 0 else cmbB.ap()[:, :]
                eng.wait_ge(sm_d, MK[("d_f", p_t, hi_t)])
                inc(eng.tensor_tensor(tmpb[:, :], ub_s[0:64, :], rsb[:, 0:T], ALU.mult), sm_v, "v")
                inc(eng.tensor_tensor(dst, tmpb[:, :], omg_bc[:, :], ALU.mult), sm_v, "v")

            def tail_vec_part2(eng, p_t, hi_t):
                sec_geo = p_t * 4 + hi_t * 2 + 1
                ub_g = u_sb[sec_geo % 4]
                dst = comb.ap()[0:64, T * p_t:T * p_t + T] if hi_t == 0 else cmbB.ap()[:, :]
                inc(eng.tensor_tensor(tmpb[:, :], ub_g[0:64, :], rsb[:, T:2 * T], ALU.mult), sm_v, "v")
                inc(eng.tensor_tensor(tmpb[:, :], tmpb[:, :], g_bc[:, :], ALU.mult), sm_v, "v")
                inc(eng.tensor_tensor(dst, dst, tmpb[:, :], ALU.add), sm_v, "v")
                MK[("v_comb", p_t, hi_t)] = val("v")

            def emit_tail_vector(blk, p_t, hi_t):
                @blk.vector
                def _(eng):
                    tail_vec_part1(eng, p_t, hi_t)
                    tail_vec_part2(eng, p_t, hi_t)

            def emit_permuteB(blk, p_t):
                t0 = val("t")
                v0p = val("v")
                @blk.tensor
                def _(eng):
                    eng.wait_ge(sm_v, MK[("v_comb", p_t, 1)])
                    eng.wait_ge(sm_v, MK[("v_vfm", p_t + 1, 1)] if ("v_vfm", p_t + 1, 1) in MK else 0)
                    for q in range(4):
                        if q == 2:
                            eng.wait_ge(sm_v, v0p + 1)
                        bank = ps_b if q % 2 == 0 else ps_c
                        inc(eng.matmul(bank[64:128, 0:512], cident[:, :],
                                       cmbB[:, 512 * q:512 * q + 512], start=True, stop=True), sm_t, "t")
                @blk.vector
                def _(eng):
                    for r in range(2):
                        eng.wait_ge(sm_t, t0 + 2 * (r + 1))
                        for q2 in range(2):
                            inc(eng.tensor_copy(comb[64:128, T * p_t + 1024 * r + 512 * q2:T * p_t + 1024 * r + 512 * q2 + 512],
                                                (ps_b if q2 == 0 else ps_c)[64:128, 0:512]), sm_v, "v")
                MK[("v_perm_done", p_t)] = val("v")

            with nc.Block() as blk:
                emit_loads(blk, 0)
                emit_cast_qk(blk, 0)
                emit_loads_vg(blk, 0)
                emit_cast_vg(blk, 0)
                emit_extract(blk, 0)
                sec = 0
                prev_sec = None
                pend_rs = None
                pend_tail = None
                for p in range(NP):
                    emit_proj(blk, p)
                    if p + 1 < NP:
                        emit_loads(blk, p + 1)
                        emit_extract(blk, p + 1)
                    if p >= 1 and pend_rs is not None:
                        emit_rs(blk, pend_rs[0], pend_rs[1], pend_rs[2],
                                ("t_pv_done", p - 1, 3))
                        pend_rs = None
                        emit_fdma(blk, p - 1, 1)
                        emit_tail_vector(blk, p - 1, 1)
                        emit_permuteB(blk, p - 1)
                    if p + 1 < NP:
                        emit_cast_qk(blk, p + 1)
                        emit_loads_vg(blk, p + 1)
                        emit_cast_vg(blk, p + 1)
                    for si, (hi, attn_i) in enumerate([(0, 0), (0, 1), (1, 0), (1, 1)]):
                        if sec >= n_secs:
                            sec += 1
                            continue
                        emit_section(blk, p, hi, attn_i, sec, prev_sec,
                                     pend_rs, pend_tail)
                        pend_tail = None
                        pend_rs = (sec, attn_i, u_sb[sec % 4])
                        prev_sec = sec
                        sec += 1
                        if si == 1:
                            pend_tail = (p, 0)
                if pend_rs is not None and sec - 1 < n_secs and (NP - 1) * 4 + 3 < n_secs and not skip_final:
                    if final_lvl >= 1:
                        emit_rs(blk, pend_rs[0], pend_rs[1], pend_rs[2], ("t_pv_done", NP - 1, 3))
                        pend_rs = None
                    if final_lvl >= 2:
                        emit_fdma(blk, NP - 1, 1)
                    if final_lvl >= 3:
                        emit_tail_vector(blk, NP - 1, 1)
                    if final_lvl >= 4:
                        emit_permuteB(blk, NP - 1)
        S_pre.close()

        with ExitStack() as S_ao:
            big_f = sbuf("big_f", [128, NTT * 1024], F32, S_ao)
            wbf = sbuf("wbf", [128, NDC * 256], BF, S_ao)
            with ExitStack() as S4:
                wout = sbuf("wout", [128, NDC * D], BF, S4)
                ob_bc = sbuf("ob_bc", [128, D], F32, S4)
                with nc.Block() as blk:
                    @blk.sync
                    def _(eng):
                        inc(eng.dma_start(out=wstage[0:1, 0:1024], in_=out_b.rearrange("(a d) -> a d", a=1)), sm_d, "d", 16)
                    @blk.vector
                    def _(eng):
                        w3(eng, "d")
                        inc(eng.tensor_copy(wbf[0:1, 0:D], wstage[0:1, 0:1024]), sm_v, "v")
                        inc(eng.tensor_tensor(wbf[0:1, D:2 * D], wstage[0:1, 0:1024], wbf[0:1, 0:D], ALU.subtract), sm_v, "v")
                    @blk.tensor
                    def _(eng):
                        w3(eng, "v")
                        for i in range(2):
                            eng.matmul(ps_d[:, 512 * i:512 * i + 512], cones[0:1, 0:128],
                                       wbf[0:1, 512 * i:512 * i + 512], start=True, stop=False)
                            inc(eng.matmul(ps_d[:, 512 * i:512 * i + 512], cones[0:1, 0:128],
                                           wbf[0:1, D + 512 * i:D + 512 * i + 512], start=False, stop=True), sm_t, "t")
                    @blk.scalar
                    def _(eng):
                        w3(eng, "t")
                        inc(eng.activation(ob_bc[:, :], ps_d[:, 0:1024], AF.Copy), sm_s, "s")
                for dc in range(NDC):
                    with nc.Block() as blk:
                        @blk.sync
                        def _(eng):
                            inc(eng.dma_start(out=wstage[:, 0:1024],
                                              in_=out_w[128 * dc:128 * dc + 128, :]), sm_d, "d", 16)
                        @blk.vector
                        def _(eng):
                            w3(eng, "d")
                            inc(eng.tensor_copy(wout[:, D * dc:D * dc + D], wstage[:, 0:1024]), sm_v, "v")
                with nc.Block() as blk:
                    t0, v0 = val("t"), val("v")
                    @blk.tensor
                    def _(eng):
                        w3(eng, "v")
                        j = 0
                        for tt in range(NTT):
                            for os_ in range(2):
                                bank = ps_b if j % 2 == 0 else ps_c
                                if j >= 2:
                                    eng.wait_ge(sm_v, v0 + j - 1)
                                for c in range(8):
                                    inc(eng.matmul(bank[:, :], comb[:, T * c + 128 * tt:T * c + 128 * tt + 128],
                                                   wout[:, D * c + 512 * os_:D * c + 512 * os_ + 512],
                                                   start=(c == 0), stop=(c == 7)),
                                        sm_t, "t", 1 if c == 7 else 0)
                                j += 1
                    @blk.vector
                    def _(eng):
                        j = 0
                        for tt in range(NTT):
                            for os_ in range(2):
                                bank = ps_b if j % 2 == 0 else ps_c
                                eng.wait_ge(sm_t, t0 + j + 1)
                                inc(eng.tensor_tensor(big_f[:, 1024 * tt + 512 * os_:1024 * tt + 512 * os_ + 512],
                                                      bank[:, :], ob_bc[:, 512 * os_:512 * os_ + 512], ALU.add), sm_v, "v")
                                j += 1
            dump("comb", comb.ap(), BF)
            dump("attn_out", big_f.ap(), F32)
            S_comb.close()

            with ExitStack() as S5:
                xstage = sbuf("xstage", [128, 2 * 1024], F32, S5)
                with nc.Block() as blk:
                    emit_ln(blk, big_f.ap(), big_f.ap(), ("v",))
                with nc.Block() as blk:
                    d0, v0 = val("d"), val("v")
                    @blk.sync
                    def _(eng):
                        for tt in range(NTT):
                            if tt >= 2:
                                eng.wait_ge(sm_v, v0 + tt - 1)
                            inc(eng.dma_start(out=xstage[:, 1024 * (tt % 2):1024 * (tt % 2) + 1024],
                                              in_=x_h[128 * tt:128 * tt + 128, :]), sm_d, "d", 16)
                    @blk.vector
                    def _(eng):
                        for tt in range(NTT):
                            eng.wait_ge(sm_d, d0 + 16 * (tt + 1))
                            inc(eng.tensor_tensor(big_f[:, 1024 * tt:1024 * tt + 1024],
                                                  big_f[:, 1024 * tt:1024 * tt + 1024],
                                                  xstage[:, 1024 * (tt % 2):1024 * (tt % 2) + 1024], ALU.add), sm_v, "v")
                dump("x2", big_f.ap(), F32)
                with nc.Block() as blk:
                    @blk.gpsimd
                    def _(eng):
                        w3(eng, "v")
                        inc(eng.dma_start(out=x2s.rearrange("(tt p) d -> p tt d", p=128),
                                          in_=big_f.ap().rearrange("p (tt d) -> p tt d", d=1024)), sm_g2, "g2", 16)
                        eng.wait_ge(sm_g2, val("g2"))
                xln2_tok = sbuf("xln2_tok", [128, NTT * 1024], BF, S5)
                with nc.Block() as blk:
                    emit_ln(blk, big_f.ap(), xln2_tok.ap(), ("v",))
                with nc.Block() as blk:
                    @blk.sync
                    def _(eng):
                        w3(eng, "v")
                        xf = xln_fm.ap().rearrange("p (dc t) -> p dc t", t=T)
                        for tt in range(NTT):
                            inc(eng.dma_start(out=xf[:, :, 128 * tt:128 * tt + 128],
                                              in_=xln2_tok[:, 1024 * tt:1024 * tt + 1024],
                                              transpose=True), sm_d, "d", 16)
                        eng.wait_ge(sm_d, val("d"))

            with ExitStack() as S6:
                wt_fm = sbuf("wt_fm", [128, NDC * D], BF, S6)
                pooled = sbuf("pooled", [64, T], BF, S6)
                for oc in range(NDC):
                    with nc.Block() as blk:
                        @blk.sync
                        def _(eng):
                            inc(eng.dma_start(out=wstage[:, 0:1024],
                                              in_=ffn_w[128 * oc:128 * oc + 128, :]), sm_d, "d", 16)
                        @blk.vector
                        def _(eng):
                            w3(eng, "d")
                            inc(eng.tensor_copy(wbf[:, 0:1024], wstage[:, 0:1024]), sm_v, "v")
                        @blk.sync
                        def _(eng):
                            w3(eng, "v")
                            wt3 = wt_fm.ap().rearrange("p (dc oc c) -> p dc oc c", oc=8, c=128)
                            inc(eng.dma_start(out=wt3[:, :, oc, :], in_=wbf[:, 0:1024], transpose=True), sm_d, "d", 16)
                            eng.wait_ge(sm_d, val("d"))
                with nc.Block() as blk:
                    t0, v0 = val("t"), val("v")
                    @blk.tensor
                    def _(eng):
                        for st_ in range(NST):
                            if st_ >= 1:
                                eng.wait_ge(sm_v, v0 + st_)
                            for dc in range(NDC):
                                inc(eng.matmul(ps_d[0:64, 0:512], cpool[:, :],
                                               xln_fm[:, T * dc + 512 * st_:T * dc + 512 * st_ + 512],
                                               start=(dc == 0), stop=(dc == NDC - 1)),
                                    sm_t, "t", 1 if dc == NDC - 1 else 0)
                    @blk.vector
                    def _(eng):
                        for st_ in range(NST):
                            eng.wait_ge(sm_t, t0 + st_ + 1)
                            inc(eng.tensor_copy(pooled[:, 512 * st_:512 * st_ + 512], ps_d[0:64, 0:512]), sm_v, "v")
                with nc.Block() as blk:
                    t0, v0 = val("t"), val("v")
                    @blk.tensor
                    def _(eng):
                        w3(eng, "v")
                        j = 0
                        for tt in range(NTT):
                            for os_ in range(2):
                                bank = ps_b if j % 2 == 0 else ps_c
                                if j >= 2:
                                    eng.wait_ge(sm_v, v0 + j - 1)
                                for dc in range(NDC):
                                    eng.matmul(bank[:, :], xln_fm[:, T * dc + 128 * tt:T * dc + 128 * tt + 128],
                                               wt_fm[:, 1024 * dc + 512 * os_:1024 * dc + 512 * os_ + 512],
                                               start=(dc == 0), stop=False)
                                inc(eng.matmul(bank[:, :], pooled[:, 128 * tt:128 * tt + 128],
                                               disha_bf[:, 512 * os_:512 * os_ + 512],
                                               start=False, stop=True), sm_t, "t")
                                j += 1
                    @blk.vector
                    def _(eng):
                        j = 0
                        for tt in range(NTT):
                            for os_ in range(2):
                                bank = ps_b if j % 2 == 0 else ps_c
                                eng.wait_ge(sm_t, t0 + j + 1)
                                inc(eng.tensor_copy(big_f[:, 1024 * tt + 512 * os_:1024 * tt + 512 * os_ + 512],
                                                    bank[:, :]), sm_v, "v")
                                j += 1

            dump("ffn_raw", big_f.ap(), F32)

            with ExitStack() as S7:
                xstage = sbuf("xstage7", [128, 2 * 1024], F32, S7)
                with nc.Block() as blk:
                    emit_ln(blk, big_f.ap(), big_f.ap(), ("v",))
                with nc.Block() as blk:
                    d0, v0 = val("d"), val("v")
                    @blk.sync
                    def _(eng):
                        for tt in range(NTT):
                            if tt >= 2:
                                eng.wait_ge(sm_v, v0 + tt - 1)
                            inc(eng.dma_start(out=xstage[:, 1024 * (tt % 2):1024 * (tt % 2) + 1024],
                                              in_=x2s[128 * tt:128 * tt + 128, :]), sm_d, "d", 16)
                    @blk.vector
                    def _(eng):
                        for tt in range(NTT):
                            eng.wait_ge(sm_d, d0 + 16 * (tt + 1))
                            inc(eng.tensor_tensor(big_f[:, 1024 * tt:1024 * tt + 1024],
                                                  big_f[:, 1024 * tt:1024 * tt + 1024],
                                                  xstage[:, 1024 * (tt % 2):1024 * (tt % 2) + 1024], ALU.add), sm_v, "v")
                with nc.Block() as blk:
                    @blk.sync
                    def _(eng):
                        w3(eng, "v")
                        inc(eng.dma_start(out=y_h.rearrange("(tt p) d -> p tt d", p=128),
                                          in_=big_f.ap().rearrange("p (tt d) -> p tt d", d=1024)), sm_d, "d", 16)
                        eng.wait_ge(sm_d, val("d"))
    return nc


_NC_CACHE = [None]


def kernel(**inputs):
    consts = host_consts()
    base = {
        "qkv_w": inputs["L0_attn_qkv_w"], "qkv_b": inputs["L0_attn_qkv_b"],
        "w1_write": inputs["L0_attn_w1_write"], "w2_write": inputs["L0_attn_w2_write"],
        "w1_read": inputs["L0_attn_w1_read"], "w2_read": inputs["L0_attn_w2_read"],
        "geo_w": inputs["L0_attn_geo_w"], "geo_b": inputs["L0_attn_geo_b"],
        "gate_w": inputs["L0_attn_gate_w"], "gate_b": inputs["L0_attn_gate_b"],
        "inc_scale": inputs["L0_attn_inc_scale"],
        "out_w": inputs["L0_attn_out_w"], "out_b": inputs["L0_attn_out_b"],
        "ffn_weight": inputs["L0_ffn_weight"], "disha": inputs["L0_ffn_disha"],
    }
    base = {k: np.ascontiguousarray(np.asarray(v, dtype=np.float32)) for k, v in base.items()}
    for k, v in consts.items():
        base["c_" + k] = np.ascontiguousarray(v)
    x_full = np.asarray(inputs["L0_attn_x"], dtype=np.float32)
    in_maps = []
    for b in range(8):
        m = dict(base)
        m["x"] = np.ascontiguousarray(x_full[b])
        in_maps.append(m)
    if _NC_CACHE[0] is None:
        _NC_CACHE[0] = build_nc()
    res = run_bass_kernel_spmd(_NC_CACHE[0], in_maps, core_ids=list(range(8)))
    return np.stack([np.asarray(r["y"], dtype=np.float32) for r in res.results], axis=0)


# version 14 (final)
# speedup vs baseline: 1.3622x; 1.1487x over previous
import numpy as np
import ml_dtypes
from contextlib import ExitStack

import concourse.bass as bass
import concourse.mybir as mybir
from concourse.bass_utils import run_bass_kernel_spmd

BF16 = ml_dtypes.bfloat16
F32 = mybir.dt.float32
BF = mybir.dt.bfloat16
AF = mybir.ActivationFunctionType
ALU = mybir.AluOpType

T = 2048
D = 1024
H = 16
NTT = 16
NDC = 8
NST = 4
PAIRS4 = [(0, 1), (0, 2), (0, 3), (1, 2), (1, 3), (2, 3)]
SIGNS = [1.0, -1.0, 1.0, 1.0, -1.0, 1.0]
PAIRS_READ = [PAIRS4[5 - p] for p in range(6)]

TILES = []
for _s in range(NST):
    for _kc in range(4 * _s + 4):
        _diag = _kc >= 4 * _s
        TILES.append((_s, _kc, 128 * (_kc - 4 * _s) if _diag else 0, _diag))
N_TILES = len(TILES)


def host_consts():
    c = {}
    E = np.zeros((4, 128, 96), dtype=np.float32)
    for h in range(H):
        for p in range(6):
            col = 6 * h + p
            E[0, 4 * h + PAIRS4[p][0], col] = 1.0
            E[1, 4 * h + PAIRS4[p][1], col] = 1.0
            E[2, 4 * h + PAIRS_READ[p][0], col] = 1.0
            E[3, 4 * h + PAIRS_READ[p][1], col] = 1.0
    for k in range(4):
        E[k, 64:128, :] = E[k, 0:64, :]
    c['E'] = np.concatenate([E[0], E[1], E[2], E[3]], axis=1)
    coll = np.zeros((96, 16), dtype=np.float32)
    for h in range(H):
        coll[6 * h:6 * h + 6, h] = 1.0
    c['collect'] = coll
    spw = np.zeros((48, 96), dtype=np.float32)
    spr = np.zeros((48, 96), dtype=np.float32)
    for h in range(H):
        for p in range(6):
            spw[h, 6 * h + p] = 1.0
            spr[h, 6 * h + p] = SIGNS[p]
    spw[32:48] = spw[0:16]
    spr[32:48] = spr[0:16]
    c['spread_w'] = spw
    c['spread_r'] = spr
    c['tri'] = np.triu(np.ones((128, 128), dtype=np.float32))
    pool = np.zeros((128, 64), dtype=np.float32)
    for r in range(64):
        pool[r, r] = 1.0
        pool[64 + r, r] = 1.0
    c['pool'] = pool
    c['ident64'] = np.eye(64, dtype=np.float32)
    return c


def build_nc(debug=False, np_pairs=8, n_secs=999, skip_final=False, final_lvl=4):
    nc = bass.Bass("TRN2", target_bir_lowering=False)
    dbg_list = []
    dI = {}

    def din(name, shape):
        dI[name] = nc.dram_tensor(name, list(shape), F32, kind="ExternalInput").ap()
        return dI[name]

    x_h = din("x", (T, D))
    qkv_w = din("qkv_w", (D, 3 * D))
    qkv_b = din("qkv_b", (3 * D,))
    for nm in ["w1_write", "w2_write", "w1_read", "w2_read"]:
        din(nm, (D, 64))
    geo_w = din("geo_w", (D, D))
    geo_b = din("geo_b", (D,))
    gate_w = din("gate_w", (D, 16))
    gate_b = din("gate_b", (16,))
    inc_scale = din("inc_scale", (16,))
    out_w = din("out_w", (D, D))
    out_b = din("out_b", (D,))
    ffn_w = din("ffn_weight", (D, D))
    disha = din("disha", (64, D))
    for k, v in host_consts().items():
        din("c_" + k, v.shape)
    y_h = nc.dram_tensor("y", [T, D], F32, kind="ExternalOutput").ap()
    x2s = nc.dram_tensor("x2s", [T, D], F32).ap()
    rs_hbm = nc.dram_tensor("rs_hbm", [2 * T], BF).ap()

    ctr = {}

    def inc(instr, sem, name, k=1):
        if k > 0:
            instr.then_inc(sem, k)
            ctr[name] = ctr.get(name, 0) + k
        return ctr.get(name, 0)

    def val(name):
        return ctr.get(name, 0)

    with ExitStack() as ST:
        def sbuf(name, shape, dt, st=None, side=None):
            kw = {} if side is None else {"side": side}
            return (st or ST).enter_context(nc.sbuf_tensor(name, shape, dt, **kw))

        xln_fm = sbuf("xln_fm", [128, NDC * T], BF)
        w_small = sbuf("w_small", [128, NDC * 320], BF)
        qkb_sb = sbuf("qkb_sb", [128, 24], F32)
        gb_sb = sbuf("gb_sb", [128, 8], F32)
        gateb = sbuf("gateb", [16, 1], F32)
        incs = sbuf("incs", [48, 1], F32)
        cE = sbuf("cE", [128, 384], BF)
        ccoll = sbuf("ccoll", [96, 16], BF)
        cspw = sbuf("cspw", [48, 96], BF)
        cspr = sbuf("cspr", [48, 96], BF)
        ctri = sbuf("ctri", [128, 128], BF)
        cpool = sbuf("cpool", [128, 64], BF)
        cident = sbuf("cident", [64, 64], BF)
        cones = sbuf("cones", [128, 128], BF)
        disha_bf = sbuf("disha_bf", [64, D], BF)
        wstage = sbuf("wstage", [128, NDC * 256], F32)
        stats = sbuf("stats", [128, NTT * 16], F32)

        S_pre = ExitStack()
        rows_f = sbuf("rows_f", [128, 1024], F32, S_pre)
        rows_b = sbuf("rows_b", [128, 2 * T], BF, S_pre)
        omg_bc = sbuf("omg_bc", [64, T], BF, S_pre)
        g_bc = sbuf("g_bc", [64, T], BF, S_pre)
        wl_buf = sbuf("wl_buf", [96, T], BF, S_pre)
        rl_buf = sbuf("rl_buf", [96, T], BF, S_pre)
        S_comb = ExitStack()

        ps_a = ST.enter_context(nc.psum_tensor("ps_a", [128, 2048], F32))
        ps_b = ST.enter_context(nc.psum_tensor("ps_b", [128, 512], F32))
        ps_c = ST.enter_context(nc.psum_tensor("ps_c", [128, 512], F32))
        ps_d = ST.enter_context(nc.psum_tensor("ps_d", [128, 1024], F32))

        sm_d = ST.enter_context(nc.semaphore("sm_d"))
        sm_g = ST.enter_context(nc.semaphore("sm_g"))
        sm_t = ST.enter_context(nc.semaphore("sm_t"))
        sm_s = ST.enter_context(nc.semaphore("sm_s"))
        sm_v = ST.enter_context(nc.semaphore("sm_v"))
        sm_f = ST.enter_context(nc.semaphore("sm_f"))
        sm_g2 = ST.enter_context(nc.semaphore("sm_g2"))
        sm_e = ST.enter_context(nc.semaphore("sm_e"))
        SEM = {"d": sm_d, "g": sm_g, "t": sm_t, "s": sm_s, "v": sm_v}

        def w3(eng, *names):
            for name in names:
                eng.wait_ge(SEM[name], val(name))

        def dump(name, src_ap, dt):
            if not debug:
                return
            p, w = src_ap.shape[0], src_ap.shape[1]
            od = nc.dram_tensor("dbg_" + name, [p, w], dt, kind="ExternalOutput").ap()
            with nc.Block() as blk:
                @blk.sync
                def _(eng):
                    w3(eng, "v", "s", "t", "g")
                    inc(eng.dma_start(out=od, in_=src_ap), sm_d, "d", 16)
                    eng.wait_ge(sm_d, val("d"))

        with ExitStack() as S0:
            cst = sbuf("cst", [128, 1024], F32, S0)
            with nc.Block() as blk:
                @blk.sync
                def _(eng):
                    ws3 = wstage.ap().rearrange("p (dc c) -> p dc c", c=256)
                    for i, nm in enumerate(["w1_write", "w2_write", "w1_read", "w2_read"]):
                        inc(eng.dma_start(out=ws3[:, :, 64 * i:64 * i + 64],
                                          in_=dI[nm].rearrange("(dc p) c -> p dc c", p=128)), sm_d, "d", 16)
                    inc(eng.dma_start(out=qkb_sb[:, :], in_=qkv_b.rearrange("(c p) -> p c", p=128), allow_slow_non_contiguous=True), sm_d, "d", 16)
                    inc(eng.dma_start(out=gb_sb[:, :], in_=geo_b.rearrange("(c p) -> p c", p=128), allow_slow_non_contiguous=True), sm_d, "d", 16)
                    inc(eng.dma_start(out=gateb[:, :], in_=gate_b.rearrange("(a b) -> a b", b=1)), sm_d, "d", 16)
                    inc(eng.dma_start(out=incs[0:16, :], in_=inc_scale.rearrange("(a b) -> a b", b=1)), sm_d, "d", 16)
                    inc(eng.dma_start(out=incs[32:48, :], in_=inc_scale.rearrange("(a b) -> a b", b=1)), sm_d, "d", 16)
                @blk.vector
                def _(eng):
                    w3(eng, "d")
                    for dc in range(NDC):
                        inc(eng.tensor_copy(w_small[:, 320 * dc:320 * dc + 256],
                                            wstage[:, 256 * dc:256 * dc + 256]), sm_v, "v")
                @blk.gpsimd
                def _(eng):
                    eng.memset(cones[:, :], 1.0)
                    inc(eng.engine_nop(), sm_g, "g")
            with nc.Block() as blk:
                @blk.sync
                def _(eng):
                    ws3 = wstage.ap().rearrange("p (dc c) -> p dc c", c=256)
                    inc(eng.dma_start(out=ws3[:, :, 0:16],
                                      in_=gate_w.rearrange("(dc p) c -> p dc c", p=128)), sm_d, "d", 16)
                    inc(eng.dma_start(out=cst[0:128, 0:384], in_=dI["c_E"]), sm_d, "d", 16)
                    inc(eng.dma_start(out=cst[0:96, 384:400], in_=dI["c_collect"]), sm_d, "d", 16)
                    inc(eng.dma_start(out=cst[0:128, 400:528], in_=dI["c_tri"]), sm_d, "d", 16)
                    inc(eng.dma_start(out=cst[0:128, 528:592], in_=dI["c_pool"]), sm_d, "d", 16)
                    inc(eng.dma_start(out=cst[0:64, 592:656], in_=dI["c_ident64"]), sm_d, "d", 16)
                    inc(eng.dma_start(out=cst[0:48, 656:752], in_=dI["c_spread_w"]), sm_d, "d", 16)
                    inc(eng.dma_start(out=cst[0:48, 752:848], in_=dI["c_spread_r"]), sm_d, "d", 16)
                @blk.vector
                def _(eng):
                    w3(eng, "d")
                    for dc in range(NDC):
                        inc(eng.tensor_copy(w_small[:, 320 * dc + 256:320 * dc + 272],
                                            wstage[:, 256 * dc:256 * dc + 16]), sm_v, "v")
                    inc(eng.tensor_copy(cE[:, :], cst[0:128, 0:384]), sm_v, "v")
                    inc(eng.tensor_copy(ccoll[:, :], cst[0:96, 384:400]), sm_v, "v")
                    inc(eng.tensor_copy(ctri[:, :], cst[0:128, 400:528]), sm_v, "v")
                    inc(eng.tensor_copy(cpool[:, :], cst[0:128, 528:592]), sm_v, "v")
                    inc(eng.tensor_copy(cident[:, :], cst[0:64, 592:656]), sm_v, "v")
                    inc(eng.tensor_copy(cspw[:, :], cst[0:48, 656:752]), sm_v, "v")
                    inc(eng.tensor_scalar(cspr[0:16, :], cst[0:16, 752:848], incs[0:16, :], None, ALU.mult, ALU.bypass), sm_v, "v")
                    inc(eng.tensor_scalar(cspr[32:48, :], cst[32:48, 752:848], incs[32:48, :], None, ALU.mult, ALU.bypass), sm_v, "v")
            with nc.Block() as blk:
                @blk.sync
                def _(eng):
                    inc(eng.dma_start(out=cst[0:64, 0:D], in_=disha), sm_d, "d", 16)
                @blk.vector
                def _(eng):
                    w3(eng, "d")
                    inc(eng.tensor_copy(disha_bf[:, :], cst[0:64, 0:D]), sm_v, "v")

        def emit_ln(blk, src, dst, waits):
            @blk.vector
            def _(eng):
                w3(eng, *waits)
                for tt in range(NTT):
                    c0 = 16 * tt
                    eng.bn_stats(stats[:, c0:c0 + 6], src[:, 1024 * tt:1024 * tt + 512])
                    eng.bn_stats(stats[:, c0 + 6:c0 + 12], src[:, 1024 * tt + 512:1024 * tt + 1024])
                for tt in range(NTT):
                    c0 = 16 * tt
                    eng.bn_aggr(stats[:, c0 + 12:c0 + 14], stats[:, c0:c0 + 12])
                for tt in range(NTT):
                    c0 = 16 * tt
                    inc(eng.tensor_scalar(stats[:, c0 + 13:c0 + 14], stats[:, c0 + 13:c0 + 14],
                                          1e-5, None, ALU.add, ALU.bypass), sm_v, "v")
            @blk.scalar
            def _(eng):
                w3(eng, "v")
                for tt in range(NTT):
                    c0 = 16 * tt
                    eng.activation(stats[:, c0 + 13:c0 + 14], stats[:, c0 + 13:c0 + 14], AF.Ln)
                    inc(eng.activation(stats[:, c0 + 14:c0 + 15], stats[:, c0 + 13:c0 + 14],
                                       AF.Exp, scale=-0.5), sm_s, "s")
            @blk.vector
            def _(eng):
                w3(eng, "s")
                for tt in range(NTT):
                    c0 = 16 * tt
                    inc(eng.tensor_scalar(dst[:, 1024 * tt:1024 * tt + 1024],
                                          src[:, 1024 * tt:1024 * tt + 1024],
                                          stats[:, c0 + 12:c0 + 13], stats[:, c0 + 14:c0 + 15],
                                          ALU.subtract, ALU.mult), sm_v, "v")

        def load_cast(blk, jobs):
            @blk.sync
            def _(eng):
                off = 0
                for src, c0, w in jobs:
                    ws3 = wstage.ap().rearrange("p (dc c) -> p dc c", c=256)
                    inc(eng.dma_start(out=ws3[:, :, off:off + w], in_=src), sm_d, "d", 16)
                    off += w
            @blk.vector
            def _(eng):
                w3(eng, "d")
                off = 0
                for src, c0, w in jobs:
                    for dc in range(NDC):
                        inc(eng.tensor_copy(wbf[:, 256 * dc + c0:256 * dc + c0 + w],
                                            wstage[:, 256 * dc + off:256 * dc + off + w]), sm_v, "v")
                    off += w

        with ExitStack() as S1:
            x_res = sbuf("x_res", [128, NTT * 1024], F32, S1)
            xln_tok = sbuf("xln_tok", [128, NTT * 1024], BF, S1)
            with nc.Block() as blk:
                @blk.sync
                def _(eng):
                    inc(eng.dma_start(out=x_res.ap().rearrange("p (tt d) -> p tt d", d=1024),
                                      in_=x_h.rearrange("(tt p) d -> p tt d", p=128)), sm_d, "d", 16)
            with nc.Block() as blk:
                emit_ln(blk, x_res.ap(), xln_tok.ap(), ("d",))
            with nc.Block() as blk:
                @blk.sync
                def _(eng):
                    w3(eng, "v")
                    xf = xln_fm.ap().rearrange("p (dc t) -> p dc t", t=T)
                    for tt in range(NTT):
                        inc(eng.dma_start(out=xf[:, :, 128 * tt:128 * tt + 128],
                                          in_=xln_tok[:, 1024 * tt:1024 * tt + 1024],
                                          transpose=True), sm_d, "d", 16)
                    eng.wait_ge(sm_d, val("d"))
        dump("xln_fm", xln_fm.ap(), BF)

        with ExitStack() as S2:
            pw_w = sbuf("pw_w", [128, T], BF, S2)
            pw_r = sbuf("pw_r", [128, T], BF, S2)
            gate_sig = sbuf("gate_sig", [16, T], BF, S2)
            gw1xi = sbuf("gw1xi", [96, T + 4], BF, S2)
            gw1xj = sbuf("gw1xj", [96, T + 4], BF, S2)
            r1xi = sbuf("r1xi", [96, 512], BF, S2)
            r1xj = sbuf("r1xj", [96, 512], BF, S2)
            m1s = sbuf("m1s", [96, 512], F32, S2)
            m2s = sbuf("m2s", [96, 512], F32, S2)
            Lwsq = sbuf("Lwsq", [96, 512], BF, S2)
            Lrsq = sbuf("Lrsq", [96, 512], BF, S2)
            Lw = sbuf("Lw", [96, 512], BF, S2)
            Lr = sbuf("Lr", [96, 512], BF, S2)
            inv_sb = sbuf("inv_sb", [48, T], BF, S2)

            with nc.Block() as blk:
                t0, s0 = val("t"), val("s")
                @blk.tensor
                def _(eng):
                    for st_ in range(NST):
                        bank = ps_b if st_ % 2 == 0 else ps_c
                        if st_ >= 2:
                            eng.wait_ge(sm_s, s0 + st_ - 1)
                        for dc in range(NDC):
                            inc(eng.matmul(bank[0:16, :], w_small[:, 320 * dc + 256:320 * dc + 272],
                                           xln_fm[:, T * dc + 512 * st_:T * dc + 512 * st_ + 512],
                                           start=(dc == 0), stop=(dc == NDC - 1)),
                                sm_t, "t", 1 if dc == NDC - 1 else 0)
                @blk.scalar
                def _(eng):
                    for st_ in range(NST):
                        eng.wait_ge(sm_t, t0 + st_ + 1)
                        inc(eng.activation(gate_sig[:, 512 * st_:512 * st_ + 512],
                                           (ps_b if st_ % 2 == 0 else ps_c)[0:16, :],
                                           AF.Sigmoid, bias=gateb[0:16, :]), sm_s, "s")
            with nc.Block() as blk:
                @blk.tensor
                def _(eng):
                    w3(eng, "s")
                    for st_ in range(NST):
                        inc(eng.matmul(ps_a[0:1, 512 * st_:512 * st_ + 512], cones[0:16, 0:1],
                                       gate_sig[:, 512 * st_:512 * st_ + 512], start=True, stop=True), sm_t, "t")
                @blk.scalar
                def _(eng):
                    w3(eng, "t")
                    inc(eng.activation(rows_f[0:1, 0:1024], ps_a[0:1, 0:1024], AF.Copy, scale=1.0 / 16.0), sm_s, "s")
                @blk.vector
                def _(eng):
                    w3(eng, "s")
                    inc(eng.tensor_copy(rows_b[0:1, 0:1024], rows_f[0:1, 0:1024]), sm_v, "v")
                    inc(eng.tensor_tensor(rows_b[0:1, T:T + 1024], rows_f[0:1, 0:1024], rows_b[0:1, 0:1024], ALU.subtract), sm_v, "v")
                @blk.scalar
                def _(eng):
                    w3(eng, "v")
                    inc(eng.activation(rows_f[0:1, 0:1024], ps_a[0:1, 1024:2048], AF.Copy, scale=1.0 / 16.0), sm_s, "s")
                @blk.vector
                def _(eng):
                    w3(eng, "s")
                    inc(eng.tensor_copy(rows_b[0:1, 1024:2048], rows_f[0:1, 0:1024]), sm_v, "v")
                    inc(eng.tensor_tensor(rows_b[0:1, T + 1024:2 * T], rows_f[0:1, 0:1024], rows_b[0:1, 1024:2048], ALU.subtract), sm_v, "v")
            for half in range(2):
                c0 = 1024 * half
                with nc.Block() as blk:
                    @blk.tensor
                    def _(eng):
                        w3(eng, "v")
                        for q in range(2):
                            eng.matmul(ps_d[0:64, 512 * q:512 * q + 512], cones[0:1, 0:64],
                                       rows_b[0:1, c0 + 512 * q:c0 + 512 * q + 512], start=True, stop=False)
                            inc(eng.matmul(ps_d[0:64, 512 * q:512 * q + 512], cones[0:1, 0:64],
                                           rows_b[0:1, T + c0 + 512 * q:T + c0 + 512 * q + 512], start=False, stop=True), sm_t, "t")
                    @blk.vector
                    def _(eng):
                        w3(eng, "t")
                        inc(eng.tensor_copy(g_bc[:, c0:c0 + 1024], ps_d[0:64, :]), sm_v, "v")
                        inc(eng.tensor_scalar(omg_bc[:, c0:c0 + 1024], ps_d[0:64, :], 1.0, -1.0, ALU.subtract, ALU.mult), sm_v, "v")

            dump("g_bc", g_bc.ap(), F32)
            dump("omg_bc", omg_bc.ap(), F32)
            with nc.Block() as blk:
                t0, v0 = val("t"), val("v")
                @blk.tensor
                def _(eng):
                    j = 0
                    for which in (0, 128):
                        for st_ in range(NST):
                            bank = ps_b if j % 2 == 0 else ps_c
                            if j >= 2:
                                eng.wait_ge(sm_v, v0 + j - 1)
                            for dc in range(NDC):
                                inc(eng.matmul(bank[:, :], w_small[:, 320 * dc + which:320 * dc + which + 128],
                                               xln_fm[:, T * dc + 512 * st_:T * dc + 512 * st_ + 512],
                                               start=(dc == 0), stop=(dc == NDC - 1)),
                                    sm_t, "t", 1 if dc == NDC - 1 else 0)
                            j += 1
                @blk.vector
                def _(eng):
                    j = 0
                    for which, dst in [(0, pw_w), (128, pw_r)]:
                        for st_ in range(NST):
                            bank = ps_b if j % 2 == 0 else ps_c
                            eng.wait_ge(sm_t, t0 + j + 1)
                            inc(eng.tensor_copy(dst[:, 512 * st_:512 * st_ + 512], bank[:, :]), sm_v, "v")
                            j += 1

            with nc.Block() as blk:
                @blk.gpsimd
                def _(eng):
                    eng.memset(gw1xi[:, 0:1], 0.0)
                    eng.memset(gw1xj[:, 0:1], 0.0)
                    inc(eng.engine_nop(), sm_g, "g")

            for st_ in range(NST):
                q0 = 512 * st_
                with nc.Block() as blk:
                    @blk.tensor
                    def _(eng):
                        w3(eng, "v", "g")
                        inc(eng.matmul(ps_a[0:96, 0:512], cE[0:64, 0:96], pw_w[0:64, q0:q0 + 512], start=True, stop=True), sm_t, "t")
                        inc(eng.matmul(ps_a[0:96, 512:1024], cE[0:64, 96:192], pw_w[0:64, q0:q0 + 512], start=True, stop=True), sm_t, "t")
                        inc(eng.matmul(ps_a[0:96, 1024:1536], cE[64:128, 0:96], pw_w[64:128, q0:q0 + 512], start=True, stop=True), sm_t, "t")
                        inc(eng.matmul(ps_a[0:96, 1536:2048], cE[64:128, 96:192], pw_w[64:128, q0:q0 + 512], start=True, stop=True), sm_t, "t")
                        inc(eng.matmul(ps_d[0:96, 0:512], cE[0:64, 192:288], pw_r[0:64, q0:q0 + 512], start=True, stop=True), sm_t, "t")
                        inc(eng.matmul(ps_d[0:96, 512:1024], cE[0:64, 288:384], pw_r[0:64, q0:q0 + 512], start=True, stop=True), sm_t, "t")
                        inc(eng.matmul(ps_b[0:96, :], cE[64:128, 192:288], pw_r[64:128, q0:q0 + 512], start=True, stop=True), sm_t, "t")
                        inc(eng.matmul(ps_c[0:96, :], cE[64:128, 288:384], pw_r[64:128, q0:q0 + 512], start=True, stop=True), sm_t, "t")
                    @blk.vector
                    def _(eng):
                        w3(eng, "t")
                        inc(eng.tensor_copy(gw1xi[:, q0 + 1:q0 + 513], ps_a[0:96, 0:512]), sm_v, "v")
                        inc(eng.tensor_copy(gw1xj[:, q0 + 1:q0 + 513], ps_a[0:96, 512:1024]), sm_v, "v")
                        inc(eng.tensor_copy(r1xi[:, :], ps_d[0:96, 0:512]), sm_v, "v")
                        inc(eng.tensor_copy(r1xj[:, :], ps_d[0:96, 512:1024]), sm_v, "v")
                        inc(eng.tensor_tensor(m1s[:, :], gw1xi[:, q0:q0 + 512], ps_a[0:96, 1536:2048], ALU.mult), sm_v, "v")
                        inc(eng.tensor_tensor(m2s[:, :], gw1xj[:, q0:q0 + 512], ps_a[0:96, 1024:1536], ALU.mult), sm_v, "v")
                        inc(eng.tensor_tensor(Lw[:, :], m1s[:, :], m2s[:, :], ALU.subtract), sm_v, "v")
                        inc(eng.tensor_tensor(m1s[:, :], r1xi[:, :], ps_c[0:96, :], ALU.mult), sm_v, "v")
                        inc(eng.tensor_tensor(m2s[:, :], r1xj[:, :], ps_b[0:96, :], ALU.mult), sm_v, "v")
                        inc(eng.tensor_tensor(Lr[:, :], m1s[:, :], m2s[:, :], ALU.subtract), sm_v, "v")
                        inc(eng.tensor_tensor(Lwsq[:, :], Lw[:, :], Lw[:, :], ALU.mult), sm_v, "v")
                        inc(eng.tensor_tensor(Lrsq[:, :], Lr[:, :], Lr[:, :], ALU.mult), sm_v, "v")
                    @blk.tensor
                    def _(eng):
                        w3(eng, "v")
                        inc(eng.matmul(ps_b[0:16, :], ccoll[:, :], Lwsq[:, :], start=True, stop=True), sm_t, "t")
                        inc(eng.matmul(ps_b[32:48, :], ccoll[:, :], Lrsq[:, :], start=True, stop=True), sm_t, "t")
                    @blk.vector
                    def _(eng):
                        w3(eng, "t")
                        inc(eng.tensor_scalar(ps_b[0:16, :], ps_b[0:16, :], 1e-19, None, ALU.max, ALU.bypass), sm_v, "v")
                        inc(eng.tensor_scalar(ps_b[32:48, :], ps_b[32:48, :], 1e-19, None, ALU.max, ALU.bypass), sm_v, "v")
                    @blk.scalar
                    def _(eng):
                        w3(eng, "v")
                        eng.activation(ps_b[0:16, :], ps_b[0:16, :], AF.Ln)
                        eng.activation(ps_b[32:48, :], ps_b[32:48, :], AF.Ln)
                        inc(eng.activation(inv_sb[0:16, q0:q0 + 512], ps_b[0:16, :], AF.Exp, scale=-0.5), sm_s, "s")
                        inc(eng.activation(inv_sb[32:48, q0:q0 + 512], ps_b[32:48, :], AF.Exp, scale=-0.5), sm_s, "s")
                    @blk.tensor
                    def _(eng):
                        w3(eng, "s")
                        inc(eng.matmul(ps_a[0:96, 0:512], cspw[0:16, :], inv_sb[0:16, q0:q0 + 512], start=True, stop=True), sm_t, "t")
                        inc(eng.matmul(ps_a[0:96, 512:1024], cspr[32:48, :], inv_sb[32:48, q0:q0 + 512], start=True, stop=True), sm_t, "t")
                    @blk.vector
                    def _(eng):
                        w3(eng, "t")
                        inc(eng.tensor_tensor(wl_buf[:, q0:q0 + 512], Lw[:, :], ps_a[0:96, 0:512], ALU.mult), sm_v, "v")
                        inc(eng.tensor_tensor(rl_buf[:, q0:q0 + 512], Lr[:, :], ps_a[0:96, 512:1024], ALU.mult), sm_v, "v")
            dump("pw_w", pw_w.ap(), BF)
            dump("pw_r", pw_r.ap(), BF)
            dump("inv_dbg", inv_sb.ap(), BF)
        dump("wl_buf", wl_buf.ap(), BF)
        dump("rl_buf", rl_buf.ap(), BF)

        comb = sbuf("comb", [128, 8 * T], BF, S_comb, side="right")
        with ExitStack() as S3:
            qt_p = sbuf("qt_p", [128, T], BF, S3)
            kt_p = sbuf("kt_p", [128, T], BF, S3)
            vfm = sbuf("vfm", [128, T], BF, S3)
            vraw = sbuf("vraw", [128, T], BF, S3)
            v_tok = sbuf("v_tok", [128, NTT * 130], BF, S3)
            gv_tok = sbuf("gv_tok", [128, NTT * 130], BF, S3)
            NSL = 8
            pt = sbuf("pt", [128, NSL * 512], BF, S3)
            wl_scr = sbuf("wl_scr", [128, 2 * T], BF, S3)
            rl_scr = sbuf("rl_scr", [128, 2 * T], BF, S3)
            u_sb = [sbuf("u_sb%d" % i, [65, T], BF, S3) for i in range(4)]
            rsb = sbuf("rsb", [64, 2 * T], BF, S3)
            tmpb = sbuf("tmpb", [64, T], BF, S3)
            cmbB = sbuf("cmbB", [64, T], BF, S3)
            wbf2 = sbuf("wbf2", [128, 2 * NDC * 512], BF, S3)

            with nc.Block() as blk:
                @blk.gpsimd
                def _(eng):
                    vt = v_tok.ap().rearrange("p (tt s c) -> p tt s c", s=2, c=65)
                    gt = gv_tok.ap().rearrange("p (tt s c) -> p tt s c", s=2, c=65)
                    eng.memset(vt[:, :, :, 64:65], 1.0)
                    eng.memset(gt[:, :, :, 64:65], 1.0)
                    eng.memset(wl_scr[:, :], 0.0)
                    eng.memset(rl_scr[:, :], 0.0)
                    inc(eng.nop(), sm_g, "g")

            MK = {}
            NP = np_pairs
            GTI = [0]
            mask_after = {-1: 0}
            EV = {}

            def wslot(p):
                return 4096 * (p % 2)

            def emit_loads(blk, p):
                @blk.sync
                def _(eng):
                    if p > 0:
                        eng.wait_ge(sm_v, MK[("cast_vg", p - 1)])
                    ws3 = wstage.ap().rearrange("p (dc c) -> p dc c", c=256)
                    inc(eng.dma_start(out=ws3[:, :, 0:128],
                                      in_=qkv_w[:, 128 * p:128 * p + 128].rearrange("(dc p) c -> p dc c", p=128)), sm_d, "d", 16)
                    inc(eng.dma_start(out=ws3[:, :, 128:256],
                                      in_=qkv_w[:, D + 128 * p:D + 128 * p + 128].rearrange("(dc p) c -> p dc c", p=128)), sm_d, "d", 16)
                    MK[("d_qk", p)] = val("d")

            def emit_loads_vg(blk, p):
                @blk.sync
                def _(eng):
                    eng.wait_ge(sm_v, MK[("cast_qk", p)])
                    ws3 = wstage.ap().rearrange("p (dc c) -> p dc c", c=256)
                    inc(eng.dma_start(out=ws3[:, :, 0:128],
                                      in_=qkv_w[:, 2 * D + 128 * p:2 * D + 128 * p + 128].rearrange("(dc p) c -> p dc c", p=128)), sm_d, "d", 16)
                    inc(eng.dma_start(out=ws3[:, :, 128:256],
                                      in_=geo_w[:, 128 * p:128 * p + 128].rearrange("(dc p) c -> p dc c", p=128)), sm_d, "d", 16)
                    MK[("d_vg", p)] = val("d")

            def emit_cast_qk(blk, p):
                @blk.vector
                def _(eng):
                    eng.wait_ge(sm_d, MK[("d_qk", p)])
                    if p >= 2:
                        eng.wait_ge(sm_t, MK[("t_proj", p - 2)])
                    sl = wslot(p)
                    for dc in range(NDC):
                        inc(eng.tensor_copy(wbf2[:, sl + 512 * dc:sl + 512 * dc + 256],
                                            wstage[:, 256 * dc:256 * dc + 256]), sm_v, "v")
                    MK[("cast_qk", p)] = val("v")

            def emit_cast_vg(blk, p):
                @blk.vector
                def _(eng):
                    eng.wait_ge(sm_d, MK[("d_vg", p)])
                    sl = wslot(p)
                    for dc in range(NDC):
                        inc(eng.tensor_copy(wbf2[:, sl + 512 * dc + 256:sl + 512 * dc + 512],
                                            wstage[:, 256 * dc:256 * dc + 256]), sm_v, "v")
                    MK[("cast_vg", p)] = val("v")

            def emit_extract(blk, p):
                @blk.sync
                def _(eng):
                    if p >= 2:
                        eng.wait_ge(sm_t, MK[("t_geo_done", p - 2)])
                    eng.wait_ge(sm_d, val("d"))
                    c0 = T * (p % 2)
                    hA, hB = 2 * p, 2 * p + 1
                    inc(eng.dma_start(out=wl_scr[0:6, c0:c0 + T], in_=wl_buf[6 * hA:6 * hA + 6, :]), sm_d, "d", 16)
                    inc(eng.dma_start(out=wl_scr[64:70, c0:c0 + T], in_=wl_buf[6 * hB:6 * hB + 6, :]), sm_d, "d", 16)
                    inc(eng.dma_start(out=rl_scr[0:6, c0:c0 + T], in_=rl_buf[6 * hA:6 * hA + 6, :]), sm_d, "d", 16)
                    inc(eng.dma_start(out=rl_scr[64:70, c0:c0 + T], in_=rl_buf[6 * hB:6 * hB + 6, :]), sm_d, "d", 16)
                    MK[("d_ext", p)] = val("d")

            def emit_proj(blk, p):
                sl = wslot(p)
                t0 = val("t")
                v0 = val("v")
                @blk.tensor
                def _(eng):
                    eng.wait_ge(sm_v, MK[("cast_qk", p)])
                    j = 0
                    for which in (0, 128):
                        for st_ in range(NST):
                            bank = ps_b if j % 2 == 0 else ps_c
                            if j >= 2:
                                eng.wait_ge(sm_v, v0 + j - 1)
                            for dc in range(NDC):
                                inc(eng.matmul(bank[:, :], wbf2[:, sl + 512 * dc + which:sl + 512 * dc + which + 128],
                                               xln_fm[:, T * dc + 512 * st_:T * dc + 512 * st_ + 512],
                                               start=(dc == 0), stop=(dc == NDC - 1)),
                                    sm_t, "t", 1 if dc == NDC - 1 else 0)
                            j += 1
                @blk.vector
                def _(eng):
                    j = 0
                    for dst, bcol in [(qt_p, p), (kt_p, 8 + p)]:
                        for st_ in range(NST):
                            bank = ps_b if j % 2 == 0 else ps_c
                            eng.wait_ge(sm_t, t0 + j + 1)
                            inc(eng.tensor_scalar(dst[:, 512 * st_:512 * st_ + 512], bank[:, :],
                                                  qkb_sb[:, bcol:bcol + 1], None, ALU.add, ALU.bypass), sm_v, "v")
                            j += 1
                for vi in range(2):
                    t1 = val("t")
                    v1 = val("v")
                    which = 256 if vi == 0 else 384
                    @blk.tensor
                    def _(eng):
                        eng.wait_ge(sm_v, MK[("cast_vg", p)])
                        for st_ in range(NST):
                            bank = ps_b if st_ % 2 == 0 else ps_c
                            if st_ >= 2:
                                eng.wait_ge(sm_v, v1 + st_ - 1)
                            for dc in range(NDC):
                                inc(eng.matmul(bank[:, :], wbf2[:, sl + 512 * dc + which:sl + 512 * dc + which + 128],
                                               xln_fm[:, T * dc + 512 * st_:T * dc + 512 * st_ + 512],
                                               start=(dc == 0), stop=(dc == NDC - 1)),
                                    sm_t, "t", 1 if dc == NDC - 1 else 0)
                    MK[("t_proj", p)] = val("t")
                    @blk.vector
                    def _(eng):
                        if vi == 1:
                            eng.wait_ge(sm_d, MK[("d_trV", p)])
                        for st_ in range(NST):
                            bank = ps_b if st_ % 2 == 0 else ps_c
                            eng.wait_ge(sm_t, t1 + st_ + 1)
                            bias = qkb_sb[:, 16 + p:17 + p] if vi == 0 else gb_sb[:, p:p + 1]
                            inc(eng.tensor_scalar(vfm[:, 512 * st_:512 * st_ + 512], bank[:, :],
                                                  bias, None, ALU.add, ALU.bypass), sm_v, "v")
                    MK[("v_vfm", p, vi)] = val("v")
                    @blk.sync
                    def _(eng):
                        eng.wait_ge(sm_v, MK[("v_vfm", p, vi)])
                        eng.wait_ge(sm_s, val("s"))
                        if vi == 0 and p >= 1:
                            eng.wait_ge(sm_v, MK[("v_scat", p - 1, 1)])
                        if vi == 1:
                            eng.wait_ge(sm_v, MK[("v_scat", p, 0)])
                        eng.wait_ge(sm_d, val("d"))
                        inc(eng.dma_start(out=vraw.ap().rearrange("p (m c) -> p m c", c=128),
                                          in_=vfm[:, :], transpose=True), sm_d, "d", 16)
                        MK[("d_trV" if vi == 0 else "d_trG", p)] = val("d")
                    @blk.vector
                    def _(eng):
                        eng.wait_ge(sm_d, MK[("d_trV" if vi == 0 else "d_trG", p)])
                        if p >= 1 and vi == 0:
                            eng.wait_ge(sm_t, MK[("t_pv_done", p - 1, 3)])
                        dst = (v_tok if vi == 0 else gv_tok).ap().rearrange("p (tt s c) -> p tt s c", s=2, c=65)
                        inc(eng.tensor_copy(dst[:, :, :, 0:64],
                                            vraw.ap().rearrange("p (tt s c) -> p tt s c", s=2, c=64)), sm_v, "v")
                        MK[("v_scat", p, vi)] = val("v")

            def emit_rs(blk, sec, attn_i, ubuf, pv_mark):
                @blk.scalar
                def _(eng):
                    eng.wait_ge(sm_e, EV[(sec, 3)])
                    for hh in range(2):
                        eng.activation(rows_f[64:65, 0:1024], ubuf[64:65, 1024 * hh:1024 * hh + 1024], AF.Ln)
                        inc(eng.activation(rows_b[64:65, T * attn_i + 1024 * hh:T * attn_i + 1024 * hh + 1024],
                                           rows_f[64:65, 0:1024], AF.Exp, scale=-1.0), sm_s, "s")
                MK[("rs", sec)] = val("s")

            def emit_section(blk, p, hi, attn_i, sec, prev_sec, pend_rs_info, pend_tail_info):
                is_std = attn_i == 0
                scale = 0.125 if is_std else 1.0
                vt3 = (v_tok if is_std else gv_tok).ap().rearrange("p (tt s c) -> p tt s c", s=2, c=65)
                ubuf = u_sb[sec % 4]
                e0, m0, t0 = val("s"), val("v"), val("t")
                g0 = GTI[0]

                exp_no = {}
                for i in range(N_TILES):
                    exp_no[i] = e0 + (i + 1)
                if pend_rs_info is not None:
                    MK[("rs", pend_rs_info[0])] = e0 + N_TILES + 2
                m0g = val("g")
                nm = 0
                for i, (s_, kc, off, diag) in enumerate(TILES):
                    if diag:
                        nm += 1
                    mask_after[g0 + i] = m0g + nm
                SBANKS = [ps_b[0:128, 0:512], ps_c[0:128, 0:512],
                          ps_d[0:128, 0:512], ps_d[0:128, 512:1024]]
                NB = len(SBANKS)
                DEPTH = 4
                for s_ in range(NST):
                    EV[(sec, s_)] = val("e") + s_ + 1
                if pend_tail_info is not None:
                    emit_fdma(blk, *pend_tail_info)
                @blk.tensor
                def _(eng):
                    if is_std:
                        eng.wait_ge(sm_v, MK[("v_scat", p, 0)])
                        eng.wait_ge(sm_v, MK[("v_vfm", p, 1)])
                        if ("v_perm_done", p - 1) in MK:
                            eng.wait_ge(sm_v, MK[("v_perm_done", p - 1)])
                    else:
                        eng.wait_ge(sm_v, MK[("v_scat", p, 1)])
                        eng.wait_ge(sm_d, MK[("d_ext", p)])
                    s_tcount = MK.setdefault(("s_tcount", sec), {})
                    def S_mm(i):
                        s_, kc, off, diag = TILES[i]
                        bank = SBANKS[i % NB]
                        if is_std:
                            lhsT = kt_p[64 * hi:64 * hi + 64, 128 * kc:128 * kc + 128]
                            rhs = qt_p[64 * hi:64 * hi + 64, 512 * s_ + off:512 * s_ + 512]
                        else:
                            b0 = 64 * hi
                            c0 = T * (p % 2)
                            lhsT = wl_scr[b0:b0 + 64, c0 + 128 * kc:c0 + 128 * kc + 128]
                            rhs = rl_scr[b0:b0 + 64, c0 + 512 * s_ + off:c0 + 512 * s_ + 512]
                        s_tcount[i] = inc(eng.matmul(bank[0:128, off:512], lhsT, rhs, start=True, stop=True), sm_t, "t")
                    def heat(nch):
                        for j in range(8 * nch):
                            eng.matmul(ps_a[0:128, 0:512],
                                       xln_fm[:, 128 * (j % 8):128 * (j % 8) + 128],
                                       xln_fm[:, 0:512], start=(j % 8 == 0), stop=(j % 8 == 7))
                    for i0 in range(4):
                        S_mm(i0)
                    if prev_sec is not None:
                        eng.wait_ge(sm_e, EV[(prev_sec, 0)])
                    heat(2)
                    for i in range(N_TILES):
                        eng.wait_ge(sm_s, exp_no[i])
                        if i + 4 < N_TILES:
                            S_mm(i + 4)
                        s_, kc, off, diag = TILES[i]
                        if diag:
                            eng.wait_ge(sm_g, mask_after[g0 + i])
                        if kc == 0 and prev_sec is not None:
                            eng.wait_ge(sm_e, EV[(prev_sec, s_)])
                        eng.matmul(ps_a[0:65, 512 * s_ + off:512 * s_ + 512],
                                   vt3[:, kc, hi, 0:65],
                                   pt[:, 512 * ((g0 + i) % NSL) + off:512 * ((g0 + i) % NSL) + 512],
                                   start=(kc == 0), stop=(kc == 4 * s_ + 3))
                    inc(eng.nop(), sm_t, "t")
                    MK[("t_pv_done", p, hi * 2 + attn_i)] = val("t")
                    if hi == 1 and attn_i == 1:
                        MK[("t_geo_done", p)] = val("t")
                @blk.scalar
                def _(eng):
                    stc = MK[("s_tcount", sec)]
                    for i in range(N_TILES):
                        s_, kc, off, diag = TILES[i]
                        eng.wait_ge(sm_t, stc[i])
                        eng.wait_ge(sm_g, mask_after.get(g0 + i - NSL, 0))
                        inc(eng.activation(pt[:, 512 * ((g0 + i) % NSL) + off:512 * ((g0 + i) % NSL) + 512],
                                           SBANKS[i % NB][0:128, off:512], AF.Exp, scale=scale), sm_s, "s")
                    if pend_rs_info is not None:
                        sec_pv, attn_pv, ubuf_pv = pend_rs_info
                        eng.wait_ge(sm_e, EV[(sec_pv, 3)])
                        for hh in range(2):
                            eng.activation(rows_f[64:65, 0:1024],
                                           ubuf_pv[64:65, 1024 * hh:1024 * hh + 1024], AF.Ln)
                            inc(eng.activation(rows_b[64:65, T * attn_pv + 1024 * hh:T * attn_pv + 1024 * hh + 1024],
                                               rows_f[64:65, 0:1024], AF.Exp, scale=-1.0), sm_s, "s")
                @blk.gpsimd
                def _(eng):
                    for i in range(N_TILES):
                        s_, kc, off, diag = TILES[i]
                        if diag:
                            eng.wait_ge(sm_s, exp_no[i])
                            inc(eng.tensor_tensor(pt[:, 512 * ((g0 + i) % NSL) + off:512 * ((g0 + i) % NSL) + off + 128],
                                                  pt[:, 512 * ((g0 + i) % NSL) + off:512 * ((g0 + i) % NSL) + off + 128],
                                                  ctri[:, :], ALU.mult), sm_g, "g")
                if pend_tail_info is not None:
                    @blk.vector
                    def _(eng):
                        tail_vec_part1(eng, *pend_tail_info)
                        tail_vec_part2(eng, *pend_tail_info)
                @blk.vector
                def _(eng):
                    eng.wait_ge(sm_t, MK[("t_pv_done", p, hi * 2 + attn_i)])
                    for s_ in range(NST):
                        got = inc(eng.tensor_copy(ubuf[0:65, 512 * s_:512 * s_ + 512],
                                                  ps_a[0:65, 512 * s_:512 * s_ + 512]), sm_e, "e")
                        assert got == EV[(sec, s_)]
                GTI[0] = g0 + N_TILES

            def emit_fdma(blk, p_t, hi_t):
                @blk.sync
                def _(eng):
                    eng.wait_ge(sm_s, MK[("rs", p_t * 4 + hi_t * 2 + 1)])
                    eng.wait_ge(sm_v, val("v"))
                    eng.wait_ge(sm_d, val("d"))
                    inc(eng.dma_start(out=rs_hbm.rearrange("(a c) -> a c", a=1),
                                      in_=rows_b.ap()[64:65, 0:2 * T]), sm_d, "d", 16)
                    eng.wait_ge(sm_d, val("d"))
                    inc(eng.dma_start(out=rsb[:, :],
                                      in_=rs_hbm.rearrange("(a c) -> a c", a=1).broadcast_to([64, 2 * T])), sm_d, "d", 16)
                    MK[("d_f", p_t, hi_t)] = val("d")

            def tail_vec_part1(eng, p_t, hi_t):
                sec_std = p_t * 4 + hi_t * 2
                ub_s = u_sb[sec_std % 4]
                dst = comb.ap()[0:64, T * p_t:T * p_t + T] if hi_t == 0 else cmbB.ap()[:, :]
                eng.wait_ge(sm_d, MK[("d_f", p_t, hi_t)])
                inc(eng.tensor_tensor(tmpb[:, :], ub_s[0:64, :], rsb[:, 0:T], ALU.mult), sm_v, "v")
                inc(eng.tensor_tensor(dst, tmpb[:, :], omg_bc[:, :], ALU.mult), sm_v, "v")

            def tail_vec_part2(eng, p_t, hi_t):
                sec_geo = p_t * 4 + hi_t * 2 + 1
                ub_g = u_sb[sec_geo % 4]
                dst = comb.ap()[0:64, T * p_t:T * p_t + T] if hi_t == 0 else cmbB.ap()[:, :]
                inc(eng.tensor_tensor(tmpb[:, :], ub_g[0:64, :], rsb[:, T:2 * T], ALU.mult), sm_v, "v")
                inc(eng.tensor_tensor(tmpb[:, :], tmpb[:, :], g_bc[:, :], ALU.mult), sm_v, "v")
                inc(eng.tensor_tensor(dst, dst, tmpb[:, :], ALU.add), sm_v, "v")
                MK[("v_comb", p_t, hi_t)] = val("v")

            def emit_tail_vector(blk, p_t, hi_t):
                @blk.vector
                def _(eng):
                    tail_vec_part1(eng, p_t, hi_t)
                    tail_vec_part2(eng, p_t, hi_t)

            def emit_permuteB(blk, p_t):
                t0 = val("t")
                v0p = val("v")
                @blk.tensor
                def _(eng):
                    eng.wait_ge(sm_v, MK[("v_comb", p_t, 1)])
                    eng.wait_ge(sm_v, MK[("v_vfm", p_t + 1, 1)] if ("v_vfm", p_t + 1, 1) in MK else 0)
                    for q in range(4):
                        if q == 2:
                            eng.wait_ge(sm_v, v0p + 1)
                        bank = ps_b if q % 2 == 0 else ps_c
                        inc(eng.matmul(bank[64:128, 0:512], cident[:, :],
                                       cmbB[:, 512 * q:512 * q + 512], start=True, stop=True), sm_t, "t")
                @blk.vector
                def _(eng):
                    for r in range(2):
                        eng.wait_ge(sm_t, t0 + 2 * (r + 1))
                        for q2 in range(2):
                            inc(eng.tensor_copy(comb[64:128, T * p_t + 1024 * r + 512 * q2:T * p_t + 1024 * r + 512 * q2 + 512],
                                                (ps_b if q2 == 0 else ps_c)[64:128, 0:512]), sm_v, "v")
                MK[("v_perm_done", p_t)] = val("v")

            with nc.Block() as blk:
                emit_loads(blk, 0)
                emit_cast_qk(blk, 0)
                emit_loads_vg(blk, 0)
                emit_cast_vg(blk, 0)
                emit_extract(blk, 0)
                sec = 0
                prev_sec = None
                pend_rs = None
                pend_tail = None
                for p in range(NP):
                    emit_proj(blk, p)
                    if p + 1 < NP:
                        emit_loads(blk, p + 1)
                        emit_extract(blk, p + 1)
                    if p >= 1 and pend_rs is not None:
                        emit_rs(blk, pend_rs[0], pend_rs[1], pend_rs[2],
                                ("t_pv_done", p - 1, 3))
                        pend_rs = None
                        emit_fdma(blk, p - 1, 1)
                        emit_tail_vector(blk, p - 1, 1)
                        emit_permuteB(blk, p - 1)
                    if p + 1 < NP:
                        emit_cast_qk(blk, p + 1)
                        emit_loads_vg(blk, p + 1)
                        emit_cast_vg(blk, p + 1)
                    for si, (hi, attn_i) in enumerate([(0, 0), (0, 1), (1, 0), (1, 1)]):
                        if sec >= n_secs:
                            sec += 1
                            continue
                        emit_section(blk, p, hi, attn_i, sec, prev_sec,
                                     pend_rs, pend_tail)
                        pend_tail = None
                        pend_rs = (sec, attn_i, u_sb[sec % 4])
                        prev_sec = sec
                        sec += 1
                        if si == 2:
                            pend_tail = (p, 0)
                if pend_rs is not None and sec - 1 < n_secs and (NP - 1) * 4 + 3 < n_secs and not skip_final:
                    if final_lvl >= 1:
                        emit_rs(blk, pend_rs[0], pend_rs[1], pend_rs[2], ("t_pv_done", NP - 1, 3))
                        pend_rs = None
                    if final_lvl >= 2:
                        emit_fdma(blk, NP - 1, 1)
                    if final_lvl >= 3:
                        emit_tail_vector(blk, NP - 1, 1)
                    if final_lvl >= 4:
                        emit_permuteB(blk, NP - 1)
        S_pre.close()

        with ExitStack() as S_ao:
            big_f = sbuf("big_f", [128, NTT * 1024], F32, S_ao)
            wbf = sbuf("wbf", [128, NDC * 256], BF, S_ao)
            with ExitStack() as S4:
                wout = sbuf("wout", [128, NDC * D], BF, S4)
                ob_bc = sbuf("ob_bc", [128, D], F32, S4)
                with nc.Block() as blk:
                    @blk.sync
                    def _(eng):
                        inc(eng.dma_start(out=wstage[0:1, 0:1024], in_=out_b.rearrange("(a d) -> a d", a=1)), sm_d, "d", 16)
                    @blk.vector
                    def _(eng):
                        w3(eng, "d")
                        inc(eng.tensor_copy(wbf[0:1, 0:D], wstage[0:1, 0:1024]), sm_v, "v")
                        inc(eng.tensor_tensor(wbf[0:1, D:2 * D], wstage[0:1, 0:1024], wbf[0:1, 0:D], ALU.subtract), sm_v, "v")
                    @blk.tensor
                    def _(eng):
                        w3(eng, "v")
                        for i in range(2):
                            eng.matmul(ps_d[:, 512 * i:512 * i + 512], cones[0:1, 0:128],
                                       wbf[0:1, 512 * i:512 * i + 512], start=True, stop=False)
                            inc(eng.matmul(ps_d[:, 512 * i:512 * i + 512], cones[0:1, 0:128],
                                           wbf[0:1, D + 512 * i:D + 512 * i + 512], start=False, stop=True), sm_t, "t")
                    @blk.scalar
                    def _(eng):
                        w3(eng, "t")
                        inc(eng.activation(ob_bc[:, :], ps_d[:, 0:1024], AF.Copy), sm_s, "s")
                for dc in range(NDC):
                    with nc.Block() as blk:
                        @blk.sync
                        def _(eng):
                            inc(eng.dma_start(out=wstage[:, 0:1024],
                                              in_=out_w[128 * dc:128 * dc + 128, :]), sm_d, "d", 16)
                        @blk.vector
                        def _(eng):
                            w3(eng, "d")
                            inc(eng.tensor_copy(wout[:, D * dc:D * dc + D], wstage[:, 0:1024]), sm_v, "v")
                with nc.Block() as blk:
                    t0, v0 = val("t"), val("v")
                    @blk.tensor
                    def _(eng):
                        w3(eng, "v")
                        j = 0
                        for tt in range(NTT):
                            for os_ in range(2):
                                bank = ps_b if j % 2 == 0 else ps_c
                                if j >= 2:
                                    eng.wait_ge(sm_v, v0 + j - 1)
                                for c in range(8):
                                    inc(eng.matmul(bank[:, :], comb[:, T * c + 128 * tt:T * c + 128 * tt + 128],
                                                   wout[:, D * c + 512 * os_:D * c + 512 * os_ + 512],
                                                   start=(c == 0), stop=(c == 7)),
                                        sm_t, "t", 1 if c == 7 else 0)
                                j += 1
                    @blk.vector
                    def _(eng):
                        j = 0
                        for tt in range(NTT):
                            for os_ in range(2):
                                bank = ps_b if j % 2 == 0 else ps_c
                                eng.wait_ge(sm_t, t0 + j + 1)
                                inc(eng.tensor_tensor(big_f[:, 1024 * tt + 512 * os_:1024 * tt + 512 * os_ + 512],
                                                      bank[:, :], ob_bc[:, 512 * os_:512 * os_ + 512], ALU.add), sm_v, "v")
                                j += 1
            dump("comb", comb.ap(), BF)
            dump("attn_out", big_f.ap(), F32)
            S_comb.close()

            with ExitStack() as S5:
                xstage = sbuf("xstage", [128, 2 * 1024], F32, S5)
                with nc.Block() as blk:
                    emit_ln(blk, big_f.ap(), big_f.ap(), ("v",))
                with nc.Block() as blk:
                    d0, v0 = val("d"), val("v")
                    @blk.sync
                    def _(eng):
                        for tt in range(NTT):
                            if tt >= 2:
                                eng.wait_ge(sm_v, v0 + tt - 1)
                            inc(eng.dma_start(out=xstage[:, 1024 * (tt % 2):1024 * (tt % 2) + 1024],
                                              in_=x_h[128 * tt:128 * tt + 128, :]), sm_d, "d", 16)
                    @blk.vector
                    def _(eng):
                        for tt in range(NTT):
                            eng.wait_ge(sm_d, d0 + 16 * (tt + 1))
                            inc(eng.tensor_tensor(big_f[:, 1024 * tt:1024 * tt + 1024],
                                                  big_f[:, 1024 * tt:1024 * tt + 1024],
                                                  xstage[:, 1024 * (tt % 2):1024 * (tt % 2) + 1024], ALU.add), sm_v, "v")
                dump("x2", big_f.ap(), F32)
                with nc.Block() as blk:
                    @blk.gpsimd
                    def _(eng):
                        w3(eng, "v")
                        inc(eng.dma_start(out=x2s.rearrange("(tt p) d -> p tt d", p=128),
                                          in_=big_f.ap().rearrange("p (tt d) -> p tt d", d=1024)), sm_g2, "g2", 16)
                        eng.wait_ge(sm_g2, val("g2"))
                xln2_tok = sbuf("xln2_tok", [128, NTT * 1024], BF, S5)
                with nc.Block() as blk:
                    emit_ln(blk, big_f.ap(), xln2_tok.ap(), ("v",))
                with nc.Block() as blk:
                    @blk.sync
                    def _(eng):
                        w3(eng, "v")
                        xf = xln_fm.ap().rearrange("p (dc t) -> p dc t", t=T)
                        for tt in range(NTT):
                            inc(eng.dma_start(out=xf[:, :, 128 * tt:128 * tt + 128],
                                              in_=xln2_tok[:, 1024 * tt:1024 * tt + 1024],
                                              transpose=True), sm_d, "d", 16)
                        eng.wait_ge(sm_d, val("d"))

            with ExitStack() as S6:
                wt_fm = sbuf("wt_fm", [128, NDC * D], BF, S6)
                pooled = sbuf("pooled", [64, T], BF, S6)
                for oc in range(NDC):
                    with nc.Block() as blk:
                        @blk.sync
                        def _(eng):
                            inc(eng.dma_start(out=wstage[:, 0:1024],
                                              in_=ffn_w[128 * oc:128 * oc + 128, :]), sm_d, "d", 16)
                        @blk.vector
                        def _(eng):
                            w3(eng, "d")
                            inc(eng.tensor_copy(wbf[:, 0:1024], wstage[:, 0:1024]), sm_v, "v")
                        @blk.sync
                        def _(eng):
                            w3(eng, "v")
                            wt3 = wt_fm.ap().rearrange("p (dc oc c) -> p dc oc c", oc=8, c=128)
                            inc(eng.dma_start(out=wt3[:, :, oc, :], in_=wbf[:, 0:1024], transpose=True), sm_d, "d", 16)
                            eng.wait_ge(sm_d, val("d"))
                with nc.Block() as blk:
                    t0, v0 = val("t"), val("v")
                    @blk.tensor
                    def _(eng):
                        for st_ in range(NST):
                            if st_ >= 1:
                                eng.wait_ge(sm_v, v0 + st_)
                            for dc in range(NDC):
                                inc(eng.matmul(ps_d[0:64, 0:512], cpool[:, :],
                                               xln_fm[:, T * dc + 512 * st_:T * dc + 512 * st_ + 512],
                                               start=(dc == 0), stop=(dc == NDC - 1)),
                                    sm_t, "t", 1 if dc == NDC - 1 else 0)
                    @blk.vector
                    def _(eng):
                        for st_ in range(NST):
                            eng.wait_ge(sm_t, t0 + st_ + 1)
                            inc(eng.tensor_copy(pooled[:, 512 * st_:512 * st_ + 512], ps_d[0:64, 0:512]), sm_v, "v")
                with nc.Block() as blk:
                    t0, v0 = val("t"), val("v")
                    @blk.tensor
                    def _(eng):
                        w3(eng, "v")
                        j = 0
                        for tt in range(NTT):
                            for os_ in range(2):
                                bank = ps_b if j % 2 == 0 else ps_c
                                if j >= 2:
                                    eng.wait_ge(sm_v, v0 + j - 1)
                                for dc in range(NDC):
                                    eng.matmul(bank[:, :], xln_fm[:, T * dc + 128 * tt:T * dc + 128 * tt + 128],
                                               wt_fm[:, 1024 * dc + 512 * os_:1024 * dc + 512 * os_ + 512],
                                               start=(dc == 0), stop=False)
                                inc(eng.matmul(bank[:, :], pooled[:, 128 * tt:128 * tt + 128],
                                               disha_bf[:, 512 * os_:512 * os_ + 512],
                                               start=False, stop=True), sm_t, "t")
                                j += 1
                    @blk.vector
                    def _(eng):
                        j = 0
                        for tt in range(NTT):
                            for os_ in range(2):
                                bank = ps_b if j % 2 == 0 else ps_c
                                eng.wait_ge(sm_t, t0 + j + 1)
                                inc(eng.tensor_copy(big_f[:, 1024 * tt + 512 * os_:1024 * tt + 512 * os_ + 512],
                                                    bank[:, :]), sm_v, "v")
                                j += 1

            dump("ffn_raw", big_f.ap(), F32)

            with ExitStack() as S7:
                xstage = sbuf("xstage7", [128, 2 * 1024], F32, S7)
                with nc.Block() as blk:
                    emit_ln(blk, big_f.ap(), big_f.ap(), ("v",))
                with nc.Block() as blk:
                    d0, v0 = val("d"), val("v")
                    @blk.sync
                    def _(eng):
                        for tt in range(NTT):
                            if tt >= 2:
                                eng.wait_ge(sm_v, v0 + tt - 1)
                            inc(eng.dma_start(out=xstage[:, 1024 * (tt % 2):1024 * (tt % 2) + 1024],
                                              in_=x2s[128 * tt:128 * tt + 128, :]), sm_d, "d", 16)
                    @blk.vector
                    def _(eng):
                        for tt in range(NTT):
                            eng.wait_ge(sm_d, d0 + 16 * (tt + 1))
                            inc(eng.tensor_tensor(big_f[:, 1024 * tt:1024 * tt + 1024],
                                                  big_f[:, 1024 * tt:1024 * tt + 1024],
                                                  xstage[:, 1024 * (tt % 2):1024 * (tt % 2) + 1024], ALU.add), sm_v, "v")
                with nc.Block() as blk:
                    @blk.sync
                    def _(eng):
                        w3(eng, "v")
                        inc(eng.dma_start(out=y_h.rearrange("(tt p) d -> p tt d", p=128),
                                          in_=big_f.ap().rearrange("p (tt d) -> p tt d", d=1024)), sm_d, "d", 16)
                        eng.wait_ge(sm_d, val("d"))
    return nc


_NC_CACHE = [None]


def kernel(**inputs):
    consts = host_consts()
    base = {
        "qkv_w": inputs["L0_attn_qkv_w"], "qkv_b": inputs["L0_attn_qkv_b"],
        "w1_write": inputs["L0_attn_w1_write"], "w2_write": inputs["L0_attn_w2_write"],
        "w1_read": inputs["L0_attn_w1_read"], "w2_read": inputs["L0_attn_w2_read"],
        "geo_w": inputs["L0_attn_geo_w"], "geo_b": inputs["L0_attn_geo_b"],
        "gate_w": inputs["L0_attn_gate_w"], "gate_b": inputs["L0_attn_gate_b"],
        "inc_scale": inputs["L0_attn_inc_scale"],
        "out_w": inputs["L0_attn_out_w"], "out_b": inputs["L0_attn_out_b"],
        "ffn_weight": inputs["L0_ffn_weight"], "disha": inputs["L0_ffn_disha"],
    }
    base = {k: np.ascontiguousarray(np.asarray(v, dtype=np.float32)) for k, v in base.items()}
    for k, v in consts.items():
        base["c_" + k] = np.ascontiguousarray(v)
    x_full = np.asarray(inputs["L0_attn_x"], dtype=np.float32)
    in_maps = []
    for b in range(8):
        m = dict(base)
        m["x"] = np.ascontiguousarray(x_full[b])
        in_maps.append(m)
    if _NC_CACHE[0] is None:
        _NC_CACHE[0] = build_nc()
    res = run_bass_kernel_spmd(_NC_CACHE[0], in_maps, core_ids=list(range(8)))
    return np.stack([np.asarray(r["y"], dtype=np.float32) for r in res.results], axis=0)
